# version 07
# baseline (speedup 1.0000x reference)
import contextlib
import math
import numpy as np
import ml_dtypes
import concourse.bass as bass
import concourse.mybir as mybir
from concourse.bass_utils import run_bass_kernel_spmd

F32 = mybir.dt.float32
BF16 = mybir.dt.bfloat16
I8 = mybir.dt.int8
AF = mybir.ActivationFunctionType
ALU = mybir.AluOpType
AX = mybir.AxisListType

ENGINES = ("sync", "gpsimd", "scalar", "vector", "tensor")
NHEAD = 16
DMODEL = 2048
TLOC = 1024
TFULL = 4096
NFC = 16
SCALE = 1.0 / math.sqrt(128.0)


class Prog:

    def __init__(self, nc, es):
        self.nc = nc
        self.es = es
        self.ops = {e: [] for e in ENGINES}
        self.sems = {}
        self.counts = {}
        self.owner = {}

    def sem(self, name, owner=""):
        self.sems[name] = self.es.enter_context(self.nc.semaphore(name))
        self.counts[name] = 0
        self.owner[name] = owner
        return name

    def now(self, *names):
        return [(n, self.counts[n]) for n in names]

    def on(self, eng, fn, waits=(), inc=None, inc_by=1):
        self.ops[eng].append((fn, list(waits), inc, inc_by))
        if inc is not None:
            self.counts[inc] += inc_by
        return self.counts.get(inc)

    def dma(self, eng, out, in_, sem, waits=()):
        return self.on(eng, lambda e, o=out, i=in_: e.dma_start(out=o, in_=i),
                       waits, inc=sem, inc_by=16)

    def emit(self):
        self.es.enter_context(self._emit_ctx())

    @contextlib.contextmanager
    def _emit_ctx(self):
        with self.nc.Block() as block:
            for ename in ENGINES:
                ops = self.ops[ename]
                if not ops:
                    continue

                own = None
                for s, o in self.owner.items():
                    if o == ename:
                        own = s
                auto_sync = ename in ("vector", "scalar", "gpsimd")

                def body(eng, ops=ops, ename=ename, own=own, auto_sync=auto_sync):
                    waited = {}
                    own_count = 0
                    for fn, waits, inc, inc_by in ops:
                        if auto_sync and own_count > waited.get(own, 0):
                            eng.wait_ge(self.sems[own], own_count)
                            waited[own] = own_count
                        for s, v in waits:
                            if v <= 0:
                                continue
                            if self.owner[s] == ename:
                                continue
                            if waited.get(s, -1) >= v:
                                continue
                            eng.wait_ge(self.sems[s], v)
                            waited[s] = v
                        instr = fn(eng)
                        if inc is not None:
                            instr.then_inc(self.sems[inc], inc_by)
                            if inc == own:
                                own_count += inc_by

                getattr(block, ename)(body)
            yield


def bc16(ap_f32):
    return ap_f32.bitcast(BF16)



def build_launch_a(debug=(), upto="all"):
    nc = bass.Bass("TRN2", target_bir_lowering=False)
    d = {}

    def inp(name, shape, dt=F32):
        d[name] = nc.dram_tensor(name, shape, dt, kind="ExternalInput").ap()
        return d[name]

    def outp(name, shape, dt=F32):
        d[name] = nc.dram_tensor(name, shape, dt, kind="ExternalOutput").ap()
        return d[name]

    inp("xloc", [TLOC, DMODEL])
    inp("xb", [TFULL, DMODEL])
    inp("cosT", [128, TFULL])
    inp("sinT", [128, TFULL])
    inp("cosl", [128, TLOC])
    inp("sinl", [128, TLOC])
    inp("wat", [DMODEL, 2304])
    inp("watqH", [128, 16 * 2048])
    inp("wprojH", [128, 16 * 2048])
    inp("ident", [128, 128])
    inp("PT", [128, 128])
    inp("gateT", [DMODEL, 8])
    inp("bias8", [1, 8])
    inp("w1H", [128, 9 * 8 * 2048])
    inp("w3H", [128, 9 * 8 * 2048])
    inp("w2H", [128, 9 * 16 * 1024])
    inp("OH8", [8, 1024])
    inp("wintH", [128, 32 * 2048])
    inp("apar", [128, 1])
    inp("wupf", [128, 16384])
    inp("wgtf", [128, 16384])
    inp("wdnf", [128, 16384])

    outp("xn2T", [128, NFC * TLOC])
    outp("carA", [128, NFC])
    outp("carB", [128, NFC])
    outp("absp", [1, 4])
    for t in debug:
        outp("dbg_" + t, [128, NFC * TLOC])

    es = contextlib.ExitStack()
    with es:
        p = Prog(nc, es)

        def sb(name, shape, dt):
            return es.enter_context(nc.sbuf_tensor(name, shape, dt))

        BX = sb("BX", [128, NFC * TLOC], F32)
        BB = sb("BB", [128, NFC * TLOC], F32)
        BC = sb("BC", [128, NFC * TLOC], BF16)

        xtm = [BB[:, 0:2048], BB[:, 2048:4096]]
        xq = [bc16(BB[:, 4096:5120]), bc16(BB[:, 5120:6144])]
        coslb = bc16(BB[:, 4096:4608])
        sinlb = bc16(BB[:, 4608:5120])
        Kbf = bc16(BB[:, 6144:8192])

        def ybf(h):
            off = 8192 + h * 512
            return bc16(BB[:, off:off + 512])

        hbuf = bc16(BX[:, 0:4096])
        w13 = [bc16(BX[:, 4096 + i * 1024: 4096 + (i + 1) * 1024])
               for i in range(2)]
        w13b = [bc16(BX[:, 6144 + i * 1024: 6144 + (i + 1) * 1024])
                for i in range(2)]
        w2s = [bc16(BX[:, 8192 + i * 512: 8192 + (i + 1) * 512])
               for i in range(2)]
        smb = BX[:, 9216:10240]
        sm_all = BX[:, 11264:11328]
        u_tm = BX[:, 11328:11336]
        m_tm = BX[:, 11336:11344]
        v12 = BX[:, 11344:11346]
        biasb = BX[:, 11352:11360]
        gx4 = BX[:, 0:4096]
        la4 = BX[:, 4096:8192]
        a4 = BX[:, 8192:12288]
        b4 = BX[:, 12288:16384]

        cosq = sb("cosq", [128, TLOC], BF16)
        sinq = sb("sinq", [128, TLOC], BF16)
        Vbf = sb("Vbf", [128, TFULL], BF16)
        mrsb_v = Vbf[:, 0:2048].bitcast(F32)
        wuni = sb("wuni", [128, 4096], BF16)
        wkv = wuni[:, 0:4096]
        wq = [wuni[:, 0:2048], wuni[:, 2048:4096]]
        identt = sb("identt", [128, 128], F32)
        PTb = sb("PTb", [128, 128], BF16)
        qbf = sb("qbf", [128, TLOC], BF16)
        kq_s = sb("kq_s", [128, TLOC], F32)
        carA_sb = kq_s[:, 0:16]
        carB_sb = kq_s[:, 16:32]
        absacc = BX[:, 15488:15492]
        absp_sb = BX[0:1, 15492:15496]
        absl2 = [BX[:, 11360 + i * 2048: 11360 + (i + 1) * 2048]
                 for i in range(2)]
        SC2 = sb("SC2", [128, 512], F32)
        kq_s2 = SC2[:, :]
        sqs = bc16(SC2[:, :])
        sqs_rd = {"T": 0}
        pbf = [bc16(BB[:, i * 512:(i + 1) * 512]) for i in range(4)]
        ones_col_bf = sb("ones_col_bf", [128, 1], BF16)
        ones_row_f = sb("ones_row_f", [1, 128], F32)
        km = sb("km", [1, 1], F32)
        qm = sb("qm", [1, 1], F32)
        negM = sb("negM", [1, 1], F32)
        negMb = sb("negMb", [128, 1], F32)
        rowA = sb("rowA", [1, TLOC], F32)
        rowB = sb("rowB", [1, TLOC], F32)
        rowC = sb("rowC", [1, TLOC], F32)
        rb = sb("rb", [128, TLOC], F32)
        mrsb = None

        gatesb = sb("gatesb", [128, 128], BF16)
        bias8_sb = sb("bias8_sb", [1, 8], F32)
        apar_sb = sb("apar_sb", [128, 1], F32)
        ones_col_f = sb("ones_col_f", [128, 1], F32)
        afneg8 = sb("afneg8", [128, 1], F32)
        OH8_sb = Vbf[:, 2048:4096].bitcast(F32)

        def psum(name):
            return es.enter_context(nc.psum_tensor(name, [128, TLOC], F32))

        PAp, PBp, PCp, PDp = psum("PA"), psum("PB"), psum("PC"), psum("PD")

        T = p.sem("T", "tensor")
        V = p.sem("V", "vector")
        S = p.sem("S", "scalar")
        G = p.sem("G", "gpsimd")
        CONST = p.sem("CONST")
        ABS = p.sem("ABS")
        DS0 = p.sem("DS0")
        DS1 = p.sem("DS1")
        CQ = p.sem("CQ")
        WQ0 = p.sem("WQ0")
        WQ1 = p.sem("WQ1")
        DO = p.sem("DO")

        p.dma("gpsimd", PTb[:, :], d["PT"], CONST)
        p.dma("gpsimd", identt[:, :], d["ident"], CONST)
        p.on("gpsimd", lambda e: e.memset(ones_col_bf[:, :], 1.0), inc=G)
        p.on("gpsimd", lambda e: e.memset(ones_row_f[:, :], 1.0), inc=G)
        p.on("gpsimd", lambda e: e.memset(km[:, :], 0.0), inc=G)
        wkv_src = d["wat"][:, 2048:2304].rearrange("(fc p) o -> p fc o", p=128)
        p.dma("gpsimd", wkv.rearrange("p (fc o) -> p fc o", fc=NFC), wkv_src, CONST)
        p.dma("gpsimd", gatesb[:, :].rearrange("p (fc o) -> p fc o", fc=NFC),
              d["gateT"].rearrange("(fc p) o -> p fc o", p=128), CONST)
        p.dma("gpsimd", bias8_sb[:, :], d["bias8"], CONST)
        p.dma("gpsimd", apar_sb[:, :], d["apar"], CONST)
        p.on("gpsimd", lambda e: e.memset(ones_col_f[:, :], 1.0), inc=G)
        consts = p.now(CONST, G)

        tr_evacs = []

        def transpose_tile(stage, dests, ld):
            for g in range(4):
                half = g % 2
                wv = ([("V", tr_evacs[-2])] if len(tr_evacs) >= 2 else p.now(V))
                for j in range(4):
                    fc = g * 4 + j
                    p.on("tensor", lambda e, s=stage, f=fc, hf=half, c=j * 128:
                         e.transpose(PAp[:, hf * 512 + c: hf * 512 + c + 128],
                                     s[:, f * 128:(f + 1) * 128], identt[:, :]),
                         waits=(ld + wv) if j == 0 else (),
                         inc=T if j == 3 else None)
                tw = p.now(T)
                src3 = PAp[:, half * 512:(half + 1) * 512].rearrange(
                    "p (f c) -> p f c", f=4)
                c = None
                for dst in dests:
                    c = p.on("vector", lambda e, o=dst(g), s=src3:
                             e.tensor_copy(o, s), waits=tw, inc=V)
                tr_evacs.append(c)

        def view3(buf, fc0, t0, span=TLOC):
            return buf.rearrange("p (f t) -> p f t", f=buf.shape[1] // span)[
                :, fc0:fc0 + 4, t0:t0 + 128]

        for tt in range(8):
            stage = xtm[tt % 2]
            dsem = DS0 if tt % 2 == 0 else DS1
            p.dma("sync", stage,
                  d["xloc"][tt * 128:(tt + 1) * 128, :], dsem,
                  waits=p.now(T) + consts)
            ld = p.now(dsem)
            transpose_tile(
                stage,
                [lambda g, tt=tt: view3(BX[:, :], g * 4, tt * 128),
                 lambda g, tt=tt: view3(BC[:, :], g * 4, tt * 128)],
                ld)

        for q in range(4):
            p.dma("gpsimd", cosq[:, :], d["cosT"][:, q * 1024:(q + 1) * 1024], CQ,
                  waits=p.now(V))
            p.dma("gpsimd", sinq[:, :], d["sinT"][:, q * 1024:(q + 1) * 1024], CQ)
            cs_ready = p.now(CQ)
            v_evacs = []
            for tt in range(8):
                stage = xtm[tt % 2]
                dsem = DS0 if tt % 2 == 0 else DS1
                p.dma("sync", stage,
                      d["xb"][q * 1024 + tt * 128: q * 1024 + (tt + 1) * 128, :],
                      dsem, waits=p.now(T))
                ld = p.now(dsem)
                xqb = xq[tt % 2]
                transpose_tile(
                    stage, [lambda g, x=xqb: view3(x, g * 4, 0, span=128)], ld)
                xq_ready = p.now(V)
                for fc in range(NFC):
                    p.on("tensor", lambda e, f=fc, t=tt, x=xqb:
                         e.matmul(PCp[:, t * 128:(t + 1) * 128],
                                  wkv[:, f * 256: f * 256 + 128],
                                  x[:, f * 128:(f + 1) * 128],
                                  start=(f == 0), stop=(f == NFC - 1)),
                         waits=(xq_ready + consts) if fc == 0 else (),
                         inc=T if fc == NFC - 1 else None)
                half = tt % 2
                wv = ([("V", v_evacs[tt - 2])] if tt >= 2 else p.now(V))
                for fc in range(NFC):
                    p.on("tensor", lambda e, t=tt, f=fc, hf=half, x=xqb:
                         e.matmul(PBp[:, hf * 512: hf * 512 + 128],
                                  x[:, f * 128:(f + 1) * 128],
                                  wkv[:, f * 256 + 128: f * 256 + 256],
                                  start=(f == 0), stop=(f == NFC - 1)),
                         waits=wv if fc == 0 else (),
                         inc=T if fc == NFC - 1 else None)
                vmm = p.now(T)
                c = p.on("vector", lambda e, t=tt, q=q, hf=half:
                         e.tensor_copy(Vbf[:, (q * 8 + t) * 128:(q * 8 + t + 1) * 128],
                                       PBp[:, hf * 512: hf * 512 + 128]),
                         waits=vmm, inc=V)
                v_evacs.append(c)
            kmm = p.now(T)
            p.on("vector", lambda e: e.tensor_copy(qbf[:, :], PCp[:, :]),
                 waits=kmm + cs_ready, inc=V)
            p.on("vector", lambda e:
                 e.tensor_tensor(kq_s[:, :], PCp[:, :], cosq[:, :], ALU.mult),
                 inc=V)
            kraw = p.now(V)
            for nh in range(2):
                p.on("tensor", lambda e, n=nh:
                     e.matmul(PBp[:, n * 512:(n + 1) * 512], PTb[:, :],
                              qbf[:, n * 512:(n + 1) * 512], start=True, stop=True),
                     waits=kraw if nh == 0 else (), inc=T)
                rmm = p.now(T)
                p.on("vector", lambda e, n=nh:
                     e.tensor_tensor(kq_s2[:, :], PBp[:, n * 512:(n + 1) * 512],
                                     sinq[:, n * 512:(n + 1) * 512], ALU.mult),
                     waits=rmm, inc=V)
                p.on("vector", lambda e, n=nh, q=q:
                     e.tensor_add(Kbf[:, q * 1024 + n * 512: q * 1024 + (n + 1) * 512],
                                  kq_s[:, n * 512:(n + 1) * 512], kq_s2[:, :]),
                     inc=V)
            kq_done = p.now(V)
            p.on("scalar", lambda e, q=q:
                 e.activation(sqs, Kbf[:, q * 1024:(q + 1) * 1024], AF.Square),
                 waits=kq_done + [("T", sqs_rd["T"])], inc=S)
            sq_done = p.now(S)
            for nh in range(2):
                p.on("tensor", lambda e, n=nh:
                     e.matmul(PDp[0:1, n * 512:(n + 1) * 512], ones_col_bf[:, :],
                              sqs[:, n * 512:(n + 1) * 512], start=True, stop=True),
                     waits=(sq_done + p.now(V)) if nh == 0 else (),
                     inc=T if nh == 1 else None)
            ksum = p.now(T)
            sqs_rd["T"] = ksum[0][1]
            p.on("vector", lambda e: e.reduce_max(rowA[:, 0:1], PDp[0:1, :],
                                                  axis=AX.X),
                 waits=ksum, inc=V)
            p.on("vector", lambda e: e.tensor_max(km[:, :], km[:, :], rowA[:, 0:1]),
                 inc=V)

        if upto == "kv":
            p.on("sync", lambda e: e.nop(), waits=p.now(V, T, S))
            p.emit()
            return nc, d

        p.dma("gpsimd", coslb, d["cosl"], CQ, waits=p.now(T))
        p.dma("gpsimd", sinlb, d["sinl"], CQ)
        csl_ready = p.now(CQ)

        def wq_src(h):
            return d["watqH"][:, h * 2048:(h + 1) * 2048]

        p.dma("gpsimd", wq[0], wq_src(0), WQ0, waits=p.now(T))

        for h in range(NHEAD):
            wqb = wq[h % 2]
            wq_ready = p.now(WQ0 if h % 2 == 0 else WQ1)
            if h + 1 < NHEAD:
                p.dma("gpsimd", wq[(h + 1) % 2], wq_src(h + 1),
                      WQ0 if (h + 1) % 2 == 0 else WQ1,
                      waits=p.now(T))
            bxc_ready = p.now(V)
            for fc in range(NFC):
                for nh in range(2):
                    p.on("tensor", lambda e, f=fc, n=nh, w=wqb:
                         e.matmul(PCp[:, n * 512:(n + 1) * 512],
                                  w[:, f * 128:(f + 1) * 128],
                                  BC[:, f * TLOC + n * 512: f * TLOC + (n + 1) * 512],
                                  start=(f == 0), stop=(f == NFC - 1)),
                         waits=(wq_ready + bxc_ready + csl_ready)
                         if (fc == 0 and nh == 0) else (),
                         inc=T if (fc == NFC - 1 and nh == 1) else None)
            qmm = p.now(T)
            p.on("vector", lambda e: e.tensor_copy(qbf[:, :], PCp[:, :]),
                 waits=qmm, inc=V)
            p.on("vector", lambda e:
                 e.tensor_tensor(kq_s[:, :], PCp[:, :], coslb, ALU.mult),
                 inc=V)
            qraw = p.now(V)
            for nh in range(2):
                p.on("tensor", lambda e, n=nh:
                     e.matmul(PBp[:, n * 512:(n + 1) * 512], PTb[:, :],
                              qbf[:, n * 512:(n + 1) * 512], start=True, stop=True),
                     waits=qraw if nh == 0 else (), inc=T)
                rmm = p.now(T)
                p.on("vector", lambda e, n=nh:
                     e.tensor_tensor(kq_s2[:, :], PBp[:, n * 512:(n + 1) * 512],
                                     sinlb[:, n * 512:(n + 1) * 512], ALU.mult),
                     waits=rmm, inc=V)
                p.on("vector", lambda e, n=nh:
                     e.tensor_add(qbf[:, n * 512:(n + 1) * 512],
                                  kq_s[:, n * 512:(n + 1) * 512], kq_s2[:, :]),
                     inc=V)
            qrope = p.now(V)
            p.on("scalar", lambda e: e.activation(sqs, qbf[:, :], AF.Square),
                 waits=qrope + [("T", sqs_rd["T"])], inc=S)
            sq_done = p.now(S)
            for nh in range(2):
                p.on("tensor", lambda e, n=nh:
                     e.matmul(PDp[0:1, n * 512:(n + 1) * 512], ones_col_bf[:, :],
                              sqs[:, n * 512:(n + 1) * 512], start=True, stop=True),
                     waits=(sq_done + p.now(V)) if nh == 0 else (),
                     inc=T if nh == 1 else None)
            qsum = p.now(T)
            sqs_rd["T"] = qsum[0][1]
            p.on("vector", lambda e: e.reduce_max(qm[:, :], PDp[0:1, :], axis=AX.X),
                 waits=qsum, inc=V)
            p.on("vector", lambda e:
                 e.tensor_tensor(negM[:, :], qm[:, :], km[:, :], ALU.add), inc=V)
            p.on("vector", lambda e:
                 e.tensor_scalar(negM[:, :], negM[:, :], -0.5 * SCALE, None,
                                 ALU.mult), inc=V)
            nm = p.now(V)
            p.on("tensor", lambda e:
                 e.matmul(PDp[:, 0:1], ones_row_f[:, :], negM[:, :],
                          start=True, stop=True), waits=nm, inc=T)
            nmm = p.now(T)
            p.on("vector", lambda e: e.tensor_copy(negMb[:, :], PDp[:, 0:1]),
                 waits=nmm, inc=V)
            negm_ready = p.now(V)

            exp_counts = []
            smm_counts = []

            def plan_s(mc):
                PS_ = PAp if mc % 2 == 0 else PCp
                ws = ([("S", exp_counts[mc - 2])] if mc >= 2 else list(p.now(S)))
                if mc == 0:
                    ws += negm_ready + p.now(V)
                for nh in range(2):
                    p.on("tensor", lambda e, m=mc, n=nh, PS=PS_:
                         e.matmul(PS[:, n * 512:(n + 1) * 512],
                                  Kbf[:, m * 128:(m + 1) * 128],
                                  qbf[:, n * 512:(n + 1) * 512],
                                  start=True, stop=True),
                         waits=ws if nh == 0 else (),
                         inc=T if nh == 1 else None)
                smm_counts.append(p.counts["T"])

            burst_Ts = [0]
            plan_s(0)
            for mc in range(32):
                buf = pbf[mc % 4]
                PS_ = PAp if mc % 2 == 0 else PCp
                ec = p.on("scalar", lambda e, b=buf, PS=PS_:
                          e.activation(b, PS[:, :],
                                       AF.Exp, bias=negMb[:, 0:1], scale=SCALE),
                          waits=[("T", max(smm_counts[mc],
                                           burst_Ts[min(mc // 4, len(burst_Ts) - 1)]))],
                          inc=S)
                exp_counts.append(ec)
                edone = p.now(S)
                if mc + 1 < 32:
                    plan_s(mc + 1)
                for nh in range(2):
                    p.on("tensor", lambda e, m=mc, n=nh, b=buf:
                         e.matmul(PBp[:, n * 512:(n + 1) * 512],
                                  Vbf[:, m * 128:(m + 1) * 128],
                                  b[:, n * 512:(n + 1) * 512],
                                  start=(m == 0), stop=(m == 31)),
                         waits=(edone if nh == 0 else []), inc=None)
                if mc % 4 == 3:
                    for q4 in range(4):
                        for nh in range(2):
                            p.on("tensor", lambda e, m4=mc - 3 + q4, n=nh,
                                 b=pbf[(mc - 3 + q4) % 4]:
                                 e.matmul(PDp[0:1, n * 512:(n + 1) * 512],
                                          ones_col_bf[:, :],
                                          b[:, n * 512:(n + 1) * 512],
                                          start=(m4 == 0), stop=(m4 == 31)),
                                 inc=T if (q4 == 3 and nh == 1) else None)
                    burst_Ts.append(p.counts["T"])
            av = p.now(T)
            p.on("vector", lambda e: e.tensor_copy(rowA[:, :], PDp[0:1, :]),
                 waits=av, inc=V)
            sr = p.now(V)
            p.on("scalar", lambda e:
                 e.activation(rowB[:, :], rowA[:, :], AF.Ln), waits=sr, inc=S)
            p.on("scalar", lambda e:
                 e.activation(rowB[:, :], rowB[:, :], AF.Exp, scale=-1.0), inc=S)
            rr = p.now(S)
            for nh in range(2):
                p.on("tensor", lambda e, n=nh:
                     e.matmul(PDp[:, n * 512:(n + 1) * 512], ones_row_f[:, :],
                              rowB[0:1, n * 512:(n + 1) * 512],
                              start=True, stop=True),
                     waits=rr if nh == 0 else (), inc=T if nh == 1 else None)
            rbmm = p.now(T)
            p.on("scalar", lambda e: e.copy(rb[:, :], PDp[:, :]), waits=rbmm, inc=S)
            rbdone = p.now(S)
            p.on("vector", lambda e, h=h:
                 e.tensor_tensor(ybf(h), PBp[:, :], rb[:, :], ALU.mult),
                 waits=rbdone, inc=V)

        def wpr_src(oc):
            return d["wprojH"][:, oc * 2048:(oc + 1) * 2048]

        p.dma("gpsimd", wq[0], wpr_src(0), WQ0, waits=p.now(T))
        for oc in range(NFC):
            wp = wq[oc % 2]
            wp_ready = p.now(WQ0 if oc % 2 == 0 else WQ1)
            if oc + 1 < NFC:
                p.dma("gpsimd", wq[(oc + 1) % 2], wpr_src(oc + 1),
                      WQ0 if (oc + 1) % 2 == 0 else WQ1,
                      waits=p.now(T))
            yready = p.now(V)
            for fc in range(NFC):
                for nh in range(2):
                    p.on("tensor", lambda e, f=fc, n=nh, w=wp:
                         e.matmul(PCp[:, n * 512:(n + 1) * 512],
                                  w[:, f * 128:(f + 1) * 128],
                                  ybf(f)[:, n * 512:(n + 1) * 512],
                                  start=(f == 0), stop=(f == NFC - 1)),
                         waits=(wp_ready + yready) if (fc == 0 and nh == 0) else (),
                         inc=T if (fc == NFC - 1 and nh == 1) else None)
            pmm = p.now(T)
            p.on("vector", lambda e, o=oc:
                 e.tensor_add(BX[:, o * TLOC:(o + 1) * TLOC],
                              PCp[:, :], BX[:, o * TLOC:(o + 1) * TLOC]),
                 waits=pmm, inc=V)
        r1_done = p.now(V)

        def layernorm(src_f32, dst_f32, dst_bf, waits):
            for fc in range(NFC):
                p.on("vector", lambda e, f=fc:
                     e.tensor_copy(dst_bf[:, f * TLOC:(f + 1) * TLOC],
                                   src_f32[:, f * TLOC:(f + 1) * TLOC]),
                     waits=waits if fc == 0 else (), inc=V)
            cast_done = p.now(V)
            for fc in range(NFC):
                for nh in range(2):
                    p.on("tensor", lambda e, f=fc, n=nh:
                         e.matmul(PDp[0:1, n * 512:(n + 1) * 512],
                                  ones_col_bf[:, :],
                                  dst_bf[:, f * TLOC + n * 512:
                                         f * TLOC + (n + 1) * 512],
                                  start=(f == 0), stop=(f == NFC - 1)),
                         waits=cast_done if (fc == 0 and nh == 0) else (),
                         inc=T if (fc == NFC - 1 and nh == 1) else None)
            for fc in range(NFC):
                p.on("scalar", lambda e, f=fc:
                     e.activation(sqs, dst_bf[:, f * TLOC:(f + 1) * TLOC],
                                  AF.Square),
                     waits=list(cast_done if fc == 0 else [])
                     + [("T", sqs_rd["T"])], inc=S)
                sqd = p.now(S)
                for nh in range(2):
                    p.on("tensor", lambda e, f=fc, n=nh:
                         e.matmul(PAp[0:1, n * 512:(n + 1) * 512],
                                  ones_col_bf[:, :],
                                  sqs[:, n * 512:(n + 1) * 512],
                                  start=(f == 0), stop=(f == NFC - 1)),
                         waits=(sqd + p.now(V)) if nh == 0 else (),
                         inc=T if nh == 1 else None)
                sqs_rd["T"] = p.now(T)[0][1]
            stats_mm = p.now(T)
            p.on("vector", lambda e:
                 e.tensor_scalar(rowA[:, :], PDp[0:1, :], 1.0 / DMODEL, None,
                                 ALU.mult), waits=stats_mm, inc=V)
            p.on("vector", lambda e:
                 e.tensor_tensor(rowB[:, :], rowA[:, :], rowA[:, :], ALU.mult),
                 inc=V)
            p.on("vector", lambda e:
                 e.scalar_tensor_tensor(rowC[:, :], PAp[0:1, :], 1.0 / DMODEL,
                                        rowB[:, :], ALU.mult, ALU.subtract),
                 inc=V)
            p.on("vector", lambda e:
                 e.tensor_scalar(rowC[:, :], rowC[:, :], 1e-5, None, ALU.add),
                 inc=V)
            vd = p.now(V)
            p.on("scalar", lambda e:
                 e.activation(rowC[:, :], rowC[:, :], AF.Ln),
                 waits=vd, inc=S)
            p.on("scalar", lambda e:
                 e.activation(rowC[:, :], rowC[:, :], AF.Exp, scale=-0.5), inc=S)
            rstd_done = p.now(S)
            p.on("vector", lambda e:
                 e.tensor_tensor(rowB[:, :], rowA[:, :], rowC[:, :], ALU.mult),
                 waits=rstd_done, inc=V)
            mrs_done = p.now(V)
            for nh in range(2):
                p.on("tensor", lambda e, n=nh:
                     e.matmul(PDp[:, n * 512:(n + 1) * 512], ones_row_f[:, :],
                              rowC[0:1, n * 512:(n + 1) * 512], start=True,
                              stop=True),
                     waits=mrs_done if nh == 0 else (), inc=None)
                p.on("tensor", lambda e, n=nh:
                     e.matmul(PAp[:, n * 512:(n + 1) * 512], ones_row_f[:, :],
                              rowB[0:1, n * 512:(n + 1) * 512], start=True,
                              stop=True),
                     inc=T if nh == 1 else None)
            bc_mm = p.now(T)
            p.on("scalar", lambda e: e.copy(rb[:, :], PDp[:, :]),
                 waits=bc_mm, inc=S)
            p.on("scalar", lambda e: e.copy(mrsb_v[:, :], PAp[:, :]), inc=S)
            bc_done = p.now(S)
            for fc in range(NFC):
                p.on("vector", lambda e, f=fc:
                     e.tensor_tensor(kq_s[:, :], src_f32[:, f * TLOC:(f + 1) * TLOC],
                                     rb[:, :], ALU.mult),
                     waits=bc_done if fc == 0 else (), inc=V)
                p.on("vector", lambda e, f=fc:
                     e.tensor_sub(dst_f32[:, f * TLOC:(f + 1) * TLOC],
                                  kq_s[:, :], mrsb_v[:, :]), inc=V)
                p.on("vector", lambda e, f=fc:
                     e.tensor_copy(dst_bf[:, f * TLOC:(f + 1) * TLOC],
                                   dst_f32[:, f * TLOC:(f + 1) * TLOC]), inc=V)

        layernorm(BX, BX, BC, r1_done)
        xn1_done = p.now(V)

        if "xn1" in debug:
            p.dma("sync", d["dbg_xn1"], BX[:, :], DO, waits=p.now(V))
        if upto == "xn1":
            p.on("sync", lambda e: e.nop(), waits=p.now(DO, V, T, S))
            p.emit()
            return nc, d

        for fc in range(NFC):
            p.on("vector", lambda e, f=fc:
                 e.tensor_copy(BB[:, f * TLOC:(f + 1) * TLOC],
                               BX[:, f * TLOC:(f + 1) * TLOC]),
                 waits=xn1_done if fc == 0 else (), inc=V)
        acc_init = p.now(V)

        p.dma("gpsimd", OH8_sb[0:8, :], d["OH8"], CONST, waits=p.now(T, V))
        oh_ready = p.now(CONST)
        p.on("tensor", lambda e:
             e.matmul(PDp[:, 0:8], ones_row_f[:, :], bias8_sb[0:1, :],
                      start=True, stop=True), waits=p.now(S, V), inc=T)
        bb_mm = p.now(T)
        p.on("vector", lambda e: e.tensor_copy(biasb, PDp[:, 0:8]),
             waits=bb_mm, inc=V)
        biasb_done = p.now(V)

        sig_counts = []
        for ts in range(8):
            col = (ts % 2) * 512
            wv = ([("S", sig_counts[ts - 2])] if ts >= 2
                  else list(p.now(S)) + list(biasb_done))
            for fc in range(NFC):
                p.on("tensor", lambda e, t=ts, f=fc, c=col:
                     e.matmul(PDp[:, c:c + 8], BC[:, f * TLOC + t * 128:
                                                  f * TLOC + (t + 1) * 128],
                              gatesb[:, f * 8:(f + 1) * 8],
                              start=(f == 0), stop=(f == NFC - 1)),
                     waits=wv if fc == 0 else (), inc=T if fc == NFC - 1 else None)
            smm = p.now(T)
            sc_ = p.on("scalar", lambda e, t=ts, c=col:
                       e.activation(sm_all[:, t * 8:(t + 1) * 8], PDp[:, c:c + 8],
                                    AF.Sigmoid), waits=smm, inc=S)
            sig_counts.append(sc_)
            sgd = p.now(S)
            p.on("vector", lambda e, t=ts:
                 e.tensor_add(u_tm, sm_all[:, t * 8:(t + 1) * 8], biasb),
                 waits=sgd + biasb_done, inc=V)
            p.on("vector", lambda e:
                 e.reduce_max(v12[:, 0:1], u_tm, axis=AX.X), inc=V)
            p.on("vector", lambda e:
                 e.tensor_scalar(m_tm, u_tm, v12[:, 0:1], None, ALU.is_equal),
                 inc=V)
            p.on("vector", lambda e:
                 e.scalar_tensor_tensor(u_tm, m_tm, -1e9, u_tm,
                                        ALU.mult, ALU.add), inc=V)
            p.on("vector", lambda e:
                 e.reduce_max(v12[:, 1:2], u_tm, axis=AX.X), inc=V)
            p.on("vector", lambda e:
                 e.tensor_scalar(u_tm, u_tm, v12[:, 1:2], None, ALU.is_ge),
                 inc=V)
            p.on("vector", lambda e:
                 e.tensor_add(m_tm, m_tm, u_tm), inc=V)
            p.on("vector", lambda e, t=ts:
                 e.tensor_mul(sm_all[:, t * 8:(t + 1) * 8],
                              sm_all[:, t * 8:(t + 1) * 8], m_tm), inc=V)
        sm_done = p.now(V)
        for ts in range(8):
            p.on("tensor", lambda e, t=ts:
                 e.transpose(PDp[0:8, t * 128:(t + 1) * 128],
                             sm_all[:, t * 8:(t + 1) * 8], identt[:, :]),
                 waits=sm_done if ts == 0 else (), inc=T if ts == 7 else None)
        smt_mm = p.now(T)
        p.on("vector", lambda e: e.tensor_copy(rb[0:8, :], PDp[0:8, :]),
             waits=smt_mm, inc=V)
        smt_done = p.now(V)

        M1 = [p.sem("M10"), p.sem("M11")]
        M3 = [p.sem("M30"), p.sem("M31")]
        M2 = [p.sem("M20"), p.sem("M21")]
        scratch2 = BX[:, 10240:11264]

        def w1_src(e, hc):
            return d["w1H"][:, (e * 8 + hc) * 2048:(e * 8 + hc + 1) * 2048]

        def w3_src(e, hc):
            return d["w3H"][:, (e * 8 + hc) * 2048:(e * 8 + hc + 1) * 2048]

        def w2_src(e, oc):
            return d["w2H"][:, (e * 16 + oc) * 1024:(e * 16 + oc + 1) * 1024]

        last_smb_rd = {"V": 0}
        last_w2_mm = {"T": 0}
        h_writes = {"G": 0, "V": 0}
        u_evac = {"V2": [0, 0], "S2": [0, 0], "G2": [0, 0]}

        p.dma("gpsimd", w13[0], w1_src(0, 0), M1[0], waits=acc_init)
        p.dma("gpsimd", w13b[0], w3_src(0, 0), M3[0], waits=acc_init)

        p.on("gpsimd", lambda e: e.memset(absacc[:, :], 0.0),
             waits=acc_init, inc=G)
        absg = {"n": p.now(G)[0][1]}
        abs_parts = [(m, ch) for m in range(3) for ch in range(8)]
        abs_rd = {"V": [0, 0]}

        def plan_abs(k):
            m, ch = abs_parts[k]
            i = k % 2
            dsem = DS0 if i == 0 else DS1
            nm = ("wupf", "wgtf", "wdnf")[m]
            p.dma("sync", absl2[i], d[nm][:, ch * 2048:(ch + 1) * 2048], dsem,
                  waits=[("V", abs_rd["V"][i]), ("G", absg["n"])])
            ld = p.now(dsem)
            p.on("vector", lambda e, i=i:
                 e.tensor_reduce(BX[:, 15484 + i:15485 + i], absl2[i],
                                 axis=AX.X, op=ALU.add,
                                 apply_absolute_value=True),
                 waits=ld, inc=V)
            c = p.on("vector", lambda e, m=m, i=i:
                     e.tensor_add(absacc[:, m:m + 1], absacc[:, m:m + 1],
                                  BX[:, 15484 + i:15485 + i]), inc=V)
            abs_rd["V"][i] = c

        for e in range(9):
            routed = e < 8
            for k in range(e * 3, min(e * 3 + 3, 24)):
                plan_abs(k)
            if routed:
                p.on("tensor", lambda e_, ex=e:
                     e_.matmul(PDp[:, 0:512], OH8_sb[0:8, ex * 128:(ex + 1) * 128],
                               rb[0:8, 0:512], start=True, stop=True),
                     waits=smt_done + oh_ready
                     + [("V", u_evac["V2"][1]), ("S", u_evac["S2"][1]),
                        ("G", u_evac["G2"][1])], inc=None)
                p.on("tensor", lambda e_, ex=e:
                     e_.matmul(PDp[:, 512:1024],
                               OH8_sb[0:8, ex * 128:(ex + 1) * 128],
                               rb[0:8, 512:1024], start=True, stop=True), inc=T)
                smb_mm = p.now(T)
                p.on("scalar", lambda e_: e_.copy(smb, PDp[:, :]),
                     waits=smb_mm + [("V", last_smb_rd["V"])], inc=S)
            smb_done = p.now(S)
            for hc in range(8):
                sl = (e * 8 + hc) % 2
                PU1 = PAp if hc % 2 == 0 else PCp
                PU3 = PBp if hc % 2 == 0 else PDp
                w1rd = p.now(M1[sl])
                w3rd = p.now(M3[sl])
                ne, nhc = (e, hc + 1) if hc < 7 else (e + 1, 0)
                if ne < 9:
                    nsl = (ne * 8 + nhc) % 2
                    p.dma("gpsimd", w13[nsl], w1_src(ne, nhc), M1[nsl],
                          waits=p.now(T))
                    p.dma("gpsimd", w13b[nsl], w3_src(ne, nhc), M3[nsl],
                          waits=p.now(T))
                uw = [("V", u_evac["V2"][hc % 2]), ("S", u_evac["S2"][hc % 2]),
                      ("G", u_evac["G2"][hc % 2])]
                for fc in range(NFC):
                    for nh in range(2):
                        p.on("tensor", lambda e_, f=fc, n=nh, s=sl, PU=PU1:
                             e_.matmul(PU[:, n * 512:(n + 1) * 512],
                                       w13[s][:, f * 128:(f + 1) * 128],
                                       BC[:, f * TLOC + n * 512:
                                          f * TLOC + (n + 1) * 512],
                                       start=(f == 0), stop=(f == NFC - 1)),
                             waits=(w1rd + uw)
                             if (fc == 0 and nh == 0) else (),
                             inc=T if (fc == NFC - 1 and nh == 1) else None)
                for fc in range(NFC):
                    for nh in range(2):
                        p.on("tensor", lambda e_, f=fc, n=nh, s=sl, PU=PU3:
                             e_.matmul(PU[:, n * 512:(n + 1) * 512],
                                       w13b[s][:, f * 128:(f + 1) * 128],
                                       BC[:, f * TLOC + n * 512:
                                          f * TLOC + (n + 1) * 512],
                                       start=(f == 0), stop=(f == NFC - 1)),
                             waits=(w3rd + smb_done)
                             if (fc == 0 and nh == 0) else (),
                             inc=T if (fc == NFC - 1 and nh == 1) else None)
                umm = p.now(T)
                hw = [("T", last_w2_mm["T"])]
                hprev = [("G", h_writes["G"]), ("V", h_writes["V"])]
                if routed:
                    p.on("vector", lambda e_, PU=PU1:
                         e_.tensor_tensor(kq_s[:, :], PU[:, :], smb, ALU.mult),
                         waits=umm + smb_done + hprev, inc=V)
                    kq1 = p.now(V)
                    p.on("scalar", lambda e_:
                         e_.activation(scratch2, kq_s[:, :], AF.Silu),
                         waits=kq1 + hprev, inc=S)
                    sld = p.now(S)
                    c = p.on("vector", lambda e_, PU=PU3:
                             e_.tensor_tensor(kq_s[:, :], PU[:, :], smb,
                                              ALU.mult),
                             waits=sld, inc=V)
                    t3d = p.now(V)
                    g = p.on("gpsimd", lambda e_, h=hc:
                             e_.tensor_mul(hbuf[:, h * 1024:(h + 1) * 1024],
                                           scratch2, kq_s[:, :]),
                             waits=t3d + hw, inc=G)
                    h_writes["G"] = g
                    u_evac["V2"][hc % 2] = c
                    u_evac["S2"][hc % 2] = sld[0][1]
                    u_evac["G2"][hc % 2] = g
                else:
                    p.on("scalar", lambda e_, PU=PU1:
                         e_.activation(scratch2, PU[:, :], AF.Silu),
                         waits=umm + hprev, inc=S)
                    sld = p.now(S)
                    c = p.on("vector", lambda e_, h=hc, PU=PU3:
                             e_.tensor_tensor(hbuf[:, h * 1024:(h + 1) * 1024],
                                              scratch2, PU[:, :], ALU.mult),
                             waits=sld + hw, inc=V)
                    h_writes["V"] = c
                    u_evac["V2"][hc % 2] = c
                    u_evac["S2"][hc % 2] = sld[0][1]
            h_done = [("G", h_writes["G"]), ("V", h_writes["V"])]
            w2_evac = [p.counts["V"], p.counts["V"]]
            p.dma("gpsimd", w2s[0], w2_src(e, 0), M2[0], waits=p.now(T))
            for oc in range(NFC):
                sl = oc % 2
                w2rd = p.now(M2[sl])
                if oc + 1 < NFC:
                    p.dma("gpsimd", w2s[(oc + 1) % 2], w2_src(e, oc + 1),
                          M2[(oc + 1) % 2], waits=p.now(T))
                PW = PCp if oc % 2 == 0 else PAp
                wv2 = ([("V", w2_evac[oc % 2])]
                       + (h_done + [("V", p.counts["V"])] if oc == 0 else []))
                for kc in range(8):
                    for nh in range(2):
                        p.on("tensor", lambda e_, k=kc, n=nh, s=sl, PW=PW:
                             e_.matmul(PW[:, n * 512:(n + 1) * 512],
                                       w2s[s][:, k * 128:(k + 1) * 128],
                                       hbuf[:, k * 1024 + n * 512:
                                            k * 1024 + (n + 1) * 512],
                                       start=(k == 0), stop=(k == 7)),
                             waits=(w2rd + wv2)
                             if (kc == 0 and nh == 0) else (),
                             inc=T if (kc == 7 and nh == 1) else None)
                wmm = p.now(T)
                last_w2_mm["T"] = wmm[0][1]
                c = p.on("vector", lambda e_, o=oc, PW=PW:
                         e_.scalar_tensor_tensor(BB[:, o * TLOC:(o + 1) * TLOC],
                                                 PW[:, :], 1.0,
                                                 BB[:, o * TLOC:(o + 1) * TLOC],
                                                 ALU.mult, ALU.add),
                         waits=wmm, inc=V)
                w2_evac[oc % 2] = c
            u_evac["V2"][0] = max(u_evac["V2"][0], p.counts["V"])
            u_evac["V2"][1] = max(u_evac["V2"][1], p.counts["V"])
            last_smb_rd["V"] = p.counts["V"]
        moe_done = p.now(V)

        abs_done = p.now(V)
        for m in range(3):
            p.on("tensor", lambda e, m=m:
                 e.matmul(PDp[0:1, m:m + 1], absacc[:, m:m + 1],
                          ones_col_f[:, :], start=True, stop=True),
                 waits=abs_done if m == 0 else (), inc=T if m == 2 else None)
        absmm = p.now(T)
        p.on("vector", lambda e: e.tensor_copy(absp_sb[0:1, 0:3],
                                               PDp[0:1, 0:3]),
             waits=absmm, inc=V)
        p.on("vector", lambda e: e.tensor_copy(absp_sb[0:1, 3:4],
                                               absp_sb[0:1, 0:1]), inc=V)
        p.dma("sync", d["absp"], absp_sb, ABS, waits=p.now(V))
        absdo = p.now(ABS)


        if "r2" in debug:
            p.dma("sync", d["dbg_r2"], BB[:, :], DO, waits=moe_done)

        layernorm(BB, BB, BC, moe_done)
        xn2_done = p.now(V)
        p.dma("sync", d["xn2T"], BB[:, :], DO, waits=xn2_done)
        if "xn2" in debug:
            p.dma("sync", d["dbg_xn2"], BB[:, :], DO, waits=xn2_done)
        if upto == "xn2":
            p.on("sync", lambda e: e.nop(), waits=p.now(DO, V, T, S, G))
            p.emit()
            return nc, d

        p.on("scalar", lambda e: e.activation(v12[:, 0:1], apar_sb[:, :],
                                              AF.Abs),
             waits=consts + p.now(V), inc=S)
        p.on("scalar", lambda e:
             e.activation(v12[:, 0:1], v12[:, 0:1], AF.Exp, scale=-1.0), inc=S)
        p.on("scalar", lambda e:
             e.activation(v12[:, 0:1], v12[:, 0:1], AF.Ln, bias=1.0), inc=S)
        sp_done = p.now(S)
        p.on("vector", lambda e: e.tensor_relu(v12[:, 1:2], apar_sb[:, :]),
             waits=sp_done, inc=V)
        p.on("vector", lambda e:
             e.tensor_add(afneg8[:, :], v12[:, 0:1], v12[:, 1:2]), inc=V)
        p.on("vector", lambda e:
             e.tensor_scalar(afneg8[:, :], afneg8[:, :], -8.0, None, ALU.mult),
             inc=V)
        af_done = p.now(V)
        p.on("gpsimd", lambda e: e.memset(rb[:, :], 0.0),
             waits=p.now(S, V), inc=G)
        zeros_done = p.now(G)

        def wg_src(oc):
            return d["wintH"][:, oc * 2048:(oc + 1) * 2048]

        gsl4 = [wuni[:, 0:2048], wuni[:, 2048:4096],
                Vbf[:, 0:2048], Vbf[:, 2048:4096]]
        gw_sem = [WQ0, WQ1, M1[0], M1[1]]
        gw_rd = {"T": [0, 0, 0, 0]}
        gw_first = p.now(V, S)

        def load_gw(c):
            for k, oc in ((0, c), (1, 16 + c)):
                sl = (2 * c + k) % 4
                p.dma("gpsimd", gsl4[sl], wg_src(oc), gw_sem[sl],
                      waits=[("T", gw_rd["T"][sl])]
                      + (gw_first if c < 2 else []))

        load_gw(0)
        for grp in range(4):
            grp_v = p.now(V)
            for j in range(4):
                c = grp * 4 + j
                slx = (2 * c) % 4
                sla = (2 * c + 1) % 4
                gxw = p.now(gw_sem[slx])
                gaw = p.now(gw_sem[sla])
                if c + 1 < 16:
                    load_gw(c + 1)
                for fc in range(NFC):
                    for nh in range(2):
                        p.on("tensor", lambda e, f=fc, n=nh, s=slx:
                             e.matmul(PAp[:, n * 512:(n + 1) * 512],
                                      gsl4[s][:, f * 128:(f + 1) * 128],
                                      BC[:, f * TLOC + n * 512:
                                         f * TLOC + (n + 1) * 512],
                                      start=(f == 0), stop=(f == NFC - 1)),
                             waits=(gxw + [("S", p.counts["S"])])
                             if (fc == 0 and nh == 0) else (),
                             inc=T if (fc == NFC - 1 and nh == 1) else None)
                gxmm = p.now(T)
                gw_rd["T"][slx] = gxmm[0][1]
                p.on("scalar", lambda e, j=j:
                     e.activation(gx4[:, j * 1024:(j + 1) * 1024], PAp[:, :],
                                  AF.Sigmoid), waits=gxmm + grp_v + absdo, inc=S)
                for fc in range(NFC):
                    for nh in range(2):
                        p.on("tensor", lambda e, f=fc, n=nh, s=sla:
                             e.matmul(PBp[:, n * 512:(n + 1) * 512],
                                      gsl4[s][:, f * 128:(f + 1) * 128],
                                      BC[:, f * TLOC + n * 512:
                                         f * TLOC + (n + 1) * 512],
                                      start=(f == 0), stop=(f == NFC - 1)),
                             waits=(gaw + [("S", p.counts["S"])])
                             if (fc == 0 and nh == 0) else (),
                             inc=T if (fc == NFC - 1 and nh == 1) else None)
                gamm = p.now(T)
                gw_rd["T"][sla] = gamm[0][1]
                p.on("scalar", lambda e, j=j:
                     e.activation(la4[:, j * 1024:(j + 1) * 1024], PBp[:, :],
                                  AF.Sigmoid), waits=gamm + grp_v, inc=S)
            sig_done = p.now(S)
            for j in range(4):
                p.on("vector", lambda e, j=j:
                     e.tensor_scalar(la4[:, j * 1024:(j + 1) * 1024],
                                     la4[:, j * 1024:(j + 1) * 1024],
                                     afneg8[:, 0:1], None, ALU.mult),
                     waits=(list(sig_done) + list(af_done)) if j == 0 else (),
                     inc=V)
            la_done = p.now(V)
            for j in range(4):
                p.on("scalar", lambda e, j=j:
                     e.activation(a4[:, j * 1024:(j + 1) * 1024],
                                  la4[:, j * 1024:(j + 1) * 1024], AF.Exp),
                     waits=la_done if j == 0 else (), inc=S)
                ps_t = (PAp, PBp, PCp, PDp)[j]
                p.on("scalar", lambda e, j=j, t=ps_t:
                     e.activation(t[:, :], la4[:, j * 1024:(j + 1) * 1024],
                                  AF.Exp, scale=2.0),
                     waits=[("T", p.counts["T"])], inc=S)
            e2_done = p.now(S)
            for j in range(4):
                ps_t = (PAp, PBp, PCp, PDp)[j]
                p.on("scalar", lambda e, j=j, t=ps_t:
                     e.activation(b4[:, j * 1024:(j + 1) * 1024], t[:, :],
                                  AF.Sqrt, bias=1.0, scale=-1.0),
                     waits=e2_done if j == 0 else (), inc=S)
            mult_done = p.now(S)
            for j in range(4):
                c = grp * 4 + j
                p.on("vector", lambda e, j=j:
                     e.tensor_mul(b4[:, j * 1024:(j + 1) * 1024],
                                  b4[:, j * 1024:(j + 1) * 1024],
                                  gx4[:, j * 1024:(j + 1) * 1024]),
                     waits=mult_done if j == 0 else (), inc=V)
                p.on("vector", lambda e, j=j, c=c:
                     e.tensor_mul(b4[:, j * 1024:(j + 1) * 1024],
                                  b4[:, j * 1024:(j + 1) * 1024],
                                  BB[:, c * TLOC:(c + 1) * TLOC]), inc=V)
                p.on("vector", lambda e, j=j:
                     e.tensor_tensor_scan(gx4[:, j * 1024:(j + 1) * 1024],
                                          a4[:, j * 1024:(j + 1) * 1024],
                                          b4[:, j * 1024:(j + 1) * 1024],
                                          0.0, ALU.mult, ALU.add),
                     waits=zeros_done, inc=V)
                p.on("vector", lambda e, j=j, c=c:
                     e.tensor_copy(carB_sb[:, c:c + 1],
                                   gx4[:, j * 1024 + 1023: j * 1024 + 1024]),
                     inc=V)
                p.on("vector", lambda e, j=j:
                     e.tensor_tensor_scan(la4[:, j * 1024:(j + 1) * 1024],
                                          a4[:, j * 1024:(j + 1) * 1024],
                                          rb[:, :], 1.0, ALU.mult, ALU.add),
                     inc=V)
                p.on("vector", lambda e, j=j, c=c:
                     e.tensor_copy(carA_sb[:, c:c + 1],
                                   la4[:, j * 1024 + 1023: j * 1024 + 1024]),
                     inc=V)
        scans_done = p.now(V)
        p.dma("sync", d["carA"], carA_sb, DO, waits=scans_done)
        p.dma("sync", d["carB"], carB_sb, DO, waits=scans_done)


        p.on("sync", lambda e: e.nop(), waits=p.now(DO, ABS, V, T, S, G))
        p.emit()

    return nc, d



def build_launch_b(debug=(), upto="all"):
    nc = bass.Bass("TRN2", target_bir_lowering=False)
    d = {}

    def inp(name, shape, dt=F32):
        d[name] = nc.dram_tensor(name, shape, dt, kind="ExternalInput").ap()
        return d[name]

    def outp(name, shape, dt=F32):
        d[name] = nc.dram_tensor(name, shape, dt, kind="ExternalOutput").ap()
        return d[name]

    inp("xn2T", [128, NFC * TLOC])
    inp("carApre", [128, 3 * NFC])
    inp("carBpre", [128, 3 * NFC])
    inp("absps", [8, 4])
    inp("wintH", [128, 32 * 2048])
    inp("apar", [128, 1])
    inp("wupH", [128, 32 * 4096])
    inp("wgtH", [128, 32 * 4096])
    inp("wdnT", [8192, DMODEL])

    outp("yT", [128, NFC * TLOC])
    for t in debug:
        outp("dbg_" + t, [128, NFC * TLOC])

    es = contextlib.ExitStack()
    with es:
        p = Prog(nc, es)

        def sb(name, shape, dt):
            return es.enter_context(nc.sbuf_tensor(name, shape, dt))

        BB = sb("BB", [128, NFC * TLOC], F32)
        BC = sb("BC", [128, NFC * TLOC], BF16)
        BX = sb("BX", [128, NFC * TLOC], F32)

        gx4 = BX[:, 0:4096]
        la4 = BX[:, 4096:8192]
        a4 = BX[:, 8192:12288]
        b4 = BX[:, 12288:16384]
        qup = bc16(BX[:, 0:4096])
        qgt = bc16(BX[:, 4096:8192])
        qdn = bc16(BX[:, 8192:12288])
        stg = [BX[:, 12288 + i * 2048: 12288 + (i + 1) * 2048] for i in range(2)]

        hb = sb("hb", [128, 2048], BF16)
        i8t = [sb(f"i8t{i}", [128, 2048], I8) for i in range(2)]
        wuni = sb("wuni", [128, 4096], BF16)
        wuni2 = sb("wuni2", [128, 4096], BF16)
        gsl4 = [wuni[:, 0:2048], wuni[:, 2048:4096],
                wuni2[:, 0:2048], wuni2[:, 2048:4096]]
        carA_t = sb("carA_t", [128, 3 * NFC], F32)
        carB_t = sb("carB_t", [128, 3 * NFC], F32)
        Hc = sb("Hc", [128, NFC], F32)
        apar_sb = sb("apar_sb", [128, 1], F32)
        afneg8 = sb("afneg8", [128, 1], F32)
        absps_sb = sb("absps_sb", [8, 4], F32)
        rsig = sb("rsig", [128, 4], F32)
        sig = sb("sig", [128, 4], F32)
        shalf = sb("shalf", [128, 4], F32)
        nshalf = sb("nshalf", [128, 4], F32)
        rows3 = sb("rows3", [1, 4], F32)
        kq_s = sb("kq_s", [128, TLOC], F32)
        SC2 = sb("SC2", [128, 512], F32)
        sqs = bc16(SC2[:, :])
        sqs_rd = {"T": 0}
        ones_col_bf = sb("ones_col_bf", [128, 1], BF16)
        ones_col_f = sb("ones_col_f", [128, 1], F32)
        ones_row_f = sb("ones_row_f", [1, 128], F32)
        rowA = sb("rowA", [1, TLOC], F32)
        rowB = sb("rowB", [1, TLOC], F32)
        rowC = sb("rowC", [1, TLOC], F32)
        rb = sb("rb", [128, TLOC], F32)
        mrsb_v = wuni2[:, 0:2048].bitcast(F32)

        def psum(name):
            return es.enter_context(nc.psum_tensor(name, [128, TLOC], F32))

        PAp, PBp, PCp, PDp = psum("PA"), psum("PB"), psum("PC"), psum("PD")
        PS4 = [PAp, PBp, PCp, PDp]

        T = p.sem("T", "tensor")
        V = p.sem("V", "vector")
        S = p.sem("S", "scalar")
        G = p.sem("G", "gpsimd")
        CONST = p.sem("CONST")
        DS0 = p.sem("DS0")
        DS1 = p.sem("DS1")
        WQ0 = p.sem("WQ0")
        WQ1 = p.sem("WQ1")
        UP0 = p.sem("UP0")
        UP1 = p.sem("UP1")
        GT0 = p.sem("GT0")
        GT1 = p.sem("GT1")
        DN0 = p.sem("DN0")
        DN1 = p.sem("DN1")
        DO = p.sem("DO")

        p.dma("gpsimd", carA_t[:, :], d["carApre"], CONST)
        p.dma("gpsimd", carB_t[:, :], d["carBpre"], CONST)
        p.dma("gpsimd", apar_sb[:, :], d["apar"], CONST)
        p.dma("gpsimd", absps_sb[:, :], d["absps"], CONST)
        p.on("gpsimd", lambda e: e.memset(ones_col_bf[:, :], 1.0), inc=G)
        p.on("gpsimd", lambda e: e.memset(ones_col_f[:, :], 1.0), inc=G)
        p.on("gpsimd", lambda e: e.memset(ones_row_f[:, :], 1.0), inc=G)
        p.dma("sync", BB[:, :], d["xn2T"], DS0)
        consts = p.now(CONST, G, DS0)

        for fc in range(NFC):
            p.on("vector", lambda e, f=fc:
                 e.tensor_copy(BC[:, f * TLOC:(f + 1) * TLOC],
                               BB[:, f * TLOC:(f + 1) * TLOC]),
                 waits=consts if fc == 0 else (), inc=V)
        bc_done = p.now(V)

        p.on("vector", lambda e: e.tensor_copy(Hc[:, :], carB_t[:, 0:NFC]),
             waits=consts, inc=V)
        for j in (1, 2):
            p.on("vector", lambda e, j=j:
                 e.tensor_mul(Hc[:, :], Hc[:, :],
                              carA_t[:, j * NFC:(j + 1) * NFC]), inc=V)
            p.on("vector", lambda e, j=j:
                 e.tensor_add(Hc[:, :], Hc[:, :],
                              carB_t[:, j * NFC:(j + 1) * NFC]), inc=V)
        h_ready = p.now(V)

        p.on("scalar", lambda e: e.activation(rsig[:, 3:4], apar_sb[:, :], AF.Abs),
             waits=consts, inc=S)
        p.on("scalar", lambda e:
             e.activation(rsig[:, 3:4], rsig[:, 3:4], AF.Exp, scale=-1.0), inc=S)
        p.on("scalar", lambda e:
             e.activation(rsig[:, 3:4], rsig[:, 3:4], AF.Ln, bias=1.0), inc=S)
        sp_done = p.now(S)
        p.on("vector", lambda e: e.tensor_relu(afneg8[:, :], apar_sb[:, :]),
             waits=sp_done, inc=V)
        p.on("vector", lambda e:
             e.tensor_add(afneg8[:, :], afneg8[:, :], rsig[:, 3:4]), inc=V)
        p.on("vector", lambda e:
             e.tensor_scalar(afneg8[:, :], afneg8[:, :], -8.0, None, ALU.mult),
             inc=V)
        af_done = p.now(V)

        def wg_src(oc):
            return d["wintH"][:, oc * 2048:(oc + 1) * 2048]

        gw_sem = [WQ0, WQ1, GT0, GT1]
        gw_rd = {"T": [0, 0, 0, 0]}

        def load_gw(c):
            for k, oc in ((0, c), (1, 16 + c)):
                sl = (2 * c + k) % 4
                p.dma("gpsimd", gsl4[sl], wg_src(oc), gw_sem[sl],
                      waits=[("T", gw_rd["T"][sl])])
            return p.now(gw_sem[(2 * c) % 4], gw_sem[(2 * c + 1) % 4])

        load_gw(0)
        for grp in range(4):
            grp_v = p.now(V)
            for j in range(4):
                c = grp * 4 + j
                slx = (2 * c) % 4
                sla = (2 * c + 1) % 4
                gxw = p.now(gw_sem[slx])
                gaw = p.now(gw_sem[sla])
                if c + 1 < 16:
                    load_gw(c + 1)
                for fc in range(NFC):
                    for nh in range(2):
                        p.on("tensor", lambda e, f=fc, n=nh, s=slx:
                             e.matmul(PAp[:, n * 512:(n + 1) * 512],
                                      gsl4[s][:, f * 128:(f + 1) * 128],
                                      BC[:, f * TLOC + n * 512:
                                         f * TLOC + (n + 1) * 512],
                                      start=(f == 0), stop=(f == NFC - 1)),
                             waits=(list(gxw) + list(bc_done)
                                    + [("S", p.counts["S"])])
                             if (fc == 0 and nh == 0) else (),
                             inc=T if (fc == NFC - 1 and nh == 1) else None)
                gxmm = p.now(T)
                gw_rd["T"][slx] = gxmm[0][1]
                p.on("scalar", lambda e, j=j:
                     e.activation(gx4[:, j * 1024:(j + 1) * 1024], PAp[:, :],
                                  AF.Sigmoid), waits=gxmm + grp_v, inc=S)
                for fc in range(NFC):
                    for nh in range(2):
                        p.on("tensor", lambda e, f=fc, n=nh, s=sla:
                             e.matmul(PBp[:, n * 512:(n + 1) * 512],
                                      gsl4[s][:, f * 128:(f + 1) * 128],
                                      BC[:, f * TLOC + n * 512:
                                         f * TLOC + (n + 1) * 512],
                                      start=(f == 0), stop=(f == NFC - 1)),
                             waits=(list(gaw) + [("S", p.counts["S"])])
                             if (fc == 0 and nh == 0) else (),
                             inc=T if (fc == NFC - 1 and nh == 1) else None)
                gamm = p.now(T)
                gw_rd["T"][sla] = gamm[0][1]
                p.on("scalar", lambda e, j=j:
                     e.activation(la4[:, j * 1024:(j + 1) * 1024], PBp[:, :],
                                  AF.Sigmoid), waits=gamm + grp_v, inc=S)
            sig_done = p.now(S)
            for j in range(4):
                p.on("vector", lambda e, j=j:
                     e.tensor_scalar(la4[:, j * 1024:(j + 1) * 1024],
                                     la4[:, j * 1024:(j + 1) * 1024],
                                     afneg8[:, 0:1], None, ALU.mult),
                     waits=(list(sig_done) + list(af_done)) if j == 0 else (),
                     inc=V)
            la_done = p.now(V)
            for j in range(4):
                p.on("scalar", lambda e, j=j:
                     e.activation(a4[:, j * 1024:(j + 1) * 1024],
                                  la4[:, j * 1024:(j + 1) * 1024], AF.Exp),
                     waits=la_done if j == 0 else (), inc=S)
                ps_t = (PAp, PBp, PCp, PDp)[j]
                p.on("scalar", lambda e, j=j, t=ps_t:
                     e.activation(t[:, :], la4[:, j * 1024:(j + 1) * 1024],
                                  AF.Exp, scale=2.0),
                     waits=[("T", p.counts["T"])], inc=S)
            e2_done = p.now(S)
            for j in range(4):
                ps_t = (PAp, PBp, PCp, PDp)[j]
                p.on("scalar", lambda e, j=j, t=ps_t:
                     e.activation(b4[:, j * 1024:(j + 1) * 1024], t[:, :],
                                  AF.Sqrt, bias=1.0, scale=-1.0),
                     waits=e2_done if j == 0 else (), inc=S)
            mult_done = p.now(S)
            for j in range(4):
                c = grp * 4 + j
                p.on("vector", lambda e, j=j:
                     e.tensor_mul(b4[:, j * 1024:(j + 1) * 1024],
                                  b4[:, j * 1024:(j + 1) * 1024],
                                  gx4[:, j * 1024:(j + 1) * 1024]),
                     waits=mult_done if j == 0 else (), inc=V)
                p.on("vector", lambda e, j=j, c=c:
                     e.tensor_mul(b4[:, j * 1024:(j + 1) * 1024],
                                  b4[:, j * 1024:(j + 1) * 1024],
                                  BB[:, c * TLOC:(c + 1) * TLOC]), inc=V)
                p.on("vector", lambda e, j=j, c=c:
                     e.tensor_tensor_scan(gx4[:, j * 1024:(j + 1) * 1024],
                                          a4[:, j * 1024:(j + 1) * 1024],
                                          b4[:, j * 1024:(j + 1) * 1024],
                                          Hc[:, c:c + 1], ALU.mult, ALU.add),
                     waits=h_ready, inc=V)
                p.on("vector", lambda e, j=j, c=c:
                     e.tensor_add(BB[:, c * TLOC:(c + 1) * TLOC],
                                  gx4[:, j * 1024:(j + 1) * 1024],
                                  BB[:, c * TLOC:(c + 1) * TLOC]), inc=V)
        r3_done = p.now(V)

        def layernorm(src_f32, dst_f32, dst_bf, waits):
            for fc in range(NFC):
                p.on("vector", lambda e, f=fc:
                     e.tensor_copy(dst_bf[:, f * TLOC:(f + 1) * TLOC],
                                   src_f32[:, f * TLOC:(f + 1) * TLOC]),
                     waits=waits if fc == 0 else (), inc=V)
            cast_done = p.now(V)
            for fc in range(NFC):
                for nh in range(2):
                    p.on("tensor", lambda e, f=fc, n=nh:
                         e.matmul(PDp[0:1, n * 512:(n + 1) * 512],
                                  ones_col_bf[:, :],
                                  dst_bf[:, f * TLOC + n * 512:
                                         f * TLOC + (n + 1) * 512],
                                  start=(f == 0), stop=(f == NFC - 1)),
                         waits=cast_done if (fc == 0 and nh == 0) else (),
                         inc=T if (fc == NFC - 1 and nh == 1) else None)
            for fc in range(NFC):
                p.on("scalar", lambda e, f=fc:
                     e.activation(sqs, dst_bf[:, f * TLOC:(f + 1) * TLOC],
                                  AF.Square),
                     waits=list(cast_done if fc == 0 else [])
                     + [("T", sqs_rd["T"])], inc=S)
                sqd = p.now(S)
                for nh in range(2):
                    p.on("tensor", lambda e, f=fc, n=nh:
                         e.matmul(PAp[0:1, n * 512:(n + 1) * 512],
                                  ones_col_bf[:, :],
                                  sqs[:, n * 512:(n + 1) * 512],
                                  start=(f == 0), stop=(f == NFC - 1)),
                         waits=(sqd + p.now(V)) if nh == 0 else (),
                         inc=T if nh == 1 else None)
                sqs_rd["T"] = p.now(T)[0][1]
            stats_mm = p.now(T)
            p.on("vector", lambda e:
                 e.tensor_scalar(rowA[:, :], PDp[0:1, :], 1.0 / DMODEL, None,
                                 ALU.mult), waits=stats_mm, inc=V)
            p.on("vector", lambda e:
                 e.tensor_tensor(rowB[:, :], rowA[:, :], rowA[:, :], ALU.mult),
                 inc=V)
            p.on("vector", lambda e:
                 e.scalar_tensor_tensor(rowC[:, :], PAp[0:1, :], 1.0 / DMODEL,
                                        rowB[:, :], ALU.mult, ALU.subtract),
                 inc=V)
            p.on("vector", lambda e:
                 e.tensor_scalar(rowC[:, :], rowC[:, :], 1e-5, None, ALU.add),
                 inc=V)
            vd = p.now(V)
            p.on("scalar", lambda e:
                 e.activation(rowC[:, :], rowC[:, :], AF.Ln),
                 waits=vd, inc=S)
            p.on("scalar", lambda e:
                 e.activation(rowC[:, :], rowC[:, :], AF.Exp, scale=-0.5), inc=S)
            rstd_done = p.now(S)
            p.on("vector", lambda e:
                 e.tensor_tensor(rowB[:, :], rowA[:, :], rowC[:, :], ALU.mult),
                 waits=rstd_done, inc=V)
            mrs_done = p.now(V)
            for nh in range(2):
                p.on("tensor", lambda e, n=nh:
                     e.matmul(PDp[:, n * 512:(n + 1) * 512], ones_row_f[:, :],
                              rowC[0:1, n * 512:(n + 1) * 512], start=True,
                              stop=True),
                     waits=mrs_done if nh == 0 else (), inc=None)
                p.on("tensor", lambda e, n=nh:
                     e.matmul(PAp[:, n * 512:(n + 1) * 512], ones_row_f[:, :],
                              rowB[0:1, n * 512:(n + 1) * 512], start=True,
                              stop=True),
                     inc=T if nh == 1 else None)
            bc_mm = p.now(T)
            p.on("scalar", lambda e: e.copy(rb[:, :], PDp[:, :]),
                 waits=bc_mm, inc=S)
            p.on("scalar", lambda e: e.copy(mrsb_v[:, :], PAp[:, :]), inc=S)
            bcd = p.now(S)
            for fc in range(NFC):
                p.on("vector", lambda e, f=fc:
                     e.tensor_tensor(kq_s[:, :], src_f32[:, f * TLOC:(f + 1) * TLOC],
                                     rb[:, :], ALU.mult),
                     waits=bcd if fc == 0 else (), inc=V)
                p.on("vector", lambda e, f=fc:
                     e.tensor_sub(dst_f32[:, f * TLOC:(f + 1) * TLOC],
                                  kq_s[:, :], mrsb_v[:, :]), inc=V)
                p.on("vector", lambda e, f=fc:
                     e.tensor_copy(dst_bf[:, f * TLOC:(f + 1) * TLOC],
                                   dst_f32[:, f * TLOC:(f + 1) * TLOC]), inc=V)

        layernorm(BB, BB, BC, r3_done)
        xn3_done = p.now(V)
        dbg_guard = []
        if "xn3" in debug:
            p.dma("sync", d["dbg_xn3"], BB[:, :], DO, waits=p.now(V))
            dbg_guard = p.now(DO)
        if upto == "xn3":
            p.on("sync", lambda e: e.nop(), waits=p.now(DO, V, T, S, G))
            p.emit()
            return nc, d

        p.on("tensor", lambda e:
             e.matmul(PDp[0:1, 0:4], ones_col_f[0:8, :], absps_sb[:, :],
                      start=True, stop=True),
             waits=consts + p.now(V, S), inc=T)
        s_mm = p.now(T)
        p.on("vector", lambda e:
             e.tensor_scalar(rows3[:, :], PDp[0:1, 0:4], 1.0 / 16777216.0,
                             None, ALU.mult), waits=s_mm, inc=V)
        p.on("vector", lambda e:
             e.tensor_scalar(rows3[:, :], rows3[:, :], 1e-8, None, ALU.add),
             inc=V)
        sc_done = p.now(V)
        p.on("scalar", lambda e: e.activation(rows3[:, :], rows3[:, :], AF.Ln),
             waits=sc_done, inc=S)
        p.on("scalar", lambda e:
             e.activation(rows3[:, :], rows3[:, :], AF.Exp, scale=-1.0), inc=S)
        rs_done = p.now(S)
        p.on("tensor", lambda e:
             e.matmul(PDp[:, 0:4], ones_row_f[:, :], rows3[0:1, :],
                      start=True, stop=True), waits=rs_done, inc=T)
        rs_mm = p.now(T)
        p.on("vector", lambda e: e.tensor_copy(rsig[:, :], PDp[:, 0:4]),
             waits=rs_mm, inc=V)
        p.on("scalar", lambda e:
             e.activation(rows3[:, :], rows3[:, :], AF.Ln),
             waits=p.now(T), inc=S)
        p.on("scalar", lambda e:
             e.activation(rows3[:, :], rows3[:, :], AF.Exp, scale=-1.0), inc=S)
        s2_done = p.now(S)
        p.on("tensor", lambda e:
             e.matmul(PDp[:, 4:8], ones_row_f[:, :], rows3[0:1, :],
                      start=True, stop=True), waits=s2_done, inc=T)
        s2_mm = p.now(T)
        p.on("vector", lambda e: e.tensor_copy(sig[:, :], PDp[:, 4:8]),
             waits=s2_mm, inc=V)
        p.on("vector", lambda e:
             e.tensor_scalar(shalf[:, :], sig[:, :], 0.5, None, ALU.mult),
             inc=V)
        p.on("vector", lambda e:
             e.tensor_scalar(nshalf[:, :], sig[:, :], -0.5, None, ALU.mult),
             inc=V)
        rsig_done = p.now(V)

        NG = 32
        QUP = [bc16(BX[:, p * 6144 + 0: p * 6144 + 2048]) for p in range(2)]
        QGT = [bc16(BX[:, p * 6144 + 2048: p * 6144 + 4096]) for p in range(2)]
        QDN = [bc16(BX[:, p * 6144 + 4096: p * 6144 + 6144]) for p in range(2)]
        stg = [BX[:, 12288 + i * 2048: 12288 + (i + 1) * 2048] for i in range(2)]

        ffn_v0 = p.counts["V"]
        ffn_s0 = p.counts["S"]
        stg_rd = {"V": [ffn_v0, ffn_v0]}
        i8_rd = {"S": [ffn_s0, ffn_s0]}
        q_rd = {"T": [0, 0]}
        hb_rd = {"T": 0}
        kq_rd = {"V": 0}
        pa_rd = {"S": 0}

        def quantize_group(g):
            par = g % 2
            for pi in range(6):
                mat, pc = divmod(pi, 2)
                sl = pi % 2
                sem = UP0 if sl == 0 else UP1
                if mat < 2:
                    w = d["wupH"] if mat == 0 else d["wgtH"]
                    srcap = w[:, g * 4096 + pc * 2048: g * 4096 + (pc + 1) * 2048]
                    dst = stg[sl]
                    qb = QUP[par] if mat == 0 else QGT[par]
                    qslice = qb[:, pc * 2048:(pc + 1) * 2048]
                else:
                    srcap = d["wdnT"][g * 256 + pc * 128:
                                      g * 256 + (pc + 1) * 128, :]
                    dst = stg[sl]
                    qslice = QDN[par][:, pc * 2048:(pc + 1) * 2048]
                eng = "sync" if sl == 0 else "gpsimd"
                p.dma(eng, dst, srcap, sem,
                      waits=[("V", stg_rd["V"][sl]), ("T", q_rd["T"][par]),
                             ("S", ffn_s0)])
                ld = p.now(sem)
                p.on("vector", lambda e, sl=sl, m=min(mat, 2):
                     e.tensor_scalar(i8t[sl][:, :], stg[sl],
                                     rsig[:, m:m + 1], None, ALU.mult),
                     waits=ld + rsig_done + [("S", i8_rd["S"][sl])], inc=V)
                stg_rd["V"][sl] = p.counts["V"]
                i8d = p.now(V)
                p.on("scalar", lambda e, sl=sl, qslice=qslice:
                     e.activation(qslice, i8t[sl][:, :], AF.Sign),
                     waits=i8d, inc=S)
                i8_rd["S"][sl] = p.counts["S"]
            return p.now(S)

        ps_evac = [0, 0, 0, 0]
        q_done = [None, None]
        q_done[0] = quantize_group(0)
        for g in range(NG):
            par = g % 2
            if g + 1 < NG:
                q_done[(g + 1) % 2] = quantize_group(g + 1)
            qd = q_done[par]
            for hc in range(2):
                for fc in range(NFC):
                    for nh in range(2):
                        p.on("tensor", lambda e, f=fc, n=nh, h=hc, par=par:
                             e.matmul(PAp[:, n * 512:(n + 1) * 512],
                                      QUP[par][:, f * 256 + h * 128:
                                               f * 256 + (h + 1) * 128],
                                      BC[:, f * TLOC + n * 512:
                                         f * TLOC + (n + 1) * 512],
                                      start=(f == 0), stop=(f == NFC - 1)),
                             waits=(qd + [("V", p.counts["V"]),
                                          ("S", pa_rd["S"])])
                             if (fc == 0 and nh == 0) else (),
                             inc=T if (fc == NFC - 1 and nh == 1) else None)
                for fc in range(NFC):
                    for nh in range(2):
                        p.on("tensor", lambda e, f=fc, n=nh, h=hc, par=par:
                             e.matmul(PBp[:, n * 512:(n + 1) * 512],
                                      QGT[par][:, f * 256 + h * 128:
                                               f * 256 + (h + 1) * 128],
                                      BC[:, f * TLOC + n * 512:
                                         f * TLOC + (n + 1) * 512],
                                      start=(f == 0), stop=(f == NFC - 1)),
                             inc=T if (fc == NFC - 1 and nh == 1) else None)
                umm = p.now(T)
                q_rd["T"][par] = umm[0][1]
                p.on("scalar", lambda e:
                     e.activation(kq_s[:, :], PAp[:, :], AF.Silu,
                                  scale=sig[:, 0:1]),
                     waits=umm + [("V", kq_rd["V"])], inc=S)
                pa_rd["S"] = p.counts["S"]
                sld = p.now(S)
                p.on("vector", lambda e, h=hc:
                     e.scalar_tensor_tensor(hb[:, h * 1024:(h + 1) * 1024],
                                            PBp[:, :], sig[:, 1:2],
                                            kq_s[:, :], ALU.mult, ALU.mult),
                     waits=sld + [("T", hb_rd["T"])], inc=V)
                kq_rd["V"] = p.counts["V"]
            h_done = p.now(V)
            for og in range(4):
                for kc in range(2):
                    for j in range(4):
                        oc = og * 4 + j
                        for nh in range(2):
                            w_ = []
                            if kc == 0 and nh == 0:
                                w_ = [("V", ps_evac[j])]
                                if j == 0:
                                    w_ += h_done + dbg_guard
                            p.on("tensor", lambda e, k=kc, j=j, o=oc, n=nh,
                                 par=par:
                                 e.matmul(PS4[j][:, n * 512:(n + 1) * 512],
                                          QDN[par][:, k * 2048 + o * 128:
                                                   k * 2048 + (o + 1) * 128],
                                          hb[:, k * 1024 + n * 512:
                                             k * 1024 + (n + 1) * 512],
                                          start=(k == 0), stop=(k == 1)),
                                 waits=w_,
                                 inc=T if (kc == 1 and j == 3 and nh == 1)
                                 else None)
                dmm = p.now(T)
                q_rd["T"][par] = dmm[0][1]
                hb_rd["T"] = dmm[0][1]
                for j in range(4):
                    oc = og * 4 + j
                    ps_evac[j] = p.on(
                        "vector", lambda e, j=j, o=oc:
                        e.scalar_tensor_tensor(BB[:, o * TLOC:(o + 1) * TLOC],
                                               PS4[j][:, :], sig[:, 2:3],
                                               BB[:, o * TLOC:(o + 1) * TLOC],
                                               ALU.mult, ALU.add),
                        waits=dmm, inc=V)
        ffn_done = p.now(V)

        if "r4" in debug:
            p.dma("sync", d["dbg_r4"], BB[:, :], DO, waits=ffn_done)
            ffn_done = list(ffn_done) + p.now(DO)
        if "qup" in debug:
            p.dma("sync", d["dbg_qup"], BX[:, :], DO, waits=ffn_done)
            ffn_done = list(ffn_done) + p.now(DO)

        layernorm(BB, BB, BC, ffn_done)
        y_done = p.now(V)
        p.dma("sync", d["yT"], BB[:, :], DO, waits=y_done)
        p.on("sync", lambda e: e.nop(), waits=p.now(DO, V, T, S, G))
        p.emit()

    return nc, d



def _rope_tables(T, dim=128, base=10000.0):
    inv = 1.0 / base ** (np.arange(0, dim, 2, dtype=np.float64) / dim)
    t = np.arange(T, dtype=np.float64)
    fr = t[:, None] * inv[None, :]
    emb = np.concatenate([fr, fr], axis=1)
    return (np.ascontiguousarray(np.cos(emb).T.astype(np.float32)),
            np.ascontiguousarray(np.sin(emb).T.astype(np.float32)))


def _rot_matrix():
    P = np.zeros((128, 128), np.float32)
    for dd in range(64):
        P[dd, dd + 64] = -1.0
        P[dd + 64, dd] = 1.0
    return np.ascontiguousarray(P.T)


def make_inputs_a(inputs):
    x = np.asarray(inputs["L0_attn_x"], np.float32)
    wat = np.ascontiguousarray(np.asarray(inputs["L0_attn_w_attn"], np.float32))
    wproj = np.asarray(inputs["L0_attn_w_proj"], np.float32)
    watqH = np.ascontiguousarray(
        wat[:, 0:2048].reshape(16, 128, 16, 128).transpose(1, 2, 0, 3)
        .reshape(128, -1))
    wprojH = np.ascontiguousarray(
        wproj.reshape(16, 128, 16, 128).transpose(1, 2, 0, 3).reshape(128, -1))
    cosT, sinT = _rope_tables(TFULL)
    PT = _rot_matrix()
    ident = np.eye(128, dtype=np.float32)
    gateT = np.ascontiguousarray(np.asarray(inputs["L0_ffn_gate"], np.float32).T)
    bias8 = np.ascontiguousarray(
        np.asarray(inputs["L0_ffn_expert_bias"], np.float32).reshape(1, 8))
    w1t = np.asarray(inputs["L0_ffn_w1"], np.float32).transpose(0, 2, 1)
    w3t = np.asarray(inputs["L0_ffn_w3"], np.float32).transpose(0, 2, 1)
    w2t = np.asarray(inputs["L0_ffn_w2"], np.float32).transpose(0, 2, 1)
    sw1t = np.asarray(inputs["L0_ffn_shared_w1"], np.float32).T
    sw3t = np.asarray(inputs["L0_ffn_shared_w3"], np.float32).T
    sw2t = np.asarray(inputs["L0_ffn_shared_w2"], np.float32).T
    w1all = np.concatenate([w1t.reshape(-1, 1024), sw1t], axis=0)
    w3all = np.concatenate([w3t.reshape(-1, 1024), sw3t], axis=0)
    w2all = np.concatenate([w2t.reshape(-1, 2048), sw2t], axis=0)
    w1H = np.ascontiguousarray(
        w1all.reshape(9, 16, 128, 8, 128).transpose(2, 0, 3, 1, 4)
        .reshape(128, -1))
    w3H = np.ascontiguousarray(
        w3all.reshape(9, 16, 128, 8, 128).transpose(2, 0, 3, 1, 4)
        .reshape(128, -1))
    w2H = np.ascontiguousarray(
        w2all.reshape(9, 8, 128, 16, 128).transpose(2, 0, 3, 1, 4)
        .reshape(128, -1))
    OH8 = np.zeros((8, 1024), np.float32)
    for e in range(8):
        OH8[e, e * 128:(e + 1) * 128] = 1.0
    wintH = np.ascontiguousarray(
        np.asarray(inputs["L1_attn_W_in"], np.float32)
        .reshape(32, 128, 16, 128).transpose(3, 0, 2, 1).reshape(128, -1))
    apar = np.ascontiguousarray(
        np.asarray(inputs["L1_attn_a_param"], np.float32).reshape(128, 1))
    wup = np.asarray(inputs["L1_ffn_W_up"], np.float32).reshape(-1)
    wgt = np.asarray(inputs["L1_ffn_W_gate"], np.float32).reshape(-1)
    wdn = np.asarray(inputs["L1_ffn_W_down"], np.float32).reshape(-1)

    maps = []
    for r in range(8):
        b, c = divmod(r, 4)
        t0 = c * TLOC
        maps.append(dict(
            xloc=np.ascontiguousarray(x[b, t0:t0 + TLOC]),
            xb=np.ascontiguousarray(x[b]),
            cosT=cosT, sinT=sinT,
            cosl=np.ascontiguousarray(cosT[:, t0:t0 + TLOC]),
            sinl=np.ascontiguousarray(sinT[:, t0:t0 + TLOC]),
            wat=wat, watqH=watqH, wprojH=wprojH, ident=ident, PT=PT,
            gateT=gateT, bias8=bias8,
            w1H=w1H, w3H=w3H, w2H=w2H, OH8=OH8,
            wintH=wintH, apar=apar,
            wupf=np.ascontiguousarray(wup[r * 2097152:(r + 1) * 2097152]
                                      .reshape(128, 16384)),
            wgtf=np.ascontiguousarray(wgt[r * 2097152:(r + 1) * 2097152]
                                      .reshape(128, 16384)),
            wdnf=np.ascontiguousarray(wdn[r * 2097152:(r + 1) * 2097152]
                                      .reshape(128, 16384)),
        ))
    return maps


def make_inputs_b(inputs, results_a):
    wintH = np.ascontiguousarray(
        np.asarray(inputs["L1_attn_W_in"], np.float32)
        .reshape(32, 128, 16, 128).transpose(3, 0, 2, 1).reshape(128, -1))
    apar = np.ascontiguousarray(
        np.asarray(inputs["L1_attn_a_param"], np.float32).reshape(128, 1))
    wupH = np.ascontiguousarray(
        np.asarray(inputs["L1_ffn_W_up"], np.float32)
        .reshape(32, 256, 16, 128).transpose(3, 0, 2, 1).reshape(128, -1))
    wgtH = np.ascontiguousarray(
        np.asarray(inputs["L1_ffn_W_gate"], np.float32)
        .reshape(32, 256, 16, 128).transpose(3, 0, 2, 1).reshape(128, -1))
    wdnT = np.ascontiguousarray(np.asarray(inputs["L1_ffn_W_down"], np.float32).T)
    absps = np.ascontiguousarray(
        np.stack([results_a[r]["absp"][0] for r in range(8)]).astype(np.float32))
    maps = []
    for r in range(8):
        b, c = divmod(r, 4)
        carA = np.ones((128, 3 * NFC), np.float32)
        carB = np.zeros((128, 3 * NFC), np.float32)
        for s in range(3):
            src_chunk = c - 3 + s
            if src_chunk >= 0:
                rr = b * 4 + src_chunk
                carA[:, s * NFC:(s + 1) * NFC] = results_a[rr]["carA"]
                carB[:, s * NFC:(s + 1) * NFC] = results_a[rr]["carB"]
        maps.append(dict(
            xn2T=np.ascontiguousarray(results_a[r]["xn2T"]),
            carApre=carA, carBpre=carB, absps=absps,
            wintH=wintH, apar=apar, wupH=wupH, wgtH=wgtH, wdnT=wdnT,
        ))
    return maps


def fm_to_tok(arr):
    return arr.reshape(128, NFC, TLOC).transpose(2, 1, 0).reshape(TLOC, DMODEL)


def kernel(**inputs):
    maps_a = make_inputs_a(inputs)
    nc_a, _ = build_launch_a()
    res_a = run_bass_kernel_spmd(nc_a, maps_a, core_ids=list(range(8)))
    maps_b = make_inputs_b(inputs, res_a.results)
    nc_b, _ = build_launch_b()
    res_b = run_bass_kernel_spmd(nc_b, maps_b, core_ids=list(range(8)))
    out = np.zeros((2, TFULL, DMODEL), np.float32)
    for r in range(8):
        b, c = divmod(r, 4)
        out[b, c * TLOC:(c + 1) * TLOC] = fm_to_tok(res_b.results[r]["yT"])
    return out


# version 08
# speedup vs baseline: 1.1528x; 1.1528x over previous
import contextlib
import math
import numpy as np
import ml_dtypes
import concourse.bass as bass
import concourse.mybir as mybir
from concourse.bass_utils import run_bass_kernel_spmd

F32 = mybir.dt.float32
BF16 = mybir.dt.bfloat16
I8 = mybir.dt.int8
AF = mybir.ActivationFunctionType
ALU = mybir.AluOpType
AX = mybir.AxisListType

ENGINES = ("sync", "gpsimd", "scalar", "vector", "tensor")
NHEAD = 16
DMODEL = 2048
TLOC = 1024
TFULL = 4096
NFC = 16
SCALE = 1.0 / math.sqrt(128.0)


class Prog:

    def __init__(self, nc, es):
        self.nc = nc
        self.es = es
        self.ops = {e: [] for e in ENGINES}
        self.sems = {}
        self.counts = {}
        self.owner = {}

    def sem(self, name, owner=""):
        self.sems[name] = self.es.enter_context(self.nc.semaphore(name))
        self.counts[name] = 0
        self.owner[name] = owner
        return name

    def now(self, *names):
        return [(n, self.counts[n]) for n in names]

    def on(self, eng, fn, waits=(), inc=None, inc_by=1):
        self.ops[eng].append((fn, list(waits), inc, inc_by))
        if inc is not None:
            self.counts[inc] += inc_by
        return self.counts.get(inc)

    def dma(self, eng, out, in_, sem, waits=()):
        return self.on(eng, lambda e, o=out, i=in_: e.dma_start(out=o, in_=i),
                       waits, inc=sem, inc_by=16)

    def emit(self):
        self.es.enter_context(self._emit_ctx())

    @contextlib.contextmanager
    def _emit_ctx(self):
        with self.nc.Block() as block:
            for ename in ENGINES:
                ops = self.ops[ename]
                if not ops:
                    continue

                own = None
                for s, o in self.owner.items():
                    if o == ename:
                        own = s
                auto_sync = ename in ("vector", "scalar", "gpsimd")

                def body(eng, ops=ops, ename=ename, own=own, auto_sync=auto_sync):
                    waited = {}
                    own_count = 0
                    for fn, waits, inc, inc_by in ops:
                        if auto_sync and own_count > waited.get(own, 0):
                            eng.wait_ge(self.sems[own], own_count)
                            waited[own] = own_count
                        for s, v in waits:
                            if v <= 0:
                                continue
                            if self.owner[s] == ename:
                                continue
                            if waited.get(s, -1) >= v:
                                continue
                            eng.wait_ge(self.sems[s], v)
                            waited[s] = v
                        instr = fn(eng)
                        if inc is not None:
                            instr.then_inc(self.sems[inc], inc_by)
                            if inc == own:
                                own_count += inc_by

                getattr(block, ename)(body)
            yield


def bc16(ap_f32):
    return ap_f32.bitcast(BF16)



def build_launch_a(debug=(), upto="all"):
    nc = bass.Bass("TRN2", target_bir_lowering=False)
    d = {}

    def inp(name, shape, dt=F32):
        d[name] = nc.dram_tensor(name, shape, dt, kind="ExternalInput").ap()
        return d[name]

    def outp(name, shape, dt=F32):
        d[name] = nc.dram_tensor(name, shape, dt, kind="ExternalOutput").ap()
        return d[name]

    inp("xloc", [TLOC, DMODEL])
    inp("xb", [TFULL, DMODEL])
    inp("cosT", [128, TFULL])
    inp("sinT", [128, TFULL])
    inp("cosl", [128, TLOC])
    inp("sinl", [128, TLOC])
    inp("wat", [DMODEL, 2304])
    inp("watqH", [128, 16 * 2048])
    inp("wprojH", [128, 16 * 2048])
    inp("ident", [128, 128])
    inp("PT", [128, 128])
    inp("gateT", [DMODEL, 8])
    inp("bias8", [1, 8])
    inp("w1H", [128, 9 * 8 * 2048])
    inp("w3H", [128, 9 * 8 * 2048])
    inp("w2H", [128, 9 * 16 * 1024])
    inp("OH8", [8, 1024])
    inp("wintH", [128, 32 * 2048])
    inp("apar", [128, 1])
    inp("wupf", [128, 16384])
    inp("wgtf", [128, 16384])
    inp("wdnf", [128, 16384])

    outp("xn2T", [128, NFC * TLOC])
    outp("carA", [128, NFC])
    outp("carB", [128, NFC])
    outp("absp", [1, 4])
    for t in debug:
        outp("dbg_" + t, [128, NFC * TLOC])

    es = contextlib.ExitStack()
    with es:
        p = Prog(nc, es)

        def sb(name, shape, dt):
            return es.enter_context(nc.sbuf_tensor(name, shape, dt))

        BX = sb("BX", [128, NFC * TLOC], F32)
        BB = sb("BB", [128, NFC * TLOC], F32)
        BC = sb("BC", [128, NFC * TLOC], BF16)

        xtm = [BB[:, 0:2048], BB[:, 2048:4096]]
        xq = [bc16(BB[:, 4096:5120]), bc16(BB[:, 5120:6144])]
        coslb = bc16(BB[:, 4096:4608])
        sinlb = bc16(BB[:, 4608:5120])
        Kbf = bc16(BB[:, 6144:8192])

        def ybf(h):
            off = 8192 + h * 512
            return bc16(BB[:, off:off + 512])

        hbuf = bc16(BX[:, 0:4096])
        w13 = [bc16(BX[:, 4096 + i * 1024: 4096 + (i + 1) * 1024])
               for i in range(2)]
        w13b = [bc16(BX[:, 6144 + i * 1024: 6144 + (i + 1) * 1024])
                for i in range(2)]
        w2s = [bc16(BX[:, 8192 + i * 512: 8192 + (i + 1) * 512])
               for i in range(2)]
        smb = BX[:, 9216:10240]
        sm_all = BX[:, 11264:11328]
        u_tm = BX[:, 11328:11336]
        m_tm = BX[:, 11336:11344]
        v12 = BX[:, 11344:11346]
        biasb = BX[:, 11352:11360]
        gx4 = BX[:, 0:4096]
        la4 = BX[:, 4096:8192]
        a4 = BX[:, 8192:12288]
        b4 = BX[:, 12288:16384]

        cosq = sb("cosq", [128, TLOC], BF16)
        sinq = sb("sinq", [128, TLOC], BF16)
        Vbf = sb("Vbf", [128, TFULL], BF16)
        mrsb_v = Vbf[:, 0:2048].bitcast(F32)
        wuni = sb("wuni", [128, 4096], BF16)
        wkv = wuni[:, 0:4096]
        wq = [wuni[:, 0:2048], wuni[:, 2048:4096]]
        identt = sb("identt", [128, 128], F32)
        PTb = sb("PTb", [128, 128], BF16)
        qbf = sb("qbf", [128, TLOC], BF16)
        kq_s = sb("kq_s", [128, TLOC], F32)
        carA_sb = kq_s[:, 0:16]
        carB_sb = kq_s[:, 16:32]
        absacc = BX[:, 15488:15492]
        absp_sb = BX[0:1, 15492:15496]
        absl2 = [BX[:, 11360 + i * 2048: 11360 + (i + 1) * 2048]
                 for i in range(2)]
        SC2 = sb("SC2", [128, 512], F32)
        kq_s2 = SC2[:, :]
        sqs = bc16(SC2[:, :])
        sqs_rd = {"T": 0}
        pbf = [bc16(BB[:, i * 512:(i + 1) * 512]) for i in range(4)]
        ones_col_bf = sb("ones_col_bf", [128, 1], BF16)
        ones_row_f = sb("ones_row_f", [1, 128], F32)
        km = sb("km", [1, 1], F32)
        qm = sb("qm", [1, 1], F32)
        negM = sb("negM", [1, 1], F32)
        negMb = sb("negMb", [128, 1], F32)
        rowA = sb("rowA", [1, TLOC], F32)
        rowB = sb("rowB", [1, TLOC], F32)
        rowC = sb("rowC", [1, TLOC], F32)
        rb = sb("rb", [128, TLOC], F32)
        mrsb = None

        gatesb = sb("gatesb", [128, 128], BF16)
        bias8_sb = sb("bias8_sb", [1, 8], F32)
        apar_sb = sb("apar_sb", [128, 1], F32)
        ones_col_f = sb("ones_col_f", [128, 1], F32)
        afneg8 = sb("afneg8", [128, 1], F32)
        OH8_sb = Vbf[:, 2048:4096].bitcast(F32)

        def psum(name):
            return es.enter_context(nc.psum_tensor(name, [128, TLOC], F32))

        PAp, PBp, PCp, PDp = psum("PA"), psum("PB"), psum("PC"), psum("PD")

        T = p.sem("T", "tensor")
        V = p.sem("V", "vector")
        S = p.sem("S", "scalar")
        G = p.sem("G", "gpsimd")
        CONST = p.sem("CONST")
        ABS = p.sem("ABS")
        DS0 = p.sem("DS0")
        DS1 = p.sem("DS1")
        CQ = p.sem("CQ")
        WQ0 = p.sem("WQ0")
        WQ1 = p.sem("WQ1")
        DO = p.sem("DO")

        p.dma("gpsimd", PTb[:, :], d["PT"], CONST)
        p.dma("gpsimd", identt[:, :], d["ident"], CONST)
        p.on("gpsimd", lambda e: e.memset(ones_col_bf[:, :], 1.0), inc=G)
        p.on("gpsimd", lambda e: e.memset(ones_row_f[:, :], 1.0), inc=G)
        p.on("gpsimd", lambda e: e.memset(km[:, :], 0.0), inc=G)
        wkv_src = d["wat"][:, 2048:2304].rearrange("(fc p) o -> p fc o", p=128)
        p.dma("gpsimd", wkv.rearrange("p (fc o) -> p fc o", fc=NFC), wkv_src, CONST)
        p.dma("gpsimd", gatesb[:, :].rearrange("p (fc o) -> p fc o", fc=NFC),
              d["gateT"].rearrange("(fc p) o -> p fc o", p=128), CONST)
        p.dma("gpsimd", bias8_sb[:, :], d["bias8"], CONST)
        p.dma("gpsimd", apar_sb[:, :], d["apar"], CONST)
        p.on("gpsimd", lambda e: e.memset(ones_col_f[:, :], 1.0), inc=G)
        consts = p.now(CONST, G)

        tr_evacs = []

        def transpose_tile(stage, dests, ld):
            for g in range(4):
                half = g % 2
                wv = ([("V", tr_evacs[-2])] if len(tr_evacs) >= 2 else p.now(V))
                for j in range(4):
                    fc = g * 4 + j
                    p.on("tensor", lambda e, s=stage, f=fc, hf=half, c=j * 128:
                         e.transpose(PAp[:, hf * 512 + c: hf * 512 + c + 128],
                                     s[:, f * 128:(f + 1) * 128], identt[:, :]),
                         waits=(ld + wv) if j == 0 else (),
                         inc=T if j == 3 else None)
                tw = p.now(T)
                src3 = PAp[:, half * 512:(half + 1) * 512].rearrange(
                    "p (f c) -> p f c", f=4)
                c = None
                for dst in dests:
                    c = p.on("vector", lambda e, o=dst(g), s=src3:
                             e.tensor_copy(o, s), waits=tw, inc=V)
                tr_evacs.append(c)

        def view3(buf, fc0, t0, span=TLOC):
            return buf.rearrange("p (f t) -> p f t", f=buf.shape[1] // span)[
                :, fc0:fc0 + 4, t0:t0 + 128]

        for tt in range(8):
            stage = xtm[tt % 2]
            dsem = DS0 if tt % 2 == 0 else DS1
            p.dma("sync", stage,
                  d["xloc"][tt * 128:(tt + 1) * 128, :], dsem,
                  waits=p.now(T) + consts)
            ld = p.now(dsem)
            transpose_tile(
                stage,
                [lambda g, tt=tt: view3(BX[:, :], g * 4, tt * 128),
                 lambda g, tt=tt: view3(BC[:, :], g * 4, tt * 128)],
                ld)

        for q in range(4):
            p.dma("gpsimd", cosq[:, :], d["cosT"][:, q * 1024:(q + 1) * 1024], CQ,
                  waits=p.now(V))
            p.dma("gpsimd", sinq[:, :], d["sinT"][:, q * 1024:(q + 1) * 1024], CQ)
            cs_ready = p.now(CQ)
            v_evacs = []
            for tt in range(8):
                stage = xtm[tt % 2]
                dsem = DS0 if tt % 2 == 0 else DS1
                p.dma("sync", stage,
                      d["xb"][q * 1024 + tt * 128: q * 1024 + (tt + 1) * 128, :],
                      dsem, waits=p.now(T))
                ld = p.now(dsem)
                xqb = xq[tt % 2]
                transpose_tile(
                    stage, [lambda g, x=xqb: view3(x, g * 4, 0, span=128)], ld)
                xq_ready = p.now(V)
                for fc in range(NFC):
                    p.on("tensor", lambda e, f=fc, t=tt, x=xqb:
                         e.matmul(PCp[:, t * 128:(t + 1) * 128],
                                  wkv[:, f * 256: f * 256 + 128],
                                  x[:, f * 128:(f + 1) * 128],
                                  start=(f == 0), stop=(f == NFC - 1)),
                         waits=(xq_ready + consts) if fc == 0 else (),
                         inc=T if fc == NFC - 1 else None)
                half = tt % 2
                wv = ([("V", v_evacs[tt - 2])] if tt >= 2 else p.now(V))
                for fc in range(NFC):
                    p.on("tensor", lambda e, t=tt, f=fc, hf=half, x=xqb:
                         e.matmul(PBp[:, hf * 512: hf * 512 + 128],
                                  x[:, f * 128:(f + 1) * 128],
                                  wkv[:, f * 256 + 128: f * 256 + 256],
                                  start=(f == 0), stop=(f == NFC - 1)),
                         waits=wv if fc == 0 else (),
                         inc=T if fc == NFC - 1 else None)
                vmm = p.now(T)
                c = p.on("vector", lambda e, t=tt, q=q, hf=half:
                         e.tensor_copy(Vbf[:, (q * 8 + t) * 128:(q * 8 + t + 1) * 128],
                                       PBp[:, hf * 512: hf * 512 + 128]),
                         waits=vmm, inc=V)
                v_evacs.append(c)
            kmm = p.now(T)
            p.on("vector", lambda e: e.tensor_copy(qbf[:, :], PCp[:, :]),
                 waits=kmm + cs_ready, inc=V)
            p.on("vector", lambda e:
                 e.tensor_tensor(kq_s[:, :], PCp[:, :], cosq[:, :], ALU.mult),
                 inc=V)
            kraw = p.now(V)
            for nh in range(2):
                p.on("tensor", lambda e, n=nh:
                     e.matmul(PBp[:, n * 512:(n + 1) * 512], PTb[:, :],
                              qbf[:, n * 512:(n + 1) * 512], start=True, stop=True),
                     waits=kraw if nh == 0 else (), inc=T)
                rmm = p.now(T)
                p.on("vector", lambda e, n=nh:
                     e.tensor_tensor(kq_s2[:, :], PBp[:, n * 512:(n + 1) * 512],
                                     sinq[:, n * 512:(n + 1) * 512], ALU.mult),
                     waits=rmm, inc=V)
                p.on("vector", lambda e, n=nh, q=q:
                     e.tensor_add(Kbf[:, q * 1024 + n * 512: q * 1024 + (n + 1) * 512],
                                  kq_s[:, n * 512:(n + 1) * 512], kq_s2[:, :]),
                     inc=V)
            kq_done = p.now(V)
            p.on("scalar", lambda e, q=q:
                 e.activation(sqs, Kbf[:, q * 1024:(q + 1) * 1024], AF.Square),
                 waits=kq_done + [("T", sqs_rd["T"])], inc=S)
            sq_done = p.now(S)
            for nh in range(2):
                p.on("tensor", lambda e, n=nh:
                     e.matmul(PDp[0:1, n * 512:(n + 1) * 512], ones_col_bf[:, :],
                              sqs[:, n * 512:(n + 1) * 512], start=True, stop=True),
                     waits=(sq_done + p.now(V)) if nh == 0 else (),
                     inc=T if nh == 1 else None)
            ksum = p.now(T)
            sqs_rd["T"] = ksum[0][1]
            p.on("vector", lambda e: e.reduce_max(rowA[:, 0:1], PDp[0:1, :],
                                                  axis=AX.X),
                 waits=ksum, inc=V)
            p.on("vector", lambda e: e.tensor_max(km[:, :], km[:, :], rowA[:, 0:1]),
                 inc=V)

        if upto == "kv":
            p.on("sync", lambda e: e.nop(), waits=p.now(V, T, S))
            p.emit()
            return nc, d

        p.dma("gpsimd", coslb, d["cosl"], CQ, waits=p.now(T))
        p.dma("gpsimd", sinlb, d["sinl"], CQ)
        csl_ready = p.now(CQ)

        def wq_src(h):
            return d["watqH"][:, h * 2048:(h + 1) * 2048]

        p.dma("gpsimd", wq[0], wq_src(0), WQ0, waits=p.now(T))

        for h in range(NHEAD):
            wqb = wq[h % 2]
            wq_ready = p.now(WQ0 if h % 2 == 0 else WQ1)
            if h + 1 < NHEAD:
                p.dma("gpsimd", wq[(h + 1) % 2], wq_src(h + 1),
                      WQ0 if (h + 1) % 2 == 0 else WQ1,
                      waits=p.now(T))
            bxc_ready = p.now(V)
            for fc in range(NFC):
                for nh in range(2):
                    p.on("tensor", lambda e, f=fc, n=nh, w=wqb:
                         e.matmul(PCp[:, n * 512:(n + 1) * 512],
                                  w[:, f * 128:(f + 1) * 128],
                                  BC[:, f * TLOC + n * 512: f * TLOC + (n + 1) * 512],
                                  start=(f == 0), stop=(f == NFC - 1)),
                         waits=(wq_ready + bxc_ready + csl_ready)
                         if (fc == 0 and nh == 0) else (),
                         inc=T if (fc == NFC - 1 and nh == 1) else None)
            qmm = p.now(T)
            p.on("vector", lambda e: e.tensor_copy(qbf[:, :], PCp[:, :]),
                 waits=qmm, inc=V)
            p.on("vector", lambda e:
                 e.tensor_tensor(kq_s[:, :], PCp[:, :], coslb, ALU.mult),
                 inc=V)
            qraw = p.now(V)
            for nh in range(2):
                p.on("tensor", lambda e, n=nh:
                     e.matmul(PBp[:, n * 512:(n + 1) * 512], PTb[:, :],
                              qbf[:, n * 512:(n + 1) * 512], start=True, stop=True),
                     waits=qraw if nh == 0 else (), inc=T)
                rmm = p.now(T)
                p.on("vector", lambda e, n=nh:
                     e.tensor_tensor(kq_s2[:, :], PBp[:, n * 512:(n + 1) * 512],
                                     sinlb[:, n * 512:(n + 1) * 512], ALU.mult),
                     waits=rmm, inc=V)
                p.on("vector", lambda e, n=nh:
                     e.tensor_add(qbf[:, n * 512:(n + 1) * 512],
                                  kq_s[:, n * 512:(n + 1) * 512], kq_s2[:, :]),
                     inc=V)
            qrope = p.now(V)
            p.on("scalar", lambda e: e.activation(sqs, qbf[:, :], AF.Square),
                 waits=qrope + [("T", sqs_rd["T"])], inc=S)
            sq_done = p.now(S)
            for nh in range(2):
                p.on("tensor", lambda e, n=nh:
                     e.matmul(PDp[0:1, n * 512:(n + 1) * 512], ones_col_bf[:, :],
                              sqs[:, n * 512:(n + 1) * 512], start=True, stop=True),
                     waits=(sq_done + p.now(V)) if nh == 0 else (),
                     inc=T if nh == 1 else None)
            qsum = p.now(T)
            sqs_rd["T"] = qsum[0][1]
            p.on("vector", lambda e: e.reduce_max(qm[:, :], PDp[0:1, :], axis=AX.X),
                 waits=qsum, inc=V)
            p.on("vector", lambda e:
                 e.tensor_tensor(negM[:, :], qm[:, :], km[:, :], ALU.add), inc=V)
            p.on("vector", lambda e:
                 e.tensor_scalar(negM[:, :], negM[:, :], -0.5 * SCALE, None,
                                 ALU.mult), inc=V)
            nm = p.now(V)
            p.on("tensor", lambda e:
                 e.matmul(PDp[:, 0:1], ones_row_f[:, :], negM[:, :],
                          start=True, stop=True), waits=nm, inc=T)
            nmm = p.now(T)
            p.on("vector", lambda e: e.tensor_copy(negMb[:, :], PDp[:, 0:1]),
                 waits=nmm, inc=V)
            negm_ready = p.now(V)

            exp_counts = []
            smm_counts = []

            def plan_s(mc):
                PS_ = PAp if mc % 2 == 0 else PCp
                ws = ([("S", exp_counts[mc - 2])] if mc >= 2 else list(p.now(S)))
                if mc == 0:
                    ws += negm_ready + p.now(V)
                for nh in range(2):
                    p.on("tensor", lambda e, m=mc, n=nh, PS=PS_:
                         e.matmul(PS[:, n * 512:(n + 1) * 512],
                                  Kbf[:, m * 128:(m + 1) * 128],
                                  qbf[:, n * 512:(n + 1) * 512],
                                  start=True, stop=True),
                         waits=ws if nh == 0 else (),
                         inc=T if nh == 1 else None)
                smm_counts.append(p.counts["T"])

            burst_Ts = [0]
            plan_s(0)
            for mc in range(32):
                buf = pbf[mc % 4]
                PS_ = PAp if mc % 2 == 0 else PCp
                ec = p.on("scalar", lambda e, b=buf, PS=PS_:
                          e.activation(b, PS[:, :],
                                       AF.Exp, bias=negMb[:, 0:1], scale=SCALE),
                          waits=[("T", max(smm_counts[mc],
                                           burst_Ts[min(mc // 4, len(burst_Ts) - 1)]))],
                          inc=S)
                exp_counts.append(ec)
                edone = p.now(S)
                if mc + 1 < 32:
                    plan_s(mc + 1)
                for nh in range(2):
                    p.on("tensor", lambda e, m=mc, n=nh, b=buf:
                         e.matmul(PBp[:, n * 512:(n + 1) * 512],
                                  Vbf[:, m * 128:(m + 1) * 128],
                                  b[:, n * 512:(n + 1) * 512],
                                  start=(m == 0), stop=(m == 31)),
                         waits=(edone if nh == 0 else []), inc=None)
                if mc % 4 == 3:
                    for q4 in range(4):
                        for nh in range(2):
                            p.on("tensor", lambda e, m4=mc - 3 + q4, n=nh,
                                 b=pbf[(mc - 3 + q4) % 4]:
                                 e.matmul(PDp[0:1, n * 512:(n + 1) * 512],
                                          ones_col_bf[:, :],
                                          b[:, n * 512:(n + 1) * 512],
                                          start=(m4 == 0), stop=(m4 == 31)),
                                 inc=T if (q4 == 3 and nh == 1) else None)
                    burst_Ts.append(p.counts["T"])
            av = p.now(T)
            p.on("vector", lambda e: e.tensor_copy(rowA[:, :], PDp[0:1, :]),
                 waits=av, inc=V)
            sr = p.now(V)
            p.on("scalar", lambda e:
                 e.activation(rowB[:, :], rowA[:, :], AF.Ln), waits=sr, inc=S)
            p.on("scalar", lambda e:
                 e.activation(rowB[:, :], rowB[:, :], AF.Exp, scale=-1.0), inc=S)
            rr = p.now(S)
            for nh in range(2):
                p.on("tensor", lambda e, n=nh:
                     e.matmul(PDp[:, n * 512:(n + 1) * 512], ones_row_f[:, :],
                              rowB[0:1, n * 512:(n + 1) * 512],
                              start=True, stop=True),
                     waits=rr if nh == 0 else (), inc=T if nh == 1 else None)
            rbmm = p.now(T)
            p.on("scalar", lambda e: e.copy(rb[:, :], PDp[:, :]), waits=rbmm, inc=S)
            rbdone = p.now(S)
            p.on("vector", lambda e, h=h:
                 e.tensor_tensor(ybf(h), PBp[:, :], rb[:, :], ALU.mult),
                 waits=rbdone, inc=V)

        def wpr_src(oc):
            return d["wprojH"][:, oc * 2048:(oc + 1) * 2048]

        p.dma("gpsimd", wq[0], wpr_src(0), WQ0, waits=p.now(T))
        for oc in range(NFC):
            wp = wq[oc % 2]
            wp_ready = p.now(WQ0 if oc % 2 == 0 else WQ1)
            if oc + 1 < NFC:
                p.dma("gpsimd", wq[(oc + 1) % 2], wpr_src(oc + 1),
                      WQ0 if (oc + 1) % 2 == 0 else WQ1,
                      waits=p.now(T))
            yready = p.now(V)
            for fc in range(NFC):
                for nh in range(2):
                    p.on("tensor", lambda e, f=fc, n=nh, w=wp:
                         e.matmul(PCp[:, n * 512:(n + 1) * 512],
                                  w[:, f * 128:(f + 1) * 128],
                                  ybf(f)[:, n * 512:(n + 1) * 512],
                                  start=(f == 0), stop=(f == NFC - 1)),
                         waits=(wp_ready + yready) if (fc == 0 and nh == 0) else (),
                         inc=T if (fc == NFC - 1 and nh == 1) else None)
            pmm = p.now(T)
            p.on("vector", lambda e, o=oc:
                 e.tensor_add(BX[:, o * TLOC:(o + 1) * TLOC],
                              PCp[:, :], BX[:, o * TLOC:(o + 1) * TLOC]),
                 waits=pmm, inc=V)
        r1_done = p.now(V)

        def layernorm(src_f32, dst_f32, dst_bf, waits):
            for fc in range(NFC):
                p.on("vector", lambda e, f=fc:
                     e.tensor_copy(dst_bf[:, f * TLOC:(f + 1) * TLOC],
                                   src_f32[:, f * TLOC:(f + 1) * TLOC]),
                     waits=waits if fc == 0 else (), inc=V)
            cast_done = p.now(V)
            for fc in range(NFC):
                for nh in range(2):
                    p.on("tensor", lambda e, f=fc, n=nh:
                         e.matmul(PDp[0:1, n * 512:(n + 1) * 512],
                                  ones_col_bf[:, :],
                                  dst_bf[:, f * TLOC + n * 512:
                                         f * TLOC + (n + 1) * 512],
                                  start=(f == 0), stop=(f == NFC - 1)),
                         waits=cast_done if (fc == 0 and nh == 0) else (),
                         inc=T if (fc == NFC - 1 and nh == 1) else None)
            for fc in range(NFC):
                p.on("scalar", lambda e, f=fc:
                     e.activation(sqs, dst_bf[:, f * TLOC:(f + 1) * TLOC],
                                  AF.Square),
                     waits=list(cast_done if fc == 0 else [])
                     + [("T", sqs_rd["T"])], inc=S)
                sqd = p.now(S)
                for nh in range(2):
                    p.on("tensor", lambda e, f=fc, n=nh:
                         e.matmul(PAp[0:1, n * 512:(n + 1) * 512],
                                  ones_col_bf[:, :],
                                  sqs[:, n * 512:(n + 1) * 512],
                                  start=(f == 0), stop=(f == NFC - 1)),
                         waits=(sqd + p.now(V)) if nh == 0 else (),
                         inc=T if nh == 1 else None)
                sqs_rd["T"] = p.now(T)[0][1]
            stats_mm = p.now(T)
            p.on("vector", lambda e:
                 e.tensor_scalar(rowA[:, :], PDp[0:1, :], 1.0 / DMODEL, None,
                                 ALU.mult), waits=stats_mm, inc=V)
            p.on("vector", lambda e:
                 e.tensor_tensor(rowB[:, :], rowA[:, :], rowA[:, :], ALU.mult),
                 inc=V)
            p.on("vector", lambda e:
                 e.scalar_tensor_tensor(rowC[:, :], PAp[0:1, :], 1.0 / DMODEL,
                                        rowB[:, :], ALU.mult, ALU.subtract),
                 inc=V)
            p.on("vector", lambda e:
                 e.tensor_scalar(rowC[:, :], rowC[:, :], 1e-5, None, ALU.add),
                 inc=V)
            vd = p.now(V)
            p.on("scalar", lambda e:
                 e.activation(rowC[:, :], rowC[:, :], AF.Ln),
                 waits=vd, inc=S)
            p.on("scalar", lambda e:
                 e.activation(rowC[:, :], rowC[:, :], AF.Exp, scale=-0.5), inc=S)
            rstd_done = p.now(S)
            p.on("vector", lambda e:
                 e.tensor_tensor(rowB[:, :], rowA[:, :], rowC[:, :], ALU.mult),
                 waits=rstd_done, inc=V)
            mrs_done = p.now(V)
            for nh in range(2):
                p.on("tensor", lambda e, n=nh:
                     e.matmul(PDp[:, n * 512:(n + 1) * 512], ones_row_f[:, :],
                              rowC[0:1, n * 512:(n + 1) * 512], start=True,
                              stop=True),
                     waits=mrs_done if nh == 0 else (), inc=None)
                p.on("tensor", lambda e, n=nh:
                     e.matmul(PAp[:, n * 512:(n + 1) * 512], ones_row_f[:, :],
                              rowB[0:1, n * 512:(n + 1) * 512], start=True,
                              stop=True),
                     inc=T if nh == 1 else None)
            bc_mm = p.now(T)
            p.on("scalar", lambda e: e.copy(rb[:, :], PDp[:, :]),
                 waits=bc_mm, inc=S)
            p.on("scalar", lambda e: e.copy(mrsb_v[:, :], PAp[:, :]), inc=S)
            bc_done = p.now(S)
            for fc in range(NFC):
                p.on("vector", lambda e, f=fc:
                     e.tensor_tensor(kq_s[:, :], src_f32[:, f * TLOC:(f + 1) * TLOC],
                                     rb[:, :], ALU.mult),
                     waits=bc_done if fc == 0 else (), inc=V)
                p.on("vector", lambda e, f=fc:
                     e.tensor_sub(dst_f32[:, f * TLOC:(f + 1) * TLOC],
                                  kq_s[:, :], mrsb_v[:, :]), inc=V)
                p.on("vector", lambda e, f=fc:
                     e.tensor_copy(dst_bf[:, f * TLOC:(f + 1) * TLOC],
                                   dst_f32[:, f * TLOC:(f + 1) * TLOC]), inc=V)

        layernorm(BX, BX, BC, r1_done)
        xn1_done = p.now(V)

        if "xn1" in debug:
            p.dma("sync", d["dbg_xn1"], BX[:, :], DO, waits=p.now(V))
        if upto == "xn1":
            p.on("sync", lambda e: e.nop(), waits=p.now(DO, V, T, S))
            p.emit()
            return nc, d

        for fc in range(NFC):
            p.on("vector", lambda e, f=fc:
                 e.tensor_copy(BB[:, f * TLOC:(f + 1) * TLOC],
                               BX[:, f * TLOC:(f + 1) * TLOC]),
                 waits=xn1_done if fc == 0 else (), inc=V)
        acc_init = p.now(V)

        p.dma("gpsimd", OH8_sb[0:8, :], d["OH8"], CONST, waits=p.now(T, V))
        oh_ready = p.now(CONST)
        p.on("tensor", lambda e:
             e.matmul(PDp[:, 0:8], ones_row_f[:, :], bias8_sb[0:1, :],
                      start=True, stop=True), waits=p.now(S, V), inc=T)
        bb_mm = p.now(T)
        p.on("vector", lambda e: e.tensor_copy(biasb, PDp[:, 0:8]),
             waits=bb_mm, inc=V)
        biasb_done = p.now(V)

        sig_counts = []
        for ts in range(8):
            col = (ts % 2) * 512
            wv = ([("S", sig_counts[ts - 2])] if ts >= 2
                  else list(p.now(S)) + list(biasb_done))
            for fc in range(NFC):
                p.on("tensor", lambda e, t=ts, f=fc, c=col:
                     e.matmul(PDp[:, c:c + 8], BC[:, f * TLOC + t * 128:
                                                  f * TLOC + (t + 1) * 128],
                              gatesb[:, f * 8:(f + 1) * 8],
                              start=(f == 0), stop=(f == NFC - 1)),
                     waits=wv if fc == 0 else (), inc=T if fc == NFC - 1 else None)
            smm = p.now(T)
            sc_ = p.on("scalar", lambda e, t=ts, c=col:
                       e.activation(sm_all[:, t * 8:(t + 1) * 8], PDp[:, c:c + 8],
                                    AF.Sigmoid), waits=smm, inc=S)
            sig_counts.append(sc_)
            sgd = p.now(S)
            p.on("vector", lambda e, t=ts:
                 e.tensor_add(u_tm, sm_all[:, t * 8:(t + 1) * 8], biasb),
                 waits=sgd + biasb_done, inc=V)
            p.on("vector", lambda e:
                 e.reduce_max(v12[:, 0:1], u_tm, axis=AX.X), inc=V)
            p.on("vector", lambda e:
                 e.tensor_scalar(m_tm, u_tm, v12[:, 0:1], None, ALU.is_equal),
                 inc=V)
            p.on("vector", lambda e:
                 e.scalar_tensor_tensor(u_tm, m_tm, -1e9, u_tm,
                                        ALU.mult, ALU.add), inc=V)
            p.on("vector", lambda e:
                 e.reduce_max(v12[:, 1:2], u_tm, axis=AX.X), inc=V)
            p.on("vector", lambda e:
                 e.tensor_scalar(u_tm, u_tm, v12[:, 1:2], None, ALU.is_ge),
                 inc=V)
            p.on("vector", lambda e:
                 e.tensor_add(m_tm, m_tm, u_tm), inc=V)
            p.on("vector", lambda e, t=ts:
                 e.tensor_mul(sm_all[:, t * 8:(t + 1) * 8],
                              sm_all[:, t * 8:(t + 1) * 8], m_tm), inc=V)
        sm_done = p.now(V)
        for ts in range(8):
            p.on("tensor", lambda e, t=ts:
                 e.transpose(PDp[0:8, t * 128:(t + 1) * 128],
                             sm_all[:, t * 8:(t + 1) * 8], identt[:, :]),
                 waits=sm_done if ts == 0 else (), inc=T if ts == 7 else None)
        smt_mm = p.now(T)
        p.on("vector", lambda e: e.tensor_copy(rb[0:8, :], PDp[0:8, :]),
             waits=smt_mm, inc=V)
        smt_done = p.now(V)

        M1 = [p.sem("M10"), p.sem("M11")]
        M3 = [p.sem("M30"), p.sem("M31")]
        M2 = [p.sem("M20"), p.sem("M21")]
        scratch2 = BX[:, 10240:11264]

        def w1_src(e, hc):
            return d["w1H"][:, (e * 8 + hc) * 2048:(e * 8 + hc + 1) * 2048]

        def w3_src(e, hc):
            return d["w3H"][:, (e * 8 + hc) * 2048:(e * 8 + hc + 1) * 2048]

        def w2_src(e, oc):
            return d["w2H"][:, (e * 16 + oc) * 1024:(e * 16 + oc + 1) * 1024]

        last_smb_rd = {"V": 0}
        last_w2_mm = {"T": 0}
        h_writes = {"G": 0, "V": 0}
        u_evac = {"V2": [0, 0], "S2": [0, 0], "G2": [0, 0]}

        p.dma("gpsimd", w13[0], w1_src(0, 0), M1[0], waits=acc_init)
        p.dma("gpsimd", w13b[0], w3_src(0, 0), M3[0], waits=acc_init)

        p.on("gpsimd", lambda e: e.memset(absacc[:, :], 0.0),
             waits=acc_init, inc=G)
        absg = {"n": p.now(G)[0][1]}
        abs_parts = [(m, ch) for m in range(3) for ch in range(8)]
        abs_rd = {"V": [0, 0]}

        def plan_abs(k):
            m, ch = abs_parts[k]
            i = k % 2
            dsem = DS0 if i == 0 else DS1
            nm = ("wupf", "wgtf", "wdnf")[m]
            p.dma("sync", absl2[i], d[nm][:, ch * 2048:(ch + 1) * 2048], dsem,
                  waits=[("V", abs_rd["V"][i]), ("G", absg["n"])])
            ld = p.now(dsem)
            p.on("vector", lambda e, i=i:
                 e.tensor_reduce(BX[:, 15484 + i:15485 + i], absl2[i],
                                 axis=AX.X, op=ALU.add,
                                 apply_absolute_value=True),
                 waits=ld, inc=V)
            c = p.on("vector", lambda e, m=m, i=i:
                     e.tensor_add(absacc[:, m:m + 1], absacc[:, m:m + 1],
                                  BX[:, 15484 + i:15485 + i]), inc=V)
            abs_rd["V"][i] = c

        for e in range(9):
            routed = e < 8
            for k in range(e * 3, min(e * 3 + 3, 24)):
                plan_abs(k)
            if routed:
                p.on("tensor", lambda e_, ex=e:
                     e_.matmul(PDp[:, 0:512], OH8_sb[0:8, ex * 128:(ex + 1) * 128],
                               rb[0:8, 0:512], start=True, stop=True),
                     waits=smt_done + oh_ready
                     + [("V", u_evac["V2"][1]), ("S", u_evac["S2"][1]),
                        ("G", u_evac["G2"][1])], inc=None)
                p.on("tensor", lambda e_, ex=e:
                     e_.matmul(PDp[:, 512:1024],
                               OH8_sb[0:8, ex * 128:(ex + 1) * 128],
                               rb[0:8, 512:1024], start=True, stop=True), inc=T)
                smb_mm = p.now(T)
                p.on("scalar", lambda e_: e_.copy(smb, PDp[:, :]),
                     waits=smb_mm + [("V", last_smb_rd["V"])], inc=S)
            smb_done = p.now(S)
            for hc in range(8):
                sl = (e * 8 + hc) % 2
                PU1 = PAp if hc % 2 == 0 else PCp
                PU3 = PBp if hc % 2 == 0 else PDp
                w1rd = p.now(M1[sl])
                w3rd = p.now(M3[sl])
                ne, nhc = (e, hc + 1) if hc < 7 else (e + 1, 0)
                if ne < 9:
                    nsl = (ne * 8 + nhc) % 2
                    p.dma("gpsimd", w13[nsl], w1_src(ne, nhc), M1[nsl],
                          waits=p.now(T))
                    p.dma("gpsimd", w13b[nsl], w3_src(ne, nhc), M3[nsl],
                          waits=p.now(T))
                uw = [("V", u_evac["V2"][hc % 2]), ("S", u_evac["S2"][hc % 2]),
                      ("G", u_evac["G2"][hc % 2])]
                for fc in range(NFC):
                    for nh in range(2):
                        p.on("tensor", lambda e_, f=fc, n=nh, s=sl, PU=PU1:
                             e_.matmul(PU[:, n * 512:(n + 1) * 512],
                                       w13[s][:, f * 128:(f + 1) * 128],
                                       BC[:, f * TLOC + n * 512:
                                          f * TLOC + (n + 1) * 512],
                                       start=(f == 0), stop=(f == NFC - 1)),
                             waits=(w1rd + uw)
                             if (fc == 0 and nh == 0) else (),
                             inc=T if (fc == NFC - 1 and nh == 1) else None)
                for fc in range(NFC):
                    for nh in range(2):
                        p.on("tensor", lambda e_, f=fc, n=nh, s=sl, PU=PU3:
                             e_.matmul(PU[:, n * 512:(n + 1) * 512],
                                       w13b[s][:, f * 128:(f + 1) * 128],
                                       BC[:, f * TLOC + n * 512:
                                          f * TLOC + (n + 1) * 512],
                                       start=(f == 0), stop=(f == NFC - 1)),
                             waits=(w3rd + smb_done)
                             if (fc == 0 and nh == 0) else (),
                             inc=T if (fc == NFC - 1 and nh == 1) else None)
                umm = p.now(T)
                hw = [("T", last_w2_mm["T"])]
                hprev = [("G", h_writes["G"]), ("V", h_writes["V"])]
                if routed:
                    p.on("vector", lambda e_, PU=PU1:
                         e_.tensor_tensor(kq_s[:, :], PU[:, :], smb, ALU.mult),
                         waits=umm + smb_done + hprev, inc=V)
                    kq1 = p.now(V)
                    p.on("scalar", lambda e_:
                         e_.activation(scratch2, kq_s[:, :], AF.Silu),
                         waits=kq1 + hprev, inc=S)
                    sld = p.now(S)
                    c = p.on("vector", lambda e_, PU=PU3:
                             e_.tensor_tensor(kq_s[:, :], PU[:, :], smb,
                                              ALU.mult),
                             waits=sld, inc=V)
                    t3d = p.now(V)
                    g = p.on("gpsimd", lambda e_, h=hc:
                             e_.tensor_mul(hbuf[:, h * 1024:(h + 1) * 1024],
                                           scratch2, kq_s[:, :]),
                             waits=t3d + hw, inc=G)
                    h_writes["G"] = g
                    u_evac["V2"][hc % 2] = c
                    u_evac["S2"][hc % 2] = sld[0][1]
                    u_evac["G2"][hc % 2] = g
                else:
                    p.on("scalar", lambda e_, PU=PU1:
                         e_.activation(scratch2, PU[:, :], AF.Silu),
                         waits=umm + hprev, inc=S)
                    sld = p.now(S)
                    c = p.on("vector", lambda e_, h=hc, PU=PU3:
                             e_.tensor_tensor(hbuf[:, h * 1024:(h + 1) * 1024],
                                              scratch2, PU[:, :], ALU.mult),
                             waits=sld + hw, inc=V)
                    h_writes["V"] = c
                    u_evac["V2"][hc % 2] = c
                    u_evac["S2"][hc % 2] = sld[0][1]
            h_done = [("G", h_writes["G"]), ("V", h_writes["V"])]
            w2_evac = [p.counts["V"], p.counts["V"]]
            p.dma("gpsimd", w2s[0], w2_src(e, 0), M2[0], waits=p.now(T))
            for oc in range(NFC):
                sl = oc % 2
                w2rd = p.now(M2[sl])
                if oc + 1 < NFC:
                    p.dma("gpsimd", w2s[(oc + 1) % 2], w2_src(e, oc + 1),
                          M2[(oc + 1) % 2], waits=p.now(T))
                PW = PCp if oc % 2 == 0 else PAp
                wv2 = ([("V", w2_evac[oc % 2])]
                       + (h_done + [("V", p.counts["V"])] if oc == 0 else []))
                for kc in range(8):
                    for nh in range(2):
                        p.on("tensor", lambda e_, k=kc, n=nh, s=sl, PW=PW:
                             e_.matmul(PW[:, n * 512:(n + 1) * 512],
                                       w2s[s][:, k * 128:(k + 1) * 128],
                                       hbuf[:, k * 1024 + n * 512:
                                            k * 1024 + (n + 1) * 512],
                                       start=(k == 0), stop=(k == 7)),
                             waits=(w2rd + wv2)
                             if (kc == 0 and nh == 0) else (),
                             inc=T if (kc == 7 and nh == 1) else None)
                wmm = p.now(T)
                last_w2_mm["T"] = wmm[0][1]
                c = p.on("vector", lambda e_, o=oc, PW=PW:
                         e_.scalar_tensor_tensor(BB[:, o * TLOC:(o + 1) * TLOC],
                                                 PW[:, :], 1.0,
                                                 BB[:, o * TLOC:(o + 1) * TLOC],
                                                 ALU.mult, ALU.add),
                         waits=wmm, inc=V)
                w2_evac[oc % 2] = c
            u_evac["V2"][0] = max(u_evac["V2"][0], p.counts["V"])
            u_evac["V2"][1] = max(u_evac["V2"][1], p.counts["V"])
            last_smb_rd["V"] = p.counts["V"]
        moe_done = p.now(V)

        abs_done = p.now(V)
        for m in range(3):
            p.on("tensor", lambda e, m=m:
                 e.matmul(PDp[0:1, m:m + 1], absacc[:, m:m + 1],
                          ones_col_f[:, :], start=True, stop=True),
                 waits=abs_done if m == 0 else (), inc=T if m == 2 else None)
        absmm = p.now(T)
        p.on("vector", lambda e: e.tensor_copy(absp_sb[0:1, 0:3],
                                               PDp[0:1, 0:3]),
             waits=absmm, inc=V)
        p.on("vector", lambda e: e.tensor_copy(absp_sb[0:1, 3:4],
                                               absp_sb[0:1, 0:1]), inc=V)
        p.dma("sync", d["absp"], absp_sb, ABS, waits=p.now(V))
        absdo = p.now(ABS)


        if "r2" in debug:
            p.dma("sync", d["dbg_r2"], BB[:, :], DO, waits=moe_done)

        layernorm(BB, BB, BC, moe_done)
        xn2_done = p.now(V)
        p.dma("sync", d["xn2T"], BB[:, :], DO, waits=xn2_done)
        if "xn2" in debug:
            p.dma("sync", d["dbg_xn2"], BB[:, :], DO, waits=xn2_done)
        if upto == "xn2":
            p.on("sync", lambda e: e.nop(), waits=p.now(DO, V, T, S, G))
            p.emit()
            return nc, d

        p.on("scalar", lambda e: e.activation(v12[:, 0:1], apar_sb[:, :],
                                              AF.Abs),
             waits=consts + p.now(V), inc=S)
        p.on("scalar", lambda e:
             e.activation(v12[:, 0:1], v12[:, 0:1], AF.Exp, scale=-1.0), inc=S)
        p.on("scalar", lambda e:
             e.activation(v12[:, 0:1], v12[:, 0:1], AF.Ln, bias=1.0), inc=S)
        sp_done = p.now(S)
        p.on("vector", lambda e: e.tensor_relu(v12[:, 1:2], apar_sb[:, :]),
             waits=sp_done, inc=V)
        p.on("vector", lambda e:
             e.tensor_add(afneg8[:, :], v12[:, 0:1], v12[:, 1:2]), inc=V)
        p.on("vector", lambda e:
             e.tensor_scalar(afneg8[:, :], afneg8[:, :], -8.0, None, ALU.mult),
             inc=V)
        af_done = p.now(V)
        p.on("gpsimd", lambda e: e.memset(rb[:, :], 0.0),
             waits=p.now(S, V), inc=G)
        zeros_done = p.now(G)

        def wg_src(oc):
            return d["wintH"][:, oc * 2048:(oc + 1) * 2048]

        gsl4 = [wuni[:, 0:2048], wuni[:, 2048:4096],
                Vbf[:, 0:2048], Vbf[:, 2048:4096]]
        gw_sem = [WQ0, WQ1, M1[0], M1[1]]
        gw_rd = {"T": [0, 0, 0, 0]}
        gw_first = p.now(V, S)

        def load_gw(c):
            for k, oc in ((0, c), (1, 16 + c)):
                sl = (2 * c + k) % 4
                p.dma("gpsimd", gsl4[sl], wg_src(oc), gw_sem[sl],
                      waits=[("T", gw_rd["T"][sl])]
                      + (gw_first if c < 2 else []))

        load_gw(0)
        for grp in range(4):
            grp_v = p.now(V)
            for j in range(4):
                c = grp * 4 + j
                slx = (2 * c) % 4
                sla = (2 * c + 1) % 4
                gxw = p.now(gw_sem[slx])
                gaw = p.now(gw_sem[sla])
                if c + 1 < 16:
                    load_gw(c + 1)
                for fc in range(NFC):
                    for nh in range(2):
                        p.on("tensor", lambda e, f=fc, n=nh, s=slx:
                             e.matmul(PAp[:, n * 512:(n + 1) * 512],
                                      gsl4[s][:, f * 128:(f + 1) * 128],
                                      BC[:, f * TLOC + n * 512:
                                         f * TLOC + (n + 1) * 512],
                                      start=(f == 0), stop=(f == NFC - 1)),
                             waits=(gxw + [("S", p.counts["S"])])
                             if (fc == 0 and nh == 0) else (),
                             inc=T if (fc == NFC - 1 and nh == 1) else None)
                gxmm = p.now(T)
                gw_rd["T"][slx] = gxmm[0][1]
                p.on("scalar", lambda e, j=j:
                     e.activation(gx4[:, j * 1024:(j + 1) * 1024], PAp[:, :],
                                  AF.Sigmoid), waits=gxmm + grp_v + absdo, inc=S)
                for fc in range(NFC):
                    for nh in range(2):
                        p.on("tensor", lambda e, f=fc, n=nh, s=sla:
                             e.matmul(PBp[:, n * 512:(n + 1) * 512],
                                      gsl4[s][:, f * 128:(f + 1) * 128],
                                      BC[:, f * TLOC + n * 512:
                                         f * TLOC + (n + 1) * 512],
                                      start=(f == 0), stop=(f == NFC - 1)),
                             waits=(gaw + [("S", p.counts["S"])])
                             if (fc == 0 and nh == 0) else (),
                             inc=T if (fc == NFC - 1 and nh == 1) else None)
                gamm = p.now(T)
                gw_rd["T"][sla] = gamm[0][1]
                p.on("scalar", lambda e, j=j:
                     e.activation(la4[:, j * 1024:(j + 1) * 1024], PBp[:, :],
                                  AF.Sigmoid), waits=gamm + grp_v, inc=S)
            sig_done = p.now(S)
            for j in range(4):
                p.on("vector", lambda e, j=j:
                     e.tensor_scalar(la4[:, j * 1024:(j + 1) * 1024],
                                     la4[:, j * 1024:(j + 1) * 1024],
                                     afneg8[:, 0:1], None, ALU.mult),
                     waits=(list(sig_done) + list(af_done)) if j == 0 else (),
                     inc=V)
            la_done = p.now(V)
            for j in range(4):
                p.on("scalar", lambda e, j=j:
                     e.activation(a4[:, j * 1024:(j + 1) * 1024],
                                  la4[:, j * 1024:(j + 1) * 1024], AF.Exp),
                     waits=la_done if j == 0 else (), inc=S)
                ps_t = (PAp, PBp, PCp, PDp)[j]
                p.on("scalar", lambda e, j=j, t=ps_t:
                     e.activation(t[:, :], la4[:, j * 1024:(j + 1) * 1024],
                                  AF.Exp, scale=2.0),
                     waits=[("T", p.counts["T"])], inc=S)
            e2_done = p.now(S)
            for j in range(4):
                ps_t = (PAp, PBp, PCp, PDp)[j]
                p.on("scalar", lambda e, j=j, t=ps_t:
                     e.activation(b4[:, j * 1024:(j + 1) * 1024], t[:, :],
                                  AF.Sqrt, bias=1.0, scale=-1.0),
                     waits=e2_done if j == 0 else (), inc=S)
            mult_done = p.now(S)
            for j in range(4):
                c = grp * 4 + j
                p.on("vector", lambda e, j=j:
                     e.tensor_mul(b4[:, j * 1024:(j + 1) * 1024],
                                  b4[:, j * 1024:(j + 1) * 1024],
                                  gx4[:, j * 1024:(j + 1) * 1024]),
                     waits=mult_done if j == 0 else (), inc=V)
                p.on("vector", lambda e, j=j, c=c:
                     e.tensor_mul(b4[:, j * 1024:(j + 1) * 1024],
                                  b4[:, j * 1024:(j + 1) * 1024],
                                  BB[:, c * TLOC:(c + 1) * TLOC]), inc=V)
                p.on("vector", lambda e, j=j:
                     e.tensor_tensor_scan(gx4[:, j * 1024:(j + 1) * 1024],
                                          a4[:, j * 1024:(j + 1) * 1024],
                                          b4[:, j * 1024:(j + 1) * 1024],
                                          0.0, ALU.mult, ALU.add),
                     waits=zeros_done, inc=V)
                p.on("vector", lambda e, j=j, c=c:
                     e.tensor_copy(carB_sb[:, c:c + 1],
                                   gx4[:, j * 1024 + 1023: j * 1024 + 1024]),
                     inc=V)
                p.on("vector", lambda e, j=j:
                     e.tensor_tensor_scan(la4[:, j * 1024:(j + 1) * 1024],
                                          a4[:, j * 1024:(j + 1) * 1024],
                                          rb[:, :], 1.0, ALU.mult, ALU.add),
                     inc=V)
                p.on("vector", lambda e, j=j, c=c:
                     e.tensor_copy(carA_sb[:, c:c + 1],
                                   la4[:, j * 1024 + 1023: j * 1024 + 1024]),
                     inc=V)
        scans_done = p.now(V)
        p.dma("sync", d["carA"], carA_sb, DO, waits=scans_done)
        p.dma("sync", d["carB"], carB_sb, DO, waits=scans_done)


        p.on("sync", lambda e: e.nop(), waits=p.now(DO, ABS, V, T, S, G))
        p.emit()

    return nc, d



def build_launch_b(debug=(), upto="all"):
    nc = bass.Bass("TRN2", target_bir_lowering=False)
    d = {}

    def inp(name, shape, dt=F32):
        d[name] = nc.dram_tensor(name, shape, dt, kind="ExternalInput").ap()
        return d[name]

    def outp(name, shape, dt=F32):
        d[name] = nc.dram_tensor(name, shape, dt, kind="ExternalOutput").ap()
        return d[name]

    inp("xn2T", [128, NFC * TLOC])
    inp("carApre", [128, 3 * NFC])
    inp("carBpre", [128, 3 * NFC])
    inp("absps", [8, 4])
    inp("wintH", [128, 32 * 2048])
    inp("apar", [128, 1])
    inp("wupH", [128, 32 * 4096])
    inp("wgtH", [128, 32 * 4096])
    inp("wdnT", [8192, DMODEL])

    outp("yT", [128, NFC * TLOC])
    for t in debug:
        outp("dbg_" + t, [128, NFC * TLOC])

    es = contextlib.ExitStack()
    with es:
        p = Prog(nc, es)

        def sb(name, shape, dt):
            return es.enter_context(nc.sbuf_tensor(name, shape, dt))

        BB = sb("BB", [128, NFC * TLOC], F32)
        BC = sb("BC", [128, NFC * TLOC], BF16)
        BX = sb("BX", [128, NFC * TLOC], F32)

        gx4 = BX[:, 0:4096]
        la4 = BX[:, 4096:8192]
        a4 = BX[:, 8192:12288]
        b4 = BX[:, 12288:16384]
        qup = bc16(BX[:, 0:4096])
        qgt = bc16(BX[:, 4096:8192])
        qdn = bc16(BX[:, 8192:12288])
        stg = [BX[:, 12288 + i * 2048: 12288 + (i + 1) * 2048] for i in range(2)]

        hb = sb("hb", [128, 2048], BF16)
        i8t = [sb(f"i8t{i}", [128, 2048], I8) for i in range(2)]
        wuni = sb("wuni", [128, 4096], BF16)
        wuni2 = sb("wuni2", [128, 4096], BF16)
        gsl4 = [wuni[:, 0:2048], wuni[:, 2048:4096],
                wuni2[:, 0:2048], wuni2[:, 2048:4096]]
        carA_t = sb("carA_t", [128, 3 * NFC], F32)
        carB_t = sb("carB_t", [128, 3 * NFC], F32)
        Hc = sb("Hc", [128, NFC], F32)
        apar_sb = sb("apar_sb", [128, 1], F32)
        afneg8 = sb("afneg8", [128, 1], F32)
        absps_sb = sb("absps_sb", [8, 4], F32)
        rsig = sb("rsig", [128, 4], F32)
        sig = sb("sig", [128, 4], F32)
        shalf = sb("shalf", [128, 4], F32)
        nshalf = sb("nshalf", [128, 4], F32)
        rows3 = sb("rows3", [1, 4], F32)
        kq_s = sb("kq_s", [128, TLOC], F32)
        SC2 = sb("SC2", [128, 512], F32)
        sqs = bc16(SC2[:, :])
        sqs_rd = {"T": 0}
        ones_col_bf = sb("ones_col_bf", [128, 1], BF16)
        ones_col_f = sb("ones_col_f", [128, 1], F32)
        ones_row_f = sb("ones_row_f", [1, 128], F32)
        rowA = sb("rowA", [1, TLOC], F32)
        rowB = sb("rowB", [1, TLOC], F32)
        rowC = sb("rowC", [1, TLOC], F32)
        rb = sb("rb", [128, TLOC], F32)
        mrsb_v = wuni2[:, 0:2048].bitcast(F32)

        def psum(name):
            return es.enter_context(nc.psum_tensor(name, [128, TLOC], F32))

        PAp, PBp, PCp, PDp = psum("PA"), psum("PB"), psum("PC"), psum("PD")
        PS4 = [PAp, PBp, PCp, PDp]

        T = p.sem("T", "tensor")
        V = p.sem("V", "vector")
        S = p.sem("S", "scalar")
        G = p.sem("G", "gpsimd")
        CONST = p.sem("CONST")
        DS0 = p.sem("DS0")
        DS1 = p.sem("DS1")
        WQ0 = p.sem("WQ0")
        WQ1 = p.sem("WQ1")
        UP0 = p.sem("UP0")
        UP1 = p.sem("UP1")
        GT0 = p.sem("GT0")
        GT1 = p.sem("GT1")
        DN0 = p.sem("DN0")
        DN1 = p.sem("DN1")
        DO = p.sem("DO")

        p.dma("gpsimd", carA_t[:, :], d["carApre"], CONST)
        p.dma("gpsimd", carB_t[:, :], d["carBpre"], CONST)
        p.dma("gpsimd", apar_sb[:, :], d["apar"], CONST)
        p.dma("gpsimd", absps_sb[:, :], d["absps"], CONST)
        p.on("gpsimd", lambda e: e.memset(ones_col_bf[:, :], 1.0), inc=G)
        p.on("gpsimd", lambda e: e.memset(ones_col_f[:, :], 1.0), inc=G)
        p.on("gpsimd", lambda e: e.memset(ones_row_f[:, :], 1.0), inc=G)
        p.dma("sync", BB[:, :], d["xn2T"], DS0)
        consts = p.now(CONST, G, DS0)

        for fc in range(NFC):
            p.on("vector", lambda e, f=fc:
                 e.tensor_copy(BC[:, f * TLOC:(f + 1) * TLOC],
                               BB[:, f * TLOC:(f + 1) * TLOC]),
                 waits=consts if fc == 0 else (), inc=V)
        bc_done = p.now(V)

        p.on("vector", lambda e: e.tensor_copy(Hc[:, :], carB_t[:, 0:NFC]),
             waits=consts, inc=V)
        for j in (1, 2):
            p.on("vector", lambda e, j=j:
                 e.tensor_mul(Hc[:, :], Hc[:, :],
                              carA_t[:, j * NFC:(j + 1) * NFC]), inc=V)
            p.on("vector", lambda e, j=j:
                 e.tensor_add(Hc[:, :], Hc[:, :],
                              carB_t[:, j * NFC:(j + 1) * NFC]), inc=V)
        h_ready = p.now(V)

        p.on("scalar", lambda e: e.activation(rsig[:, 3:4], apar_sb[:, :], AF.Abs),
             waits=consts, inc=S)
        p.on("scalar", lambda e:
             e.activation(rsig[:, 3:4], rsig[:, 3:4], AF.Exp, scale=-1.0), inc=S)
        p.on("scalar", lambda e:
             e.activation(rsig[:, 3:4], rsig[:, 3:4], AF.Ln, bias=1.0), inc=S)
        sp_done = p.now(S)
        p.on("vector", lambda e: e.tensor_relu(afneg8[:, :], apar_sb[:, :]),
             waits=sp_done, inc=V)
        p.on("vector", lambda e:
             e.tensor_add(afneg8[:, :], afneg8[:, :], rsig[:, 3:4]), inc=V)
        p.on("vector", lambda e:
             e.tensor_scalar(afneg8[:, :], afneg8[:, :], -8.0, None, ALU.mult),
             inc=V)
        af_done = p.now(V)

        def wg_src(oc):
            return d["wintH"][:, oc * 2048:(oc + 1) * 2048]

        gw_sem = [WQ0, WQ1, GT0, GT1]
        gw_rd = {"T": [0, 0, 0, 0]}

        def load_gw(c):
            for k, oc in ((0, c), (1, 16 + c)):
                sl = (2 * c + k) % 4
                p.dma("gpsimd", gsl4[sl], wg_src(oc), gw_sem[sl],
                      waits=[("T", gw_rd["T"][sl])])
            return p.now(gw_sem[(2 * c) % 4], gw_sem[(2 * c + 1) % 4])

        load_gw(0)
        for grp in range(4):
            grp_v = p.now(V)
            for j in range(4):
                c = grp * 4 + j
                slx = (2 * c) % 4
                sla = (2 * c + 1) % 4
                gxw = p.now(gw_sem[slx])
                gaw = p.now(gw_sem[sla])
                if c + 1 < 16:
                    load_gw(c + 1)
                for fc in range(NFC):
                    for nh in range(2):
                        p.on("tensor", lambda e, f=fc, n=nh, s=slx:
                             e.matmul(PAp[:, n * 512:(n + 1) * 512],
                                      gsl4[s][:, f * 128:(f + 1) * 128],
                                      BC[:, f * TLOC + n * 512:
                                         f * TLOC + (n + 1) * 512],
                                      start=(f == 0), stop=(f == NFC - 1)),
                             waits=(list(gxw) + list(bc_done)
                                    + [("S", p.counts["S"])])
                             if (fc == 0 and nh == 0) else (),
                             inc=T if (fc == NFC - 1 and nh == 1) else None)
                gxmm = p.now(T)
                gw_rd["T"][slx] = gxmm[0][1]
                p.on("scalar", lambda e, j=j:
                     e.activation(gx4[:, j * 1024:(j + 1) * 1024], PAp[:, :],
                                  AF.Sigmoid), waits=gxmm + grp_v, inc=S)
                for fc in range(NFC):
                    for nh in range(2):
                        p.on("tensor", lambda e, f=fc, n=nh, s=sla:
                             e.matmul(PBp[:, n * 512:(n + 1) * 512],
                                      gsl4[s][:, f * 128:(f + 1) * 128],
                                      BC[:, f * TLOC + n * 512:
                                         f * TLOC + (n + 1) * 512],
                                      start=(f == 0), stop=(f == NFC - 1)),
                             waits=(list(gaw) + [("S", p.counts["S"])])
                             if (fc == 0 and nh == 0) else (),
                             inc=T if (fc == NFC - 1 and nh == 1) else None)
                gamm = p.now(T)
                gw_rd["T"][sla] = gamm[0][1]
                p.on("scalar", lambda e, j=j:
                     e.activation(la4[:, j * 1024:(j + 1) * 1024], PBp[:, :],
                                  AF.Sigmoid), waits=gamm + grp_v, inc=S)
            sig_done = p.now(S)
            for j in range(4):
                p.on("vector", lambda e, j=j:
                     e.tensor_scalar(la4[:, j * 1024:(j + 1) * 1024],
                                     la4[:, j * 1024:(j + 1) * 1024],
                                     afneg8[:, 0:1], None, ALU.mult),
                     waits=(list(sig_done) + list(af_done)) if j == 0 else (),
                     inc=V)
            la_done = p.now(V)
            for j in range(4):
                p.on("scalar", lambda e, j=j:
                     e.activation(a4[:, j * 1024:(j + 1) * 1024],
                                  la4[:, j * 1024:(j + 1) * 1024], AF.Exp),
                     waits=la_done if j == 0 else (), inc=S)
                ps_t = (PAp, PBp, PCp, PDp)[j]
                p.on("scalar", lambda e, j=j, t=ps_t:
                     e.activation(t[:, :], la4[:, j * 1024:(j + 1) * 1024],
                                  AF.Exp, scale=2.0),
                     waits=[("T", p.counts["T"])], inc=S)
            e2_done = p.now(S)
            for j in range(4):
                ps_t = (PAp, PBp, PCp, PDp)[j]
                p.on("scalar", lambda e, j=j, t=ps_t:
                     e.activation(b4[:, j * 1024:(j + 1) * 1024], t[:, :],
                                  AF.Sqrt, bias=1.0, scale=-1.0),
                     waits=e2_done if j == 0 else (), inc=S)
            mult_done = p.now(S)
            for j in range(4):
                c = grp * 4 + j
                p.on("vector", lambda e, j=j:
                     e.tensor_mul(b4[:, j * 1024:(j + 1) * 1024],
                                  b4[:, j * 1024:(j + 1) * 1024],
                                  gx4[:, j * 1024:(j + 1) * 1024]),
                     waits=mult_done if j == 0 else (), inc=V)
                p.on("vector", lambda e, j=j, c=c:
                     e.tensor_mul(b4[:, j * 1024:(j + 1) * 1024],
                                  b4[:, j * 1024:(j + 1) * 1024],
                                  BB[:, c * TLOC:(c + 1) * TLOC]), inc=V)
                p.on("vector", lambda e, j=j, c=c:
                     e.tensor_tensor_scan(gx4[:, j * 1024:(j + 1) * 1024],
                                          a4[:, j * 1024:(j + 1) * 1024],
                                          b4[:, j * 1024:(j + 1) * 1024],
                                          Hc[:, c:c + 1], ALU.mult, ALU.add),
                     waits=h_ready, inc=V)
                p.on("vector", lambda e, j=j, c=c:
                     e.tensor_add(BB[:, c * TLOC:(c + 1) * TLOC],
                                  gx4[:, j * 1024:(j + 1) * 1024],
                                  BB[:, c * TLOC:(c + 1) * TLOC]), inc=V)
        r3_done = p.now(V)

        def layernorm(src_f32, dst_f32, dst_bf, waits):
            for fc in range(NFC):
                p.on("vector", lambda e, f=fc:
                     e.tensor_copy(dst_bf[:, f * TLOC:(f + 1) * TLOC],
                                   src_f32[:, f * TLOC:(f + 1) * TLOC]),
                     waits=waits if fc == 0 else (), inc=V)
            cast_done = p.now(V)
            for fc in range(NFC):
                for nh in range(2):
                    p.on("tensor", lambda e, f=fc, n=nh:
                         e.matmul(PDp[0:1, n * 512:(n + 1) * 512],
                                  ones_col_bf[:, :],
                                  dst_bf[:, f * TLOC + n * 512:
                                         f * TLOC + (n + 1) * 512],
                                  start=(f == 0), stop=(f == NFC - 1)),
                         waits=cast_done if (fc == 0 and nh == 0) else (),
                         inc=T if (fc == NFC - 1 and nh == 1) else None)
            for fc in range(NFC):
                p.on("scalar", lambda e, f=fc:
                     e.activation(sqs, dst_bf[:, f * TLOC:(f + 1) * TLOC],
                                  AF.Square),
                     waits=list(cast_done if fc == 0 else [])
                     + [("T", sqs_rd["T"])], inc=S)
                sqd = p.now(S)
                for nh in range(2):
                    p.on("tensor", lambda e, f=fc, n=nh:
                         e.matmul(PAp[0:1, n * 512:(n + 1) * 512],
                                  ones_col_bf[:, :],
                                  sqs[:, n * 512:(n + 1) * 512],
                                  start=(f == 0), stop=(f == NFC - 1)),
                         waits=(sqd + p.now(V)) if nh == 0 else (),
                         inc=T if nh == 1 else None)
                sqs_rd["T"] = p.now(T)[0][1]
            stats_mm = p.now(T)
            p.on("vector", lambda e:
                 e.tensor_scalar(rowA[:, :], PDp[0:1, :], 1.0 / DMODEL, None,
                                 ALU.mult), waits=stats_mm, inc=V)
            p.on("vector", lambda e:
                 e.tensor_tensor(rowB[:, :], rowA[:, :], rowA[:, :], ALU.mult),
                 inc=V)
            p.on("vector", lambda e:
                 e.scalar_tensor_tensor(rowC[:, :], PAp[0:1, :], 1.0 / DMODEL,
                                        rowB[:, :], ALU.mult, ALU.subtract),
                 inc=V)
            p.on("vector", lambda e:
                 e.tensor_scalar(rowC[:, :], rowC[:, :], 1e-5, None, ALU.add),
                 inc=V)
            vd = p.now(V)
            p.on("scalar", lambda e:
                 e.activation(rowC[:, :], rowC[:, :], AF.Ln),
                 waits=vd, inc=S)
            p.on("scalar", lambda e:
                 e.activation(rowC[:, :], rowC[:, :], AF.Exp, scale=-0.5), inc=S)
            rstd_done = p.now(S)
            p.on("vector", lambda e:
                 e.tensor_tensor(rowB[:, :], rowA[:, :], rowC[:, :], ALU.mult),
                 waits=rstd_done, inc=V)
            mrs_done = p.now(V)
            for nh in range(2):
                p.on("tensor", lambda e, n=nh:
                     e.matmul(PDp[:, n * 512:(n + 1) * 512], ones_row_f[:, :],
                              rowC[0:1, n * 512:(n + 1) * 512], start=True,
                              stop=True),
                     waits=mrs_done if nh == 0 else (), inc=None)
                p.on("tensor", lambda e, n=nh:
                     e.matmul(PAp[:, n * 512:(n + 1) * 512], ones_row_f[:, :],
                              rowB[0:1, n * 512:(n + 1) * 512], start=True,
                              stop=True),
                     inc=T if nh == 1 else None)
            bc_mm = p.now(T)
            p.on("scalar", lambda e: e.copy(rb[:, :], PDp[:, :]),
                 waits=bc_mm, inc=S)
            p.on("scalar", lambda e: e.copy(mrsb_v[:, :], PAp[:, :]), inc=S)
            bcd = p.now(S)
            for fc in range(NFC):
                p.on("vector", lambda e, f=fc:
                     e.tensor_tensor(kq_s[:, :], src_f32[:, f * TLOC:(f + 1) * TLOC],
                                     rb[:, :], ALU.mult),
                     waits=bcd if fc == 0 else (), inc=V)
                p.on("vector", lambda e, f=fc:
                     e.tensor_sub(dst_f32[:, f * TLOC:(f + 1) * TLOC],
                                  kq_s[:, :], mrsb_v[:, :]), inc=V)
                p.on("vector", lambda e, f=fc:
                     e.tensor_copy(dst_bf[:, f * TLOC:(f + 1) * TLOC],
                                   dst_f32[:, f * TLOC:(f + 1) * TLOC]), inc=V)

        layernorm(BB, BB, BC, r3_done)
        xn3_done = p.now(V)
        dbg_guard = []
        if "xn3" in debug:
            p.dma("sync", d["dbg_xn3"], BB[:, :], DO, waits=p.now(V))
            dbg_guard = p.now(DO)
        if upto == "xn3":
            p.on("sync", lambda e: e.nop(), waits=p.now(DO, V, T, S, G))
            p.emit()
            return nc, d

        p.on("tensor", lambda e:
             e.matmul(PDp[0:1, 0:4], ones_col_f[0:8, :], absps_sb[:, :],
                      start=True, stop=True),
             waits=consts + p.now(V, S), inc=T)
        s_mm = p.now(T)
        p.on("vector", lambda e:
             e.tensor_scalar(rows3[:, :], PDp[0:1, 0:4], 1.0 / 16777216.0,
                             None, ALU.mult), waits=s_mm, inc=V)
        p.on("vector", lambda e:
             e.tensor_scalar(rows3[:, :], rows3[:, :], 1e-8, None, ALU.add),
             inc=V)
        sc_done = p.now(V)
        p.on("scalar", lambda e: e.activation(rows3[:, :], rows3[:, :], AF.Ln),
             waits=sc_done, inc=S)
        p.on("scalar", lambda e:
             e.activation(rows3[:, :], rows3[:, :], AF.Exp, scale=-1.0), inc=S)
        rs_done = p.now(S)
        p.on("tensor", lambda e:
             e.matmul(PDp[:, 0:4], ones_row_f[:, :], rows3[0:1, :],
                      start=True, stop=True), waits=rs_done, inc=T)
        rs_mm = p.now(T)
        p.on("vector", lambda e: e.tensor_copy(rsig[:, :], PDp[:, 0:4]),
             waits=rs_mm, inc=V)
        p.on("scalar", lambda e:
             e.activation(rows3[:, :], rows3[:, :], AF.Ln),
             waits=p.now(T), inc=S)
        p.on("scalar", lambda e:
             e.activation(rows3[:, :], rows3[:, :], AF.Exp, scale=-1.0), inc=S)
        s2_done = p.now(S)
        p.on("tensor", lambda e:
             e.matmul(PDp[:, 4:8], ones_row_f[:, :], rows3[0:1, :],
                      start=True, stop=True), waits=s2_done, inc=T)
        s2_mm = p.now(T)
        p.on("vector", lambda e: e.tensor_copy(sig[:, :], PDp[:, 4:8]),
             waits=s2_mm, inc=V)
        p.on("vector", lambda e:
             e.tensor_scalar(shalf[:, :], sig[:, :], 0.5, None, ALU.mult),
             inc=V)
        p.on("vector", lambda e:
             e.tensor_scalar(nshalf[:, :], sig[:, :], -0.5, None, ALU.mult),
             inc=V)
        rsig_done = p.now(V)

        NG = 32
        QUP = [bc16(BX[:, i * 2048:(i + 1) * 2048]) for i in range(2)]
        QGT = [bc16(BX[:, 4096 + i * 2048: 4096 + (i + 1) * 2048])
               for i in range(2)]
        QDN = [bc16(BX[:, 8192 + i * 2048: 8192 + (i + 1) * 2048])
               for i in range(3)]
        stgl = wuni[:, 0:4096].bitcast(F32)
        stg = [BX[:, 14336 + i * 1024: 14336 + (i + 1) * 1024] for i in range(2)]
        stg = stg + [stgl[:, 0:1024], stgl[:, 1024:2048]]
        i8q = [i8t[0][:, 0:1024], i8t[1][:, 0:1024],
               i8t[0][:, 1024:2048], i8t[1][:, 1024:2048]]
        hb2 = [hb[:, :], wuni2[:, 2048:4096]]

        ffn_v0 = p.counts["V"]
        ffn_s0 = p.counts["S"]
        ffn_t0 = p.counts["T"]
        stg_rd = {"V": [ffn_v0] * 4}
        i8_rd = {"S": [ffn_s0] * 4}
        q_rd_ug = [ffn_t0, ffn_t0]
        q_rd_dn = [ffn_t0, ffn_t0, ffn_t0]
        hb_rd = [ffn_t0, ffn_t0]
        kq_rd = {"V": ffn_v0}
        pa_rd = {"S": ffn_s0}
        ps_evac = [ffn_v0, ffn_v0]
        h_done_g = [(), ()]
        q_done = {}

        def quantize_pieces(g, plo, phi):
            par = g % 2
            dpar = g % 3
            for pi in range(plo, phi):
                mat, pc = divmod(pi, 4)
                sl = pi % 2 + 2 * ((pi // 2) % 2)
                st = stg[sl]
                i8v = i8q[sl]
                if mat < 2:
                    w = d["wupH"] if mat == 0 else d["wgtH"]
                    srcap = w[:, g * 4096 + pc * 1024: g * 4096 + (pc + 1) * 1024]
                    qb = QUP[par] if mat == 0 else QGT[par]
                    qslice = qb[:, pc * 1024:(pc + 1) * 1024]
                else:
                    kc2, half = divmod(pc, 2)
                    srcap = d["wdnT"][g * 256 + kc2 * 128:
                                      g * 256 + (kc2 + 1) * 128,
                                      half * 1024:(half + 1) * 1024]
                    qslice = QDN[dpar][:, kc2 * 2048 + half * 1024:
                                       kc2 * 2048 + (half + 1) * 1024]
                lane = pi % 2
                eng = "sync" if lane == 0 else "gpsimd"
                qsem = (UP0, UP1, DS1, GT0)[sl]
                tg = ("T", q_rd_ug[par]) if mat < 2 else ("T", q_rd_dn[dpar])
                p.dma(eng, st, srcap, qsem,
                      waits=[("V", stg_rd["V"][sl]), ("S", ffn_s0),
                             ("T", ffn_t0)])
                ld = p.now(qsem)
                p.on("vector", lambda e, st=st, i8v=i8v, m=min(mat, 2):
                     e.tensor_scalar(i8v, st, rsig[:, m:m + 1], None, ALU.mult),
                     waits=ld + rsig_done + [("S", i8_rd["S"][sl])], inc=V)
                stg_rd["V"][sl] = p.counts["V"]
                i8d = p.now(V)
                p.on("scalar", lambda e, i8v=i8v, qslice=qslice:
                     e.activation(qslice, i8v, AF.Sign),
                     waits=i8d + [tg], inc=S)
                i8_rd["S"][sl] = p.counts["S"]
            return p.now(S)

        def down_og(gd, og):
            par = gd % 2
            dpar = gd % 3
            PW = PCp if og % 2 == 0 else PDp
            w_ = [("V", ps_evac[og % 2])]
            if og == 0:
                w_ = w_ + list(h_done_g[par]) + dbg_guard
            for kc in range(2):
                for nh in range(2):
                    p.on("tensor", lambda e, k=kc, o=og, n=nh, PW=PW,
                         dpar=dpar, par=par:
                         e.matmul(PW[:, n * 512:(n + 1) * 512],
                                  QDN[dpar][:, k * 2048 + o * 128:
                                           k * 2048 + (o + 1) * 128],
                                  hb2[par][:, k * 1024 + n * 512:
                                           k * 1024 + (n + 1) * 512],
                                  start=(k == 0), stop=(k == 1)),
                         waits=w_ if (kc == 0 and nh == 0) else (),
                         inc=T if (kc == 1 and nh == 1) else None)
            dmm = p.now(T)
            q_rd_dn[dpar] = dmm[0][1]
            hb_rd[par] = dmm[0][1]
            c = p.on("vector", lambda e, o=og, PW=PW:
                     e.scalar_tensor_tensor(BB[:, o * TLOC:(o + 1) * TLOC],
                                            PW[:, :], sig[:, 2:3],
                                            BB[:, o * TLOC:(o + 1) * TLOC],
                                            ALU.mult, ALU.add),
                     waits=dmm, inc=V)
            ps_evac[og % 2] = c

        q_done[0] = quantize_pieces(0, 0, 12)
        q_done[1] = quantize_pieces(1, 0, 12)
        for g in range(NG):
            par = g % 2
            qd = q_done.pop(g)
            for hc in range(2):
                for fc in range(NFC):
                    for nh in range(2):
                        p.on("tensor", lambda e, f=fc, n=nh, h=hc, par=par:
                             e.matmul(PAp[:, n * 512:(n + 1) * 512],
                                      QUP[par][:, f * 256 + h * 128:
                                               f * 256 + (h + 1) * 128],
                                      BC[:, f * TLOC + n * 512:
                                         f * TLOC + (n + 1) * 512],
                                      start=(f == 0), stop=(f == NFC - 1)),
                             waits=(qd + [("V", kq_rd["V"]),
                                          ("S", pa_rd["S"])])
                             if (fc == 0 and nh == 0) else (),
                             inc=T if (fc == NFC - 1 and nh == 1) else None)
                if g > 0:
                    for og in range(8 * hc, 8 * hc + 4):
                        down_og(g - 1, og)
                for fc in range(NFC):
                    for nh in range(2):
                        p.on("tensor", lambda e, f=fc, n=nh, h=hc, par=par:
                             e.matmul(PBp[:, n * 512:(n + 1) * 512],
                                      QGT[par][:, f * 256 + h * 128:
                                               f * 256 + (h + 1) * 128],
                                      BC[:, f * TLOC + n * 512:
                                         f * TLOC + (n + 1) * 512],
                                      start=(f == 0), stop=(f == NFC - 1)),
                             inc=T if (fc == NFC - 1 and nh == 1) else None)
                if g > 0:
                    for og in range(8 * hc + 4, 8 * hc + 8):
                        down_og(g - 1, og)
                umm = p.now(T)
                q_rd_ug[par] = umm[0][1]
                p.on("scalar", lambda e:
                     e.activation(kq_s[:, :], PAp[:, :], AF.Silu,
                                  scale=sig[:, 0:1]),
                     waits=umm + [("V", kq_rd["V"])], inc=S)
                pa_rd["S"] = p.counts["S"]
                sld = p.now(S)
                p.on("vector", lambda e, h=hc, par=par:
                     e.scalar_tensor_tensor(hb2[par][:, h * 1024:(h + 1) * 1024],
                                            PBp[:, :], sig[:, 1:2],
                                            kq_s[:, :], ALU.mult, ALU.mult),
                     waits=sld + [("T", hb_rd[par])], inc=V)
                kq_rd["V"] = p.counts["V"]
                if hc == 0 and 2 <= g + 1 < NG:
                    q_done[g + 1] = quantize_pieces(g + 1, 8, 12)
            h_done_g[par] = p.now(V)
            if g + 2 < NG:
                quantize_pieces(g + 2, 0, 8)
        for og in range(16):
            down_og(NG - 1, og)
        ffn_done = p.now(V)

        if "r4" in debug:
            p.dma("sync", d["dbg_r4"], BB[:, :], DO, waits=ffn_done)
            ffn_done = list(ffn_done) + p.now(DO)
        if "qup" in debug:
            p.dma("sync", d["dbg_qup"], BX[:, :], DO, waits=ffn_done)
            ffn_done = list(ffn_done) + p.now(DO)

        layernorm(BB, BB, BC, ffn_done)
        y_done = p.now(V)
        p.dma("sync", d["yT"], BB[:, :], DO, waits=y_done)
        p.on("sync", lambda e: e.nop(), waits=p.now(DO, V, T, S, G))
        p.emit()

    return nc, d



def _rope_tables(T, dim=128, base=10000.0):
    inv = 1.0 / base ** (np.arange(0, dim, 2, dtype=np.float64) / dim)
    t = np.arange(T, dtype=np.float64)
    fr = t[:, None] * inv[None, :]
    emb = np.concatenate([fr, fr], axis=1)
    return (np.ascontiguousarray(np.cos(emb).T.astype(np.float32)),
            np.ascontiguousarray(np.sin(emb).T.astype(np.float32)))


def _rot_matrix():
    P = np.zeros((128, 128), np.float32)
    for dd in range(64):
        P[dd, dd + 64] = -1.0
        P[dd + 64, dd] = 1.0
    return np.ascontiguousarray(P.T)


def make_inputs_a(inputs):
    x = np.asarray(inputs["L0_attn_x"], np.float32)
    wat = np.ascontiguousarray(np.asarray(inputs["L0_attn_w_attn"], np.float32))
    wproj = np.asarray(inputs["L0_attn_w_proj"], np.float32)
    watqH = np.ascontiguousarray(
        wat[:, 0:2048].reshape(16, 128, 16, 128).transpose(1, 2, 0, 3)
        .reshape(128, -1))
    wprojH = np.ascontiguousarray(
        wproj.reshape(16, 128, 16, 128).transpose(1, 2, 0, 3).reshape(128, -1))
    cosT, sinT = _rope_tables(TFULL)
    PT = _rot_matrix()
    ident = np.eye(128, dtype=np.float32)
    gateT = np.ascontiguousarray(np.asarray(inputs["L0_ffn_gate"], np.float32).T)
    bias8 = np.ascontiguousarray(
        np.asarray(inputs["L0_ffn_expert_bias"], np.float32).reshape(1, 8))
    w1t = np.asarray(inputs["L0_ffn_w1"], np.float32).transpose(0, 2, 1)
    w3t = np.asarray(inputs["L0_ffn_w3"], np.float32).transpose(0, 2, 1)
    w2t = np.asarray(inputs["L0_ffn_w2"], np.float32).transpose(0, 2, 1)
    sw1t = np.asarray(inputs["L0_ffn_shared_w1"], np.float32).T
    sw3t = np.asarray(inputs["L0_ffn_shared_w3"], np.float32).T
    sw2t = np.asarray(inputs["L0_ffn_shared_w2"], np.float32).T
    w1all = np.concatenate([w1t.reshape(-1, 1024), sw1t], axis=0)
    w3all = np.concatenate([w3t.reshape(-1, 1024), sw3t], axis=0)
    w2all = np.concatenate([w2t.reshape(-1, 2048), sw2t], axis=0)
    w1H = np.ascontiguousarray(
        w1all.reshape(9, 16, 128, 8, 128).transpose(2, 0, 3, 1, 4)
        .reshape(128, -1))
    w3H = np.ascontiguousarray(
        w3all.reshape(9, 16, 128, 8, 128).transpose(2, 0, 3, 1, 4)
        .reshape(128, -1))
    w2H = np.ascontiguousarray(
        w2all.reshape(9, 8, 128, 16, 128).transpose(2, 0, 3, 1, 4)
        .reshape(128, -1))
    OH8 = np.zeros((8, 1024), np.float32)
    for e in range(8):
        OH8[e, e * 128:(e + 1) * 128] = 1.0
    wintH = np.ascontiguousarray(
        np.asarray(inputs["L1_attn_W_in"], np.float32)
        .reshape(32, 128, 16, 128).transpose(3, 0, 2, 1).reshape(128, -1))
    apar = np.ascontiguousarray(
        np.asarray(inputs["L1_attn_a_param"], np.float32).reshape(128, 1))
    wup = np.asarray(inputs["L1_ffn_W_up"], np.float32).reshape(-1)
    wgt = np.asarray(inputs["L1_ffn_W_gate"], np.float32).reshape(-1)
    wdn = np.asarray(inputs["L1_ffn_W_down"], np.float32).reshape(-1)

    maps = []
    for r in range(8):
        b, c = divmod(r, 4)
        t0 = c * TLOC
        maps.append(dict(
            xloc=np.ascontiguousarray(x[b, t0:t0 + TLOC]),
            xb=np.ascontiguousarray(x[b]),
            cosT=cosT, sinT=sinT,
            cosl=np.ascontiguousarray(cosT[:, t0:t0 + TLOC]),
            sinl=np.ascontiguousarray(sinT[:, t0:t0 + TLOC]),
            wat=wat, watqH=watqH, wprojH=wprojH, ident=ident, PT=PT,
            gateT=gateT, bias8=bias8,
            w1H=w1H, w3H=w3H, w2H=w2H, OH8=OH8,
            wintH=wintH, apar=apar,
            wupf=np.ascontiguousarray(wup[r * 2097152:(r + 1) * 2097152]
                                      .reshape(128, 16384)),
            wgtf=np.ascontiguousarray(wgt[r * 2097152:(r + 1) * 2097152]
                                      .reshape(128, 16384)),
            wdnf=np.ascontiguousarray(wdn[r * 2097152:(r + 1) * 2097152]
                                      .reshape(128, 16384)),
        ))
    return maps


def make_inputs_b(inputs, results_a):
    wintH = np.ascontiguousarray(
        np.asarray(inputs["L1_attn_W_in"], np.float32)
        .reshape(32, 128, 16, 128).transpose(3, 0, 2, 1).reshape(128, -1))
    apar = np.ascontiguousarray(
        np.asarray(inputs["L1_attn_a_param"], np.float32).reshape(128, 1))
    wupH = np.ascontiguousarray(
        np.asarray(inputs["L1_ffn_W_up"], np.float32)
        .reshape(32, 256, 16, 128).transpose(3, 0, 2, 1).reshape(128, -1))
    wgtH = np.ascontiguousarray(
        np.asarray(inputs["L1_ffn_W_gate"], np.float32)
        .reshape(32, 256, 16, 128).transpose(3, 0, 2, 1).reshape(128, -1))
    wdnT = np.ascontiguousarray(np.asarray(inputs["L1_ffn_W_down"], np.float32).T)
    absps = np.ascontiguousarray(
        np.stack([results_a[r]["absp"][0] for r in range(8)]).astype(np.float32))
    maps = []
    for r in range(8):
        b, c = divmod(r, 4)
        carA = np.ones((128, 3 * NFC), np.float32)
        carB = np.zeros((128, 3 * NFC), np.float32)
        for s in range(3):
            src_chunk = c - 3 + s
            if src_chunk >= 0:
                rr = b * 4 + src_chunk
                carA[:, s * NFC:(s + 1) * NFC] = results_a[rr]["carA"]
                carB[:, s * NFC:(s + 1) * NFC] = results_a[rr]["carB"]
        maps.append(dict(
            xn2T=np.ascontiguousarray(results_a[r]["xn2T"]),
            carApre=carA, carBpre=carB, absps=absps,
            wintH=wintH, apar=apar, wupH=wupH, wgtH=wgtH, wdnT=wdnT,
        ))
    return maps


def fm_to_tok(arr):
    return arr.reshape(128, NFC, TLOC).transpose(2, 1, 0).reshape(TLOC, DMODEL)


def kernel(**inputs):
    maps_a = make_inputs_a(inputs)
    nc_a, _ = build_launch_a()
    res_a = run_bass_kernel_spmd(nc_a, maps_a, core_ids=list(range(8)))
    maps_b = make_inputs_b(inputs, res_a.results)
    nc_b, _ = build_launch_b()
    res_b = run_bass_kernel_spmd(nc_b, maps_b, core_ids=list(range(8)))
    out = np.zeros((2, TFULL, DMODEL), np.float32)
    for r in range(8):
        b, c = divmod(r, 4)
        out[b, c * TLOC:(c + 1) * TLOC] = fm_to_tok(res_b.results[r]["yT"])
    return out


# version 09
# speedup vs baseline: 1.1797x; 1.0233x over previous
import contextlib
import math
import numpy as np
import ml_dtypes
import concourse.bass as bass
import concourse.mybir as mybir
from concourse.bass_utils import run_bass_kernel_spmd

F32 = mybir.dt.float32
BF16 = mybir.dt.bfloat16
I8 = mybir.dt.int8
AF = mybir.ActivationFunctionType
ALU = mybir.AluOpType
AX = mybir.AxisListType

ENGINES = ("sync", "gpsimd", "scalar", "vector", "tensor")
NHEAD = 16
DMODEL = 2048
TLOC = 1024
TFULL = 4096
NFC = 16
SCALE = 1.0 / math.sqrt(128.0)


class Prog:

    def __init__(self, nc, es):
        self.nc = nc
        self.es = es
        self.ops = {e: [] for e in ENGINES}
        self.sems = {}
        self.counts = {}
        self.owner = {}

    def sem(self, name, owner=""):
        self.sems[name] = self.es.enter_context(self.nc.semaphore(name))
        self.counts[name] = 0
        self.owner[name] = owner
        return name

    def now(self, *names):
        return [(n, self.counts[n]) for n in names]

    def on(self, eng, fn, waits=(), inc=None, inc_by=1):
        self.ops[eng].append((fn, list(waits), inc, inc_by))
        if inc is not None:
            self.counts[inc] += inc_by
        return self.counts.get(inc)

    def dma(self, eng, out, in_, sem, waits=()):
        return self.on(eng, lambda e, o=out, i=in_: e.dma_start(out=o, in_=i),
                       waits, inc=sem, inc_by=16)

    def emit(self):
        self.es.enter_context(self._emit_ctx())

    @contextlib.contextmanager
    def _emit_ctx(self):
        with self.nc.Block() as block:
            for ename in ENGINES:
                ops = self.ops[ename]
                if not ops:
                    continue

                own = None
                for s, o in self.owner.items():
                    if o == ename:
                        own = s
                auto_sync = ename in ("vector", "scalar", "gpsimd")

                def body(eng, ops=ops, ename=ename, own=own, auto_sync=auto_sync):
                    waited = {}
                    own_count = 0
                    for fn, waits, inc, inc_by in ops:
                        if auto_sync and own_count > waited.get(own, 0):
                            eng.wait_ge(self.sems[own], own_count)
                            waited[own] = own_count
                        for s, v in waits:
                            if v <= 0:
                                continue
                            if self.owner[s] == ename:
                                continue
                            if waited.get(s, -1) >= v:
                                continue
                            eng.wait_ge(self.sems[s], v)
                            waited[s] = v
                        instr = fn(eng)
                        if inc is not None:
                            instr.then_inc(self.sems[inc], inc_by)
                            if inc == own:
                                own_count += inc_by

                getattr(block, ename)(body)
            yield


def bc16(ap_f32):
    return ap_f32.bitcast(BF16)



def build_launch_a(debug=(), upto="all"):
    nc = bass.Bass("TRN2", target_bir_lowering=False)
    d = {}

    def inp(name, shape, dt=F32):
        d[name] = nc.dram_tensor(name, shape, dt, kind="ExternalInput").ap()
        return d[name]

    def outp(name, shape, dt=F32):
        d[name] = nc.dram_tensor(name, shape, dt, kind="ExternalOutput").ap()
        return d[name]

    inp("xloc", [TLOC, DMODEL])
    inp("xb", [TFULL, DMODEL])
    inp("cosT", [128, TFULL])
    inp("sinT", [128, TFULL])
    inp("cosl", [128, TLOC])
    inp("sinl", [128, TLOC])
    inp("wat", [DMODEL, 2304])
    inp("watqH", [128, 16 * 2048])
    inp("wprojH", [128, 16 * 2048])
    inp("ident", [128, 128])
    inp("PT", [128, 128])
    inp("gateT", [DMODEL, 8])
    inp("bias8", [1, 8])
    inp("w1H", [128, 9 * 8 * 2048])
    inp("w3H", [128, 9 * 8 * 2048])
    inp("w2H", [128, 9 * 16 * 1024])
    inp("OH8", [8, 1024])
    inp("wintH", [128, 32 * 2048])
    inp("apar", [128, 1])
    inp("wupf", [128, 16384])
    inp("wgtf", [128, 16384])
    inp("wdnf", [128, 16384])

    outp("xn2T", [128, NFC * TLOC])
    outp("carA", [128, NFC])
    outp("carB", [128, NFC])
    outp("absp", [1, 4])
    for t in debug:
        outp("dbg_" + t, [128, NFC * TLOC])

    es = contextlib.ExitStack()
    with es:
        p = Prog(nc, es)

        def sb(name, shape, dt):
            return es.enter_context(nc.sbuf_tensor(name, shape, dt))

        BX = sb("BX", [128, NFC * TLOC], F32)
        BB = sb("BB", [128, NFC * TLOC], F32)
        BC = sb("BC", [128, NFC * TLOC], BF16)

        xtm = [BB[:, 0:2048], BB[:, 2048:4096]]
        xq = [bc16(BB[:, 4096:5120]), bc16(BB[:, 5120:6144])]
        coslb = bc16(BB[:, 4096:4608])
        sinlb = bc16(BB[:, 4608:5120])
        Kbf = bc16(BB[:, 6144:8192])

        def ybf(h):
            off = 8192 + h * 512
            return bc16(BB[:, off:off + 512])

        hbuf = bc16(BX[:, 0:4096])
        w13 = [bc16(BX[:, 4096 + i * 1024: 4096 + (i + 1) * 1024])
               for i in range(2)]
        w13b = [bc16(BX[:, 6144 + i * 1024: 6144 + (i + 1) * 1024])
                for i in range(2)]
        w2s = [bc16(BX[:, 8192 + i * 512: 8192 + (i + 1) * 512])
               for i in range(2)]
        smb = BX[:, 9216:10240]
        sm_all = BX[:, 11264:11328]
        u_tm = BX[:, 11328:11336]
        m_tm = BX[:, 11336:11344]
        v12 = BX[:, 11344:11346]
        biasb = BX[:, 11352:11360]
        gx4 = BX[:, 0:4096]
        la4 = BX[:, 4096:8192]
        a4 = BX[:, 8192:12288]
        b4 = BX[:, 12288:16384]

        cosq = sb("cosq", [128, TLOC], BF16)
        sinq = sb("sinq", [128, TLOC], BF16)
        Vbf = sb("Vbf", [128, TFULL], BF16)
        mrsb_v = Vbf[:, 0:2048].bitcast(F32)
        wuni = sb("wuni", [128, 4096], BF16)
        wkv = wuni[:, 0:4096]
        wq = [wuni[:, 0:2048], wuni[:, 2048:4096]]
        identt = sb("identt", [128, 128], F32)
        PTb = sb("PTb", [128, 128], BF16)
        qbf = sb("qbf", [128, TLOC], BF16)
        kq_s = sb("kq_s", [128, TLOC], F32)
        carA_sb = kq_s[:, 0:16]
        carB_sb = kq_s[:, 16:32]
        absacc = BX[:, 15488:15492]
        absp_sb = BX[0:1, 15492:15496]
        absl2 = [BX[:, 11360 + i * 2048: 11360 + (i + 1) * 2048]
                 for i in range(2)]
        SC2 = sb("SC2", [128, 512], F32)
        kq_s2 = SC2[:, :]
        sqs = bc16(SC2[:, :])
        sqs_rd = {"T": 0}
        pbf = [bc16(BB[:, i * 512:(i + 1) * 512]) for i in range(4)]
        ones_col_bf = sb("ones_col_bf", [128, 1], BF16)
        ones_row_f = sb("ones_row_f", [1, 128], F32)
        km = sb("km", [1, 1], F32)
        qm = sb("qm", [1, 1], F32)
        negM = sb("negM", [1, 1], F32)
        negMb = sb("negMb", [128, 1], F32)
        rowA = sb("rowA", [1, TLOC], F32)
        rowB = sb("rowB", [1, TLOC], F32)
        rowC = sb("rowC", [1, TLOC], F32)
        rb = sb("rb", [128, TLOC], F32)
        mrsb = None

        gatesb = sb("gatesb", [128, 128], BF16)
        bias8_sb = sb("bias8_sb", [1, 8], F32)
        apar_sb = sb("apar_sb", [128, 1], F32)
        ones_col_f = sb("ones_col_f", [128, 1], F32)
        afneg8 = sb("afneg8", [128, 1], F32)
        OH8_sb = Vbf[:, 2048:4096].bitcast(F32)

        def psum(name):
            return es.enter_context(nc.psum_tensor(name, [128, TLOC], F32))

        PAp, PBp, PCp, PDp = psum("PA"), psum("PB"), psum("PC"), psum("PD")

        T = p.sem("T", "tensor")
        V = p.sem("V", "vector")
        S = p.sem("S", "scalar")
        G = p.sem("G", "gpsimd")
        CONST = p.sem("CONST")
        ABS = p.sem("ABS")
        DS0 = p.sem("DS0")
        DS1 = p.sem("DS1")
        CQ = p.sem("CQ")
        WQ0 = p.sem("WQ0")
        WQ1 = p.sem("WQ1")
        DO = p.sem("DO")

        p.dma("gpsimd", PTb[:, :], d["PT"], CONST)
        p.dma("gpsimd", identt[:, :], d["ident"], CONST)
        p.on("gpsimd", lambda e: e.memset(ones_col_bf[:, :], 1.0), inc=G)
        p.on("gpsimd", lambda e: e.memset(ones_row_f[:, :], 1.0), inc=G)
        p.on("gpsimd", lambda e: e.memset(km[:, :], 0.0), inc=G)
        wkv_src = d["wat"][:, 2048:2304].rearrange("(fc p) o -> p fc o", p=128)
        p.dma("gpsimd", wkv.rearrange("p (fc o) -> p fc o", fc=NFC), wkv_src, CONST)
        p.dma("gpsimd", gatesb[:, :].rearrange("p (fc o) -> p fc o", fc=NFC),
              d["gateT"].rearrange("(fc p) o -> p fc o", p=128), CONST)
        p.dma("gpsimd", bias8_sb[:, :], d["bias8"], CONST)
        p.dma("gpsimd", apar_sb[:, :], d["apar"], CONST)
        p.on("gpsimd", lambda e: e.memset(ones_col_f[:, :], 1.0), inc=G)
        consts = p.now(CONST, G)

        tr_evacs = []

        def transpose_tile(stage, dests, ld):
            for g in range(4):
                half = g % 2
                wv = ([("V", tr_evacs[-2])] if len(tr_evacs) >= 2 else p.now(V))
                for j in range(4):
                    fc = g * 4 + j
                    p.on("tensor", lambda e, s=stage, f=fc, hf=half, c=j * 128:
                         e.transpose(PAp[:, hf * 512 + c: hf * 512 + c + 128],
                                     s[:, f * 128:(f + 1) * 128], identt[:, :]),
                         waits=(ld + wv) if j == 0 else (),
                         inc=T if j == 3 else None)
                tw = p.now(T)
                src3 = PAp[:, half * 512:(half + 1) * 512].rearrange(
                    "p (f c) -> p f c", f=4)
                c = None
                for dst in dests:
                    c = p.on("vector", lambda e, o=dst(g), s=src3:
                             e.tensor_copy(o, s), waits=tw, inc=V)
                tr_evacs.append(c)

        def view3(buf, fc0, t0, span=TLOC):
            return buf.rearrange("p (f t) -> p f t", f=buf.shape[1] // span)[
                :, fc0:fc0 + 4, t0:t0 + 128]

        for tt in range(8):
            stage = xtm[tt % 2]
            dsem = DS0 if tt % 2 == 0 else DS1
            p.dma("sync", stage,
                  d["xloc"][tt * 128:(tt + 1) * 128, :], dsem,
                  waits=p.now(T) + consts)
            ld = p.now(dsem)
            transpose_tile(
                stage,
                [lambda g, tt=tt: view3(BX[:, :], g * 4, tt * 128),
                 lambda g, tt=tt: view3(BC[:, :], g * 4, tt * 128)],
                ld)

        for q in range(4):
            p.dma("gpsimd", cosq[:, :], d["cosT"][:, q * 1024:(q + 1) * 1024], CQ,
                  waits=p.now(V))
            p.dma("gpsimd", sinq[:, :], d["sinT"][:, q * 1024:(q + 1) * 1024], CQ)
            cs_ready = p.now(CQ)
            v_evacs = []
            for tt in range(8):
                stage = xtm[tt % 2]
                dsem = DS0 if tt % 2 == 0 else DS1
                p.dma("sync", stage,
                      d["xb"][q * 1024 + tt * 128: q * 1024 + (tt + 1) * 128, :],
                      dsem, waits=p.now(T))
                ld = p.now(dsem)
                xqb = xq[tt % 2]
                transpose_tile(
                    stage, [lambda g, x=xqb: view3(x, g * 4, 0, span=128)], ld)
                xq_ready = p.now(V)
                for fc in range(NFC):
                    p.on("tensor", lambda e, f=fc, t=tt, x=xqb:
                         e.matmul(PCp[:, t * 128:(t + 1) * 128],
                                  wkv[:, f * 256: f * 256 + 128],
                                  x[:, f * 128:(f + 1) * 128],
                                  start=(f == 0), stop=(f == NFC - 1)),
                         waits=(xq_ready + consts) if fc == 0 else (),
                         inc=T if fc == NFC - 1 else None)
                half = tt % 2
                wv = ([("V", v_evacs[tt - 2])] if tt >= 2 else p.now(V))
                for fc in range(NFC):
                    p.on("tensor", lambda e, t=tt, f=fc, hf=half, x=xqb:
                         e.matmul(PBp[:, hf * 512: hf * 512 + 128],
                                  x[:, f * 128:(f + 1) * 128],
                                  wkv[:, f * 256 + 128: f * 256 + 256],
                                  start=(f == 0), stop=(f == NFC - 1)),
                         waits=wv if fc == 0 else (),
                         inc=T if fc == NFC - 1 else None)
                vmm = p.now(T)
                c = p.on("vector", lambda e, t=tt, q=q, hf=half:
                         e.tensor_copy(Vbf[:, (q * 8 + t) * 128:(q * 8 + t + 1) * 128],
                                       PBp[:, hf * 512: hf * 512 + 128]),
                         waits=vmm, inc=V)
                v_evacs.append(c)
            kmm = p.now(T)
            p.on("vector", lambda e: e.tensor_copy(qbf[:, :], PCp[:, :]),
                 waits=kmm + cs_ready, inc=V)
            p.on("vector", lambda e:
                 e.tensor_tensor(kq_s[:, :], PCp[:, :], cosq[:, :], ALU.mult),
                 inc=V)
            kraw = p.now(V)
            for nh in range(2):
                p.on("tensor", lambda e, n=nh:
                     e.matmul(PBp[:, n * 512:(n + 1) * 512], PTb[:, :],
                              qbf[:, n * 512:(n + 1) * 512], start=True, stop=True),
                     waits=kraw if nh == 0 else (), inc=T)
                rmm = p.now(T)
                p.on("vector", lambda e, n=nh:
                     e.tensor_tensor(kq_s2[:, :], PBp[:, n * 512:(n + 1) * 512],
                                     sinq[:, n * 512:(n + 1) * 512], ALU.mult),
                     waits=rmm, inc=V)
                p.on("vector", lambda e, n=nh, q=q:
                     e.tensor_add(Kbf[:, q * 1024 + n * 512: q * 1024 + (n + 1) * 512],
                                  kq_s[:, n * 512:(n + 1) * 512], kq_s2[:, :]),
                     inc=V)
            kq_done = p.now(V)
            p.on("scalar", lambda e, q=q:
                 e.activation(sqs, Kbf[:, q * 1024:(q + 1) * 1024], AF.Square),
                 waits=kq_done + [("T", sqs_rd["T"])], inc=S)
            sq_done = p.now(S)
            for nh in range(2):
                p.on("tensor", lambda e, n=nh:
                     e.matmul(PDp[0:1, n * 512:(n + 1) * 512], ones_col_bf[:, :],
                              sqs[:, n * 512:(n + 1) * 512], start=True, stop=True),
                     waits=(sq_done + p.now(V)) if nh == 0 else (),
                     inc=T if nh == 1 else None)
            ksum = p.now(T)
            sqs_rd["T"] = ksum[0][1]
            p.on("vector", lambda e: e.reduce_max(rowA[:, 0:1], PDp[0:1, :],
                                                  axis=AX.X),
                 waits=ksum, inc=V)
            p.on("vector", lambda e: e.tensor_max(km[:, :], km[:, :], rowA[:, 0:1]),
                 inc=V)

        if upto == "kv":
            p.on("sync", lambda e: e.nop(), waits=p.now(V, T, S))
            p.emit()
            return nc, d

        p.dma("gpsimd", coslb, d["cosl"], CQ, waits=p.now(T))
        p.dma("gpsimd", sinlb, d["sinl"], CQ)
        csl_ready = p.now(CQ)

        def wq_src(h):
            return d["watqH"][:, h * 2048:(h + 1) * 2048]

        p.dma("gpsimd", wq[0], wq_src(0), WQ0, waits=p.now(T))

        for h in range(NHEAD):
            wqb = wq[h % 2]
            wq_ready = p.now(WQ0 if h % 2 == 0 else WQ1)
            if h + 1 < NHEAD:
                p.dma("gpsimd", wq[(h + 1) % 2], wq_src(h + 1),
                      WQ0 if (h + 1) % 2 == 0 else WQ1,
                      waits=p.now(T))
            bxc_ready = p.now(V)
            for fc in range(NFC):
                for nh in range(2):
                    p.on("tensor", lambda e, f=fc, n=nh, w=wqb:
                         e.matmul(PCp[:, n * 512:(n + 1) * 512],
                                  w[:, f * 128:(f + 1) * 128],
                                  BC[:, f * TLOC + n * 512: f * TLOC + (n + 1) * 512],
                                  start=(f == 0), stop=(f == NFC - 1)),
                         waits=(wq_ready + bxc_ready + csl_ready)
                         if (fc == 0 and nh == 0) else (),
                         inc=T if (fc == NFC - 1 and nh == 1) else None)
            qmm = p.now(T)
            p.on("vector", lambda e: e.tensor_copy(qbf[:, :], PCp[:, :]),
                 waits=qmm, inc=V)
            p.on("vector", lambda e:
                 e.tensor_tensor(kq_s[:, :], PCp[:, :], coslb, ALU.mult),
                 inc=V)
            qraw = p.now(V)
            for nh in range(2):
                p.on("tensor", lambda e, n=nh:
                     e.matmul(PBp[:, n * 512:(n + 1) * 512], PTb[:, :],
                              qbf[:, n * 512:(n + 1) * 512], start=True, stop=True),
                     waits=qraw if nh == 0 else (), inc=T)
                rmm = p.now(T)
                p.on("vector", lambda e, n=nh:
                     e.tensor_tensor(kq_s2[:, :], PBp[:, n * 512:(n + 1) * 512],
                                     sinlb[:, n * 512:(n + 1) * 512], ALU.mult),
                     waits=rmm, inc=V)
                p.on("vector", lambda e, n=nh:
                     e.tensor_add(qbf[:, n * 512:(n + 1) * 512],
                                  kq_s[:, n * 512:(n + 1) * 512], kq_s2[:, :]),
                     inc=V)
            qrope = p.now(V)
            p.on("scalar", lambda e: e.activation(sqs, qbf[:, :], AF.Square),
                 waits=qrope + [("T", sqs_rd["T"])], inc=S)
            sq_done = p.now(S)
            for nh in range(2):
                p.on("tensor", lambda e, n=nh:
                     e.matmul(PDp[0:1, n * 512:(n + 1) * 512], ones_col_bf[:, :],
                              sqs[:, n * 512:(n + 1) * 512], start=True, stop=True),
                     waits=(sq_done + p.now(V)) if nh == 0 else (),
                     inc=T if nh == 1 else None)
            qsum = p.now(T)
            sqs_rd["T"] = qsum[0][1]
            p.on("vector", lambda e: e.reduce_max(qm[:, :], PDp[0:1, :], axis=AX.X),
                 waits=qsum, inc=V)
            p.on("vector", lambda e:
                 e.tensor_tensor(negM[:, :], qm[:, :], km[:, :], ALU.add), inc=V)
            p.on("vector", lambda e:
                 e.tensor_scalar(negM[:, :], negM[:, :], -0.5 * SCALE, None,
                                 ALU.mult), inc=V)
            nm = p.now(V)
            p.on("tensor", lambda e:
                 e.matmul(PDp[:, 0:1], ones_row_f[:, :], negM[:, :],
                          start=True, stop=True), waits=nm, inc=T)
            nmm = p.now(T)
            p.on("vector", lambda e: e.tensor_copy(negMb[:, :], PDp[:, 0:1]),
                 waits=nmm, inc=V)
            negm_ready = p.now(V)

            exp_counts = []
            smm_counts = []

            def plan_s(mc):
                PS_ = PAp if mc % 2 == 0 else PCp
                ws = ([("S", exp_counts[mc - 2])] if mc >= 2 else list(p.now(S)))
                if mc == 0:
                    ws += negm_ready + p.now(V)
                for nh in range(2):
                    p.on("tensor", lambda e, m=mc, n=nh, PS=PS_:
                         e.matmul(PS[:, n * 512:(n + 1) * 512],
                                  Kbf[:, m * 128:(m + 1) * 128],
                                  qbf[:, n * 512:(n + 1) * 512],
                                  start=True, stop=True),
                         waits=ws if nh == 0 else (),
                         inc=T if nh == 1 else None)
                smm_counts.append(p.counts["T"])

            burst_Ts = [0]
            plan_s(0)
            for mc in range(32):
                buf = pbf[mc % 4]
                PS_ = PAp if mc % 2 == 0 else PCp
                ec = p.on("scalar", lambda e, b=buf, PS=PS_:
                          e.activation(b, PS[:, :],
                                       AF.Exp, bias=negMb[:, 0:1], scale=SCALE),
                          waits=[("T", max(smm_counts[mc],
                                           burst_Ts[min(mc // 4, len(burst_Ts) - 1)]))],
                          inc=S)
                exp_counts.append(ec)
                edone = p.now(S)
                if mc + 1 < 32:
                    plan_s(mc + 1)
                for nh in range(2):
                    p.on("tensor", lambda e, m=mc, n=nh, b=buf:
                         e.matmul(PBp[:, n * 512:(n + 1) * 512],
                                  Vbf[:, m * 128:(m + 1) * 128],
                                  b[:, n * 512:(n + 1) * 512],
                                  start=(m == 0), stop=(m == 31)),
                         waits=(edone if nh == 0 else []), inc=None)
                if mc % 4 == 3:
                    for q4 in range(4):
                        for nh in range(2):
                            p.on("tensor", lambda e, m4=mc - 3 + q4, n=nh,
                                 b=pbf[(mc - 3 + q4) % 4]:
                                 e.matmul(PDp[0:1, n * 512:(n + 1) * 512],
                                          ones_col_bf[:, :],
                                          b[:, n * 512:(n + 1) * 512],
                                          start=(m4 == 0), stop=(m4 == 31)),
                                 inc=T if (q4 == 3 and nh == 1) else None)
                    burst_Ts.append(p.counts["T"])
            av = p.now(T)
            p.on("vector", lambda e: e.tensor_copy(rowA[:, :], PDp[0:1, :]),
                 waits=av, inc=V)
            sr = p.now(V)
            p.on("scalar", lambda e:
                 e.activation(rowB[:, :], rowA[:, :], AF.Ln), waits=sr, inc=S)
            p.on("scalar", lambda e:
                 e.activation(rowB[:, :], rowB[:, :], AF.Exp, scale=-1.0), inc=S)
            rr = p.now(S)
            for nh in range(2):
                p.on("tensor", lambda e, n=nh:
                     e.matmul(PDp[:, n * 512:(n + 1) * 512], ones_row_f[:, :],
                              rowB[0:1, n * 512:(n + 1) * 512],
                              start=True, stop=True),
                     waits=rr if nh == 0 else (), inc=T if nh == 1 else None)
            rbmm = p.now(T)
            p.on("scalar", lambda e: e.copy(rb[:, :], PDp[:, :]), waits=rbmm, inc=S)
            rbdone = p.now(S)
            p.on("vector", lambda e, h=h:
                 e.tensor_tensor(ybf(h), PBp[:, :], rb[:, :], ALU.mult),
                 waits=rbdone, inc=V)

        def wpr_src(oc):
            return d["wprojH"][:, oc * 2048:(oc + 1) * 2048]

        p.dma("gpsimd", wq[0], wpr_src(0), WQ0, waits=p.now(T))
        for oc in range(NFC):
            wp = wq[oc % 2]
            wp_ready = p.now(WQ0 if oc % 2 == 0 else WQ1)
            if oc + 1 < NFC:
                p.dma("gpsimd", wq[(oc + 1) % 2], wpr_src(oc + 1),
                      WQ0 if (oc + 1) % 2 == 0 else WQ1,
                      waits=p.now(T))
            yready = p.now(V)
            for fc in range(NFC):
                for nh in range(2):
                    p.on("tensor", lambda e, f=fc, n=nh, w=wp:
                         e.matmul(PCp[:, n * 512:(n + 1) * 512],
                                  w[:, f * 128:(f + 1) * 128],
                                  ybf(f)[:, n * 512:(n + 1) * 512],
                                  start=(f == 0), stop=(f == NFC - 1)),
                         waits=(wp_ready + yready) if (fc == 0 and nh == 0) else (),
                         inc=T if (fc == NFC - 1 and nh == 1) else None)
            pmm = p.now(T)
            p.on("vector", lambda e, o=oc:
                 e.tensor_add(BX[:, o * TLOC:(o + 1) * TLOC],
                              PCp[:, :], BX[:, o * TLOC:(o + 1) * TLOC]),
                 waits=pmm, inc=V)
        r1_done = p.now(V)

        def layernorm(src_f32, dst_f32, dst_bf, waits):
            for fc in range(NFC):
                p.on("vector", lambda e, f=fc:
                     e.tensor_copy(dst_bf[:, f * TLOC:(f + 1) * TLOC],
                                   src_f32[:, f * TLOC:(f + 1) * TLOC]),
                     waits=waits if fc == 0 else (), inc=V)
            cast_done = p.now(V)
            for fc in range(NFC):
                for nh in range(2):
                    p.on("tensor", lambda e, f=fc, n=nh:
                         e.matmul(PDp[0:1, n * 512:(n + 1) * 512],
                                  ones_col_bf[:, :],
                                  dst_bf[:, f * TLOC + n * 512:
                                         f * TLOC + (n + 1) * 512],
                                  start=(f == 0), stop=(f == NFC - 1)),
                         waits=cast_done if (fc == 0 and nh == 0) else (),
                         inc=T if (fc == NFC - 1 and nh == 1) else None)
            for fc in range(NFC):
                p.on("scalar", lambda e, f=fc:
                     e.activation(sqs, dst_bf[:, f * TLOC:(f + 1) * TLOC],
                                  AF.Square),
                     waits=list(cast_done if fc == 0 else [])
                     + [("T", sqs_rd["T"])], inc=S)
                sqd = p.now(S)
                for nh in range(2):
                    p.on("tensor", lambda e, f=fc, n=nh:
                         e.matmul(PAp[0:1, n * 512:(n + 1) * 512],
                                  ones_col_bf[:, :],
                                  sqs[:, n * 512:(n + 1) * 512],
                                  start=(f == 0), stop=(f == NFC - 1)),
                         waits=(sqd + p.now(V)) if nh == 0 else (),
                         inc=T if nh == 1 else None)
                sqs_rd["T"] = p.now(T)[0][1]
            stats_mm = p.now(T)
            p.on("vector", lambda e:
                 e.tensor_scalar(rowA[:, :], PDp[0:1, :], 1.0 / DMODEL, None,
                                 ALU.mult), waits=stats_mm, inc=V)
            p.on("vector", lambda e:
                 e.tensor_tensor(rowB[:, :], rowA[:, :], rowA[:, :], ALU.mult),
                 inc=V)
            p.on("vector", lambda e:
                 e.scalar_tensor_tensor(rowC[:, :], PAp[0:1, :], 1.0 / DMODEL,
                                        rowB[:, :], ALU.mult, ALU.subtract),
                 inc=V)
            p.on("vector", lambda e:
                 e.tensor_scalar(rowC[:, :], rowC[:, :], 1e-5, None, ALU.add),
                 inc=V)
            vd = p.now(V)
            p.on("scalar", lambda e:
                 e.activation(rowC[:, :], rowC[:, :], AF.Ln),
                 waits=vd, inc=S)
            p.on("scalar", lambda e:
                 e.activation(rowC[:, :], rowC[:, :], AF.Exp, scale=-0.5), inc=S)
            rstd_done = p.now(S)
            p.on("vector", lambda e:
                 e.tensor_tensor(rowB[:, :], rowA[:, :], rowC[:, :], ALU.mult),
                 waits=rstd_done, inc=V)
            mrs_done = p.now(V)
            for nh in range(2):
                p.on("tensor", lambda e, n=nh:
                     e.matmul(PDp[:, n * 512:(n + 1) * 512], ones_row_f[:, :],
                              rowC[0:1, n * 512:(n + 1) * 512], start=True,
                              stop=True),
                     waits=mrs_done if nh == 0 else (), inc=None)
                p.on("tensor", lambda e, n=nh:
                     e.matmul(PAp[:, n * 512:(n + 1) * 512], ones_row_f[:, :],
                              rowB[0:1, n * 512:(n + 1) * 512], start=True,
                              stop=True),
                     inc=T if nh == 1 else None)
            bc_mm = p.now(T)
            p.on("scalar", lambda e: e.copy(rb[:, :], PDp[:, :]),
                 waits=bc_mm, inc=S)
            p.on("scalar", lambda e: e.copy(mrsb_v[:, :], PAp[:, :]), inc=S)
            bc_done = p.now(S)
            for fc in range(NFC):
                p.on("vector", lambda e, f=fc:
                     e.tensor_tensor(kq_s[:, :], src_f32[:, f * TLOC:(f + 1) * TLOC],
                                     rb[:, :], ALU.mult),
                     waits=bc_done if fc == 0 else (), inc=V)
                p.on("vector", lambda e, f=fc:
                     e.tensor_sub(dst_f32[:, f * TLOC:(f + 1) * TLOC],
                                  kq_s[:, :], mrsb_v[:, :]), inc=V)
                p.on("vector", lambda e, f=fc:
                     e.tensor_copy(dst_bf[:, f * TLOC:(f + 1) * TLOC],
                                   dst_f32[:, f * TLOC:(f + 1) * TLOC]), inc=V)

        layernorm(BX, BX, BC, r1_done)
        xn1_done = p.now(V)

        if "xn1" in debug:
            p.dma("sync", d["dbg_xn1"], BX[:, :], DO, waits=p.now(V))
        if upto == "xn1":
            p.on("sync", lambda e: e.nop(), waits=p.now(DO, V, T, S))
            p.emit()
            return nc, d

        for fc in range(NFC):
            p.on("vector", lambda e, f=fc:
                 e.tensor_copy(BB[:, f * TLOC:(f + 1) * TLOC],
                               BX[:, f * TLOC:(f + 1) * TLOC]),
                 waits=xn1_done if fc == 0 else (), inc=V)
        acc_init = p.now(V)

        p.dma("gpsimd", OH8_sb[0:8, :], d["OH8"], CONST, waits=p.now(T, V))
        oh_ready = p.now(CONST)
        p.on("tensor", lambda e:
             e.matmul(PDp[:, 0:8], ones_row_f[:, :], bias8_sb[0:1, :],
                      start=True, stop=True), waits=p.now(S, V), inc=T)
        bb_mm = p.now(T)
        p.on("vector", lambda e: e.tensor_copy(biasb, PDp[:, 0:8]),
             waits=bb_mm, inc=V)
        biasb_done = p.now(V)

        sig_counts = []
        for ts in range(8):
            col = (ts % 2) * 512
            wv = ([("S", sig_counts[ts - 2])] if ts >= 2
                  else list(p.now(S)) + list(biasb_done))
            for fc in range(NFC):
                p.on("tensor", lambda e, t=ts, f=fc, c=col:
                     e.matmul(PDp[:, c:c + 8], BC[:, f * TLOC + t * 128:
                                                  f * TLOC + (t + 1) * 128],
                              gatesb[:, f * 8:(f + 1) * 8],
                              start=(f == 0), stop=(f == NFC - 1)),
                     waits=wv if fc == 0 else (), inc=T if fc == NFC - 1 else None)
            smm = p.now(T)
            sc_ = p.on("scalar", lambda e, t=ts, c=col:
                       e.activation(sm_all[:, t * 8:(t + 1) * 8], PDp[:, c:c + 8],
                                    AF.Sigmoid), waits=smm, inc=S)
            sig_counts.append(sc_)
            sgd = p.now(S)
            p.on("vector", lambda e, t=ts:
                 e.tensor_add(u_tm, sm_all[:, t * 8:(t + 1) * 8], biasb),
                 waits=sgd + biasb_done, inc=V)
            p.on("vector", lambda e:
                 e.reduce_max(v12[:, 0:1], u_tm, axis=AX.X), inc=V)
            p.on("vector", lambda e:
                 e.tensor_scalar(m_tm, u_tm, v12[:, 0:1], None, ALU.is_equal),
                 inc=V)
            p.on("vector", lambda e:
                 e.scalar_tensor_tensor(u_tm, m_tm, -1e9, u_tm,
                                        ALU.mult, ALU.add), inc=V)
            p.on("vector", lambda e:
                 e.reduce_max(v12[:, 1:2], u_tm, axis=AX.X), inc=V)
            p.on("vector", lambda e:
                 e.tensor_scalar(u_tm, u_tm, v12[:, 1:2], None, ALU.is_ge),
                 inc=V)
            p.on("vector", lambda e:
                 e.tensor_add(m_tm, m_tm, u_tm), inc=V)
            p.on("vector", lambda e, t=ts:
                 e.tensor_mul(sm_all[:, t * 8:(t + 1) * 8],
                              sm_all[:, t * 8:(t + 1) * 8], m_tm), inc=V)
        sm_done = p.now(V)
        for ts in range(8):
            p.on("tensor", lambda e, t=ts:
                 e.transpose(PDp[0:8, t * 128:(t + 1) * 128],
                             sm_all[:, t * 8:(t + 1) * 8], identt[:, :]),
                 waits=sm_done if ts == 0 else (), inc=T if ts == 7 else None)
        smt_mm = p.now(T)
        p.on("vector", lambda e: e.tensor_copy(rb[0:8, :], PDp[0:8, :]),
             waits=smt_mm, inc=V)
        smt_done = p.now(V)

        M1 = [p.sem("M10"), p.sem("M11")]
        M3 = [p.sem("M30"), p.sem("M31")]
        M2 = [p.sem("M20"), p.sem("M21")]
        scratch2 = BX[:, 10240:11264]

        def w1_src(e, hc):
            return d["w1H"][:, (e * 8 + hc) * 2048:(e * 8 + hc + 1) * 2048]

        def w3_src(e, hc):
            return d["w3H"][:, (e * 8 + hc) * 2048:(e * 8 + hc + 1) * 2048]

        def w2_src(e, oc):
            return d["w2H"][:, (e * 16 + oc) * 1024:(e * 16 + oc + 1) * 1024]

        last_smb_rd = {"V": 0}
        last_w2_mm = {"T": 0}
        h_writes = {"G": 0, "V": 0}
        u_evac = {"V2": [0, 0], "S2": [0, 0], "G2": [0, 0]}

        p.dma("gpsimd", w13[0], w1_src(0, 0), M1[0], waits=acc_init)
        p.dma("gpsimd", w13b[0], w3_src(0, 0), M3[0], waits=acc_init)

        p.on("gpsimd", lambda e: e.memset(absacc[:, :], 0.0),
             waits=acc_init, inc=G)
        absg = {"n": p.now(G)[0][1]}
        abs_parts = [(m, ch) for m in range(3) for ch in range(8)]
        abs_rd = {"V": [0, 0]}

        def plan_abs(k):
            m, ch = abs_parts[k]
            i = k % 2
            dsem = DS0 if i == 0 else DS1
            nm = ("wupf", "wgtf", "wdnf")[m]
            p.dma("sync", absl2[i], d[nm][:, ch * 2048:(ch + 1) * 2048], dsem,
                  waits=[("V", abs_rd["V"][i]), ("G", absg["n"])])
            ld = p.now(dsem)
            p.on("vector", lambda e, i=i:
                 e.tensor_reduce(BX[:, 15484 + i:15485 + i], absl2[i],
                                 axis=AX.X, op=ALU.add,
                                 apply_absolute_value=True),
                 waits=ld, inc=V)
            c = p.on("vector", lambda e, m=m, i=i:
                     e.tensor_add(absacc[:, m:m + 1], absacc[:, m:m + 1],
                                  BX[:, 15484 + i:15485 + i]), inc=V)
            abs_rd["V"][i] = c

        for e in range(9):
            routed = e < 8
            for k in range(e * 3, min(e * 3 + 3, 24)):
                plan_abs(k)
            if routed:
                p.on("tensor", lambda e_, ex=e:
                     e_.matmul(PDp[:, 0:512], OH8_sb[0:8, ex * 128:(ex + 1) * 128],
                               rb[0:8, 0:512], start=True, stop=True),
                     waits=smt_done + oh_ready
                     + [("V", u_evac["V2"][1]), ("S", u_evac["S2"][1]),
                        ("G", u_evac["G2"][1])], inc=None)
                p.on("tensor", lambda e_, ex=e:
                     e_.matmul(PDp[:, 512:1024],
                               OH8_sb[0:8, ex * 128:(ex + 1) * 128],
                               rb[0:8, 512:1024], start=True, stop=True), inc=T)
                smb_mm = p.now(T)
                p.on("scalar", lambda e_: e_.copy(smb, PDp[:, :]),
                     waits=smb_mm + [("V", last_smb_rd["V"])], inc=S)
            smb_done = p.now(S)
            for hc in range(8):
                sl = (e * 8 + hc) % 2
                PU1 = PAp if hc % 2 == 0 else PCp
                PU3 = PBp if hc % 2 == 0 else PDp
                w1rd = p.now(M1[sl])
                w3rd = p.now(M3[sl])
                ne, nhc = (e, hc + 1) if hc < 7 else (e + 1, 0)
                if ne < 9:
                    nsl = (ne * 8 + nhc) % 2
                    p.dma("gpsimd", w13[nsl], w1_src(ne, nhc), M1[nsl],
                          waits=p.now(T))
                    p.dma("gpsimd", w13b[nsl], w3_src(ne, nhc), M3[nsl],
                          waits=p.now(T))
                uw = [("V", u_evac["V2"][hc % 2]), ("S", u_evac["S2"][hc % 2]),
                      ("G", u_evac["G2"][hc % 2])]
                for fc in range(NFC):
                    for nh in range(2):
                        p.on("tensor", lambda e_, f=fc, n=nh, s=sl, PU=PU1:
                             e_.matmul(PU[:, n * 512:(n + 1) * 512],
                                       w13[s][:, f * 128:(f + 1) * 128],
                                       BC[:, f * TLOC + n * 512:
                                          f * TLOC + (n + 1) * 512],
                                       start=(f == 0), stop=(f == NFC - 1)),
                             waits=(w1rd + uw)
                             if (fc == 0 and nh == 0) else (),
                             inc=T if (fc == NFC - 1 and nh == 1) else None)
                for fc in range(NFC):
                    for nh in range(2):
                        p.on("tensor", lambda e_, f=fc, n=nh, s=sl, PU=PU3:
                             e_.matmul(PU[:, n * 512:(n + 1) * 512],
                                       w13b[s][:, f * 128:(f + 1) * 128],
                                       BC[:, f * TLOC + n * 512:
                                          f * TLOC + (n + 1) * 512],
                                       start=(f == 0), stop=(f == NFC - 1)),
                             waits=(w3rd + smb_done)
                             if (fc == 0 and nh == 0) else (),
                             inc=T if (fc == NFC - 1 and nh == 1) else None)
                umm = p.now(T)
                hw = [("T", last_w2_mm["T"])]
                hprev = [("G", h_writes["G"]), ("V", h_writes["V"])]
                if routed:
                    p.on("vector", lambda e_, PU=PU1:
                         e_.tensor_tensor(kq_s[:, :], PU[:, :], smb, ALU.mult),
                         waits=umm + smb_done + hprev, inc=V)
                    kq1 = p.now(V)
                    p.on("scalar", lambda e_:
                         e_.activation(scratch2, kq_s[:, :], AF.Silu),
                         waits=kq1 + hprev, inc=S)
                    sld = p.now(S)
                    c = p.on("vector", lambda e_, PU=PU3:
                             e_.tensor_tensor(kq_s[:, :], PU[:, :], smb,
                                              ALU.mult),
                             waits=sld, inc=V)
                    t3d = p.now(V)
                    g = p.on("gpsimd", lambda e_, h=hc:
                             e_.tensor_mul(hbuf[:, h * 1024:(h + 1) * 1024],
                                           scratch2, kq_s[:, :]),
                             waits=t3d + hw, inc=G)
                    h_writes["G"] = g
                    u_evac["V2"][hc % 2] = c
                    u_evac["S2"][hc % 2] = sld[0][1]
                    u_evac["G2"][hc % 2] = g
                else:
                    p.on("scalar", lambda e_, PU=PU1:
                         e_.activation(scratch2, PU[:, :], AF.Silu),
                         waits=umm + hprev, inc=S)
                    sld = p.now(S)
                    c = p.on("vector", lambda e_, h=hc, PU=PU3:
                             e_.tensor_tensor(hbuf[:, h * 1024:(h + 1) * 1024],
                                              scratch2, PU[:, :], ALU.mult),
                             waits=sld + hw, inc=V)
                    h_writes["V"] = c
                    u_evac["V2"][hc % 2] = c
                    u_evac["S2"][hc % 2] = sld[0][1]
            h_done = [("G", h_writes["G"]), ("V", h_writes["V"])]
            w2_evac = [p.counts["V"], p.counts["V"]]
            p.dma("gpsimd", w2s[0], w2_src(e, 0), M2[0], waits=p.now(T))
            for oc in range(NFC):
                sl = oc % 2
                w2rd = p.now(M2[sl])
                if oc + 1 < NFC:
                    p.dma("gpsimd", w2s[(oc + 1) % 2], w2_src(e, oc + 1),
                          M2[(oc + 1) % 2], waits=p.now(T))
                PW = PCp if oc % 2 == 0 else PAp
                wv2 = ([("V", w2_evac[oc % 2])]
                       + (h_done + [("V", p.counts["V"])] if oc == 0 else []))
                for kc in range(8):
                    for nh in range(2):
                        p.on("tensor", lambda e_, k=kc, n=nh, s=sl, PW=PW:
                             e_.matmul(PW[:, n * 512:(n + 1) * 512],
                                       w2s[s][:, k * 128:(k + 1) * 128],
                                       hbuf[:, k * 1024 + n * 512:
                                            k * 1024 + (n + 1) * 512],
                                       start=(k == 0), stop=(k == 7)),
                             waits=(w2rd + wv2)
                             if (kc == 0 and nh == 0) else (),
                             inc=T if (kc == 7 and nh == 1) else None)
                wmm = p.now(T)
                last_w2_mm["T"] = wmm[0][1]
                c = p.on("vector", lambda e_, o=oc, PW=PW:
                         e_.scalar_tensor_tensor(BB[:, o * TLOC:(o + 1) * TLOC],
                                                 PW[:, :], 1.0,
                                                 BB[:, o * TLOC:(o + 1) * TLOC],
                                                 ALU.mult, ALU.add),
                         waits=wmm, inc=V)
                w2_evac[oc % 2] = c
            u_evac["V2"][0] = max(u_evac["V2"][0], p.counts["V"])
            u_evac["V2"][1] = max(u_evac["V2"][1], p.counts["V"])
            last_smb_rd["V"] = p.counts["V"]
        moe_done = p.now(V)

        abs_done = p.now(V)
        for m in range(3):
            p.on("tensor", lambda e, m=m:
                 e.matmul(PDp[0:1, m:m + 1], absacc[:, m:m + 1],
                          ones_col_f[:, :], start=True, stop=True),
                 waits=abs_done if m == 0 else (), inc=T if m == 2 else None)
        absmm = p.now(T)
        p.on("vector", lambda e: e.tensor_copy(absp_sb[0:1, 0:3],
                                               PDp[0:1, 0:3]),
             waits=absmm, inc=V)
        p.on("vector", lambda e: e.tensor_copy(absp_sb[0:1, 3:4],
                                               absp_sb[0:1, 0:1]), inc=V)
        p.dma("sync", d["absp"], absp_sb, ABS, waits=p.now(V))
        absdo = p.now(ABS)


        if "r2" in debug:
            p.dma("sync", d["dbg_r2"], BB[:, :], DO, waits=moe_done)

        layernorm(BB, BB, BC, moe_done)
        xn2_done = p.now(V)
        p.dma("sync", d["xn2T"], BB[:, :], DO, waits=xn2_done)
        if "xn2" in debug:
            p.dma("sync", d["dbg_xn2"], BB[:, :], DO, waits=xn2_done)
        if upto == "xn2":
            p.on("sync", lambda e: e.nop(), waits=p.now(DO, V, T, S, G))
            p.emit()
            return nc, d

        p.on("scalar", lambda e: e.activation(v12[:, 0:1], apar_sb[:, :],
                                              AF.Abs),
             waits=consts + p.now(V), inc=S)
        p.on("scalar", lambda e:
             e.activation(v12[:, 0:1], v12[:, 0:1], AF.Exp, scale=-1.0), inc=S)
        p.on("scalar", lambda e:
             e.activation(v12[:, 0:1], v12[:, 0:1], AF.Ln, bias=1.0), inc=S)
        sp_done = p.now(S)
        p.on("vector", lambda e: e.tensor_relu(v12[:, 1:2], apar_sb[:, :]),
             waits=sp_done, inc=V)
        p.on("vector", lambda e:
             e.tensor_add(afneg8[:, :], v12[:, 0:1], v12[:, 1:2]), inc=V)
        p.on("vector", lambda e:
             e.tensor_scalar(afneg8[:, :], afneg8[:, :], -8.0, None, ALU.mult),
             inc=V)
        af_done = p.now(V)
        p.on("gpsimd", lambda e: e.memset(rb[:, :], 0.0),
             waits=p.now(S, V), inc=G)
        zeros_done = p.now(G)

        def wg_src(oc):
            return d["wintH"][:, oc * 2048:(oc + 1) * 2048]

        gsl4 = [wuni[:, 0:2048], wuni[:, 2048:4096],
                Vbf[:, 0:2048], Vbf[:, 2048:4096]]
        gw_sem = [WQ0, WQ1, M1[0], M1[1]]
        gw_rd = {"T": [0, 0, 0, 0]}
        gw_first = p.now(V, S)

        def load_gw(c):
            for k, oc in ((0, c), (1, 16 + c)):
                sl = (2 * c + k) % 4
                p.dma("gpsimd", gsl4[sl], wg_src(oc), gw_sem[sl],
                      waits=[("T", gw_rd["T"][sl])]
                      + (gw_first if c < 2 else []))

        load_gw(0)
        for grp in range(4):
            grp_v = p.now(V)
            for j in range(4):
                c = grp * 4 + j
                slx = (2 * c) % 4
                sla = (2 * c + 1) % 4
                gxw = p.now(gw_sem[slx])
                gaw = p.now(gw_sem[sla])
                if c + 1 < 16:
                    load_gw(c + 1)
                for fc in range(NFC):
                    for nh in range(2):
                        p.on("tensor", lambda e, f=fc, n=nh, s=slx:
                             e.matmul(PAp[:, n * 512:(n + 1) * 512],
                                      gsl4[s][:, f * 128:(f + 1) * 128],
                                      BC[:, f * TLOC + n * 512:
                                         f * TLOC + (n + 1) * 512],
                                      start=(f == 0), stop=(f == NFC - 1)),
                             waits=(gxw + [("S", p.counts["S"])])
                             if (fc == 0 and nh == 0) else (),
                             inc=T if (fc == NFC - 1 and nh == 1) else None)
                gxmm = p.now(T)
                gw_rd["T"][slx] = gxmm[0][1]
                p.on("scalar", lambda e, j=j:
                     e.activation(gx4[:, j * 1024:(j + 1) * 1024], PAp[:, :],
                                  AF.Sigmoid), waits=gxmm + grp_v + absdo, inc=S)
                for fc in range(NFC):
                    for nh in range(2):
                        p.on("tensor", lambda e, f=fc, n=nh, s=sla:
                             e.matmul(PBp[:, n * 512:(n + 1) * 512],
                                      gsl4[s][:, f * 128:(f + 1) * 128],
                                      BC[:, f * TLOC + n * 512:
                                         f * TLOC + (n + 1) * 512],
                                      start=(f == 0), stop=(f == NFC - 1)),
                             waits=(gaw + [("S", p.counts["S"])])
                             if (fc == 0 and nh == 0) else (),
                             inc=T if (fc == NFC - 1 and nh == 1) else None)
                gamm = p.now(T)
                gw_rd["T"][sla] = gamm[0][1]
                p.on("scalar", lambda e, j=j:
                     e.activation(la4[:, j * 1024:(j + 1) * 1024], PBp[:, :],
                                  AF.Sigmoid), waits=gamm + grp_v, inc=S)
            sig_done = p.now(S)
            for j in range(4):
                p.on("vector", lambda e, j=j:
                     e.tensor_scalar(la4[:, j * 1024:(j + 1) * 1024],
                                     la4[:, j * 1024:(j + 1) * 1024],
                                     afneg8[:, 0:1], None, ALU.mult),
                     waits=(list(sig_done) + list(af_done)) if j == 0 else (),
                     inc=V)
            la_done = p.now(V)
            for j in range(4):
                p.on("scalar", lambda e, j=j:
                     e.activation(a4[:, j * 1024:(j + 1) * 1024],
                                  la4[:, j * 1024:(j + 1) * 1024], AF.Exp),
                     waits=la_done if j == 0 else (), inc=S)
                ps_t = (PAp, PBp, PCp, PDp)[j]
                p.on("scalar", lambda e, j=j, t=ps_t:
                     e.activation(t[:, :], la4[:, j * 1024:(j + 1) * 1024],
                                  AF.Exp, scale=2.0),
                     waits=[("T", p.counts["T"])], inc=S)
            e2_done = p.now(S)
            for j in range(4):
                ps_t = (PAp, PBp, PCp, PDp)[j]
                p.on("scalar", lambda e, j=j, t=ps_t:
                     e.activation(b4[:, j * 1024:(j + 1) * 1024], t[:, :],
                                  AF.Sqrt, bias=1.0, scale=-1.0),
                     waits=e2_done if j == 0 else (), inc=S)
            mult_done = p.now(S)
            for j in range(4):
                c = grp * 4 + j
                p.on("vector", lambda e, j=j:
                     e.tensor_mul(b4[:, j * 1024:(j + 1) * 1024],
                                  b4[:, j * 1024:(j + 1) * 1024],
                                  gx4[:, j * 1024:(j + 1) * 1024]),
                     waits=mult_done if j == 0 else (), inc=V)
                p.on("vector", lambda e, j=j, c=c:
                     e.tensor_mul(b4[:, j * 1024:(j + 1) * 1024],
                                  b4[:, j * 1024:(j + 1) * 1024],
                                  BB[:, c * TLOC:(c + 1) * TLOC]), inc=V)
                p.on("vector", lambda e, j=j:
                     e.tensor_tensor_scan(gx4[:, j * 1024:(j + 1) * 1024],
                                          a4[:, j * 1024:(j + 1) * 1024],
                                          b4[:, j * 1024:(j + 1) * 1024],
                                          0.0, ALU.mult, ALU.add),
                     waits=zeros_done, inc=V)
                p.on("vector", lambda e, j=j, c=c:
                     e.tensor_copy(carB_sb[:, c:c + 1],
                                   gx4[:, j * 1024 + 1023: j * 1024 + 1024]),
                     inc=V)
                p.on("vector", lambda e, j=j:
                     e.tensor_tensor_scan(la4[:, j * 1024:(j + 1) * 1024],
                                          a4[:, j * 1024:(j + 1) * 1024],
                                          rb[:, :], 1.0, ALU.mult, ALU.add),
                     inc=V)
                p.on("vector", lambda e, j=j, c=c:
                     e.tensor_copy(carA_sb[:, c:c + 1],
                                   la4[:, j * 1024 + 1023: j * 1024 + 1024]),
                     inc=V)
        scans_done = p.now(V)
        p.dma("sync", d["carA"], carA_sb, DO, waits=scans_done)
        p.dma("sync", d["carB"], carB_sb, DO, waits=scans_done)


        p.on("sync", lambda e: e.nop(), waits=p.now(DO, ABS, V, T, S, G))
        p.emit()

    return nc, d



def build_launch_b(debug=(), upto="all"):
    nc = bass.Bass("TRN2", target_bir_lowering=False)
    d = {}

    def inp(name, shape, dt=F32):
        d[name] = nc.dram_tensor(name, shape, dt, kind="ExternalInput").ap()
        return d[name]

    def outp(name, shape, dt=F32):
        d[name] = nc.dram_tensor(name, shape, dt, kind="ExternalOutput").ap()
        return d[name]

    inp("xn2T", [128, NFC * TLOC])
    inp("carApre", [128, 3 * NFC])
    inp("carBpre", [128, 3 * NFC])
    inp("absps", [8, 4])
    inp("wintH", [128, 32 * 2048])
    inp("apar", [128, 1])
    inp("wupH", [128, 32 * 4096])
    inp("wgtH", [128, 32 * 4096])
    inp("wdnT", [8192, DMODEL])

    outp("yT", [128, NFC * TLOC])
    for t in debug:
        outp("dbg_" + t, [128, NFC * TLOC])

    es = contextlib.ExitStack()
    with es:
        p = Prog(nc, es)

        def sb(name, shape, dt):
            return es.enter_context(nc.sbuf_tensor(name, shape, dt))

        BB = sb("BB", [128, NFC * TLOC], F32)
        BC = sb("BC", [128, NFC * TLOC], BF16)
        BX = sb("BX", [128, NFC * TLOC], F32)

        gx4 = BX[:, 0:4096]
        la4 = BX[:, 4096:8192]
        a4 = BX[:, 8192:12288]
        b4 = BX[:, 12288:16384]
        qup = bc16(BX[:, 0:4096])
        qgt = bc16(BX[:, 4096:8192])
        qdn = bc16(BX[:, 8192:12288])
        stg = [BX[:, 12288 + i * 2048: 12288 + (i + 1) * 2048] for i in range(2)]

        hb = sb("hb", [128, 2048], BF16)
        i8t = [sb(f"i8t{i}", [128, 2048], I8) for i in range(2)]
        wuni = sb("wuni", [128, 4096], BF16)
        wuni2 = sb("wuni2", [128, 4096], BF16)
        gsl4 = [wuni[:, 0:2048], wuni[:, 2048:4096],
                wuni2[:, 0:2048], wuni2[:, 2048:4096]]
        carA_t = sb("carA_t", [128, 3 * NFC], F32)
        carB_t = sb("carB_t", [128, 3 * NFC], F32)
        Hc = sb("Hc", [128, NFC], F32)
        apar_sb = sb("apar_sb", [128, 1], F32)
        afneg8 = sb("afneg8", [128, 1], F32)
        absps_sb = sb("absps_sb", [8, 4], F32)
        rsig = sb("rsig", [128, 4], F32)
        sig = sb("sig", [128, 4], F32)
        shalf = sb("shalf", [128, 4], F32)
        nshalf = sb("nshalf", [128, 4], F32)
        rows3 = sb("rows3", [1, 4], F32)
        kq_s = sb("kq_s", [128, TLOC], F32)
        SC2 = sb("SC2", [128, 512], F32)
        sqs = bc16(SC2[:, :])
        sqs_rd = {"T": 0}
        ones_col_bf = sb("ones_col_bf", [128, 1], BF16)
        ones_col_f = sb("ones_col_f", [128, 1], F32)
        ones_row_f = sb("ones_row_f", [1, 128], F32)
        rowA = sb("rowA", [1, TLOC], F32)
        rowB = sb("rowB", [1, TLOC], F32)
        rowC = sb("rowC", [1, TLOC], F32)
        rb = sb("rb", [128, TLOC], F32)
        mrsb_v = wuni2[:, 0:2048].bitcast(F32)

        def psum(name):
            return es.enter_context(nc.psum_tensor(name, [128, TLOC], F32))

        PAp, PBp, PCp, PDp = psum("PA"), psum("PB"), psum("PC"), psum("PD")
        PS4 = [PAp, PBp, PCp, PDp]

        T = p.sem("T", "tensor")
        V = p.sem("V", "vector")
        S = p.sem("S", "scalar")
        G = p.sem("G", "gpsimd")
        CONST = p.sem("CONST")
        DS0 = p.sem("DS0")
        DS1 = p.sem("DS1")
        WQ0 = p.sem("WQ0")
        WQ1 = p.sem("WQ1")
        UP0 = p.sem("UP0")
        UP1 = p.sem("UP1")
        GT0 = p.sem("GT0")
        GT1 = p.sem("GT1")
        DN0 = p.sem("DN0")
        DN1 = p.sem("DN1")
        DO = p.sem("DO")

        p.dma("gpsimd", carA_t[:, :], d["carApre"], CONST)
        p.dma("gpsimd", carB_t[:, :], d["carBpre"], CONST)
        p.dma("gpsimd", apar_sb[:, :], d["apar"], CONST)
        p.dma("gpsimd", absps_sb[:, :], d["absps"], CONST)
        p.on("gpsimd", lambda e: e.memset(ones_col_bf[:, :], 1.0), inc=G)
        p.on("gpsimd", lambda e: e.memset(ones_col_f[:, :], 1.0), inc=G)
        p.on("gpsimd", lambda e: e.memset(ones_row_f[:, :], 1.0), inc=G)
        p.dma("sync", BB[:, :], d["xn2T"], DS0)
        consts = p.now(CONST, G, DS0)

        for fc in range(NFC):
            p.on("vector", lambda e, f=fc:
                 e.tensor_copy(BC[:, f * TLOC:(f + 1) * TLOC],
                               BB[:, f * TLOC:(f + 1) * TLOC]),
                 waits=consts if fc == 0 else (), inc=V)
        bc_done = p.now(V)

        p.on("vector", lambda e: e.tensor_copy(Hc[:, :], carB_t[:, 0:NFC]),
             waits=consts, inc=V)
        for j in (1, 2):
            p.on("vector", lambda e, j=j:
                 e.tensor_mul(Hc[:, :], Hc[:, :],
                              carA_t[:, j * NFC:(j + 1) * NFC]), inc=V)
            p.on("vector", lambda e, j=j:
                 e.tensor_add(Hc[:, :], Hc[:, :],
                              carB_t[:, j * NFC:(j + 1) * NFC]), inc=V)
        h_ready = p.now(V)

        p.on("scalar", lambda e: e.activation(rsig[:, 3:4], apar_sb[:, :], AF.Abs),
             waits=consts, inc=S)
        p.on("scalar", lambda e:
             e.activation(rsig[:, 3:4], rsig[:, 3:4], AF.Exp, scale=-1.0), inc=S)
        p.on("scalar", lambda e:
             e.activation(rsig[:, 3:4], rsig[:, 3:4], AF.Ln, bias=1.0), inc=S)
        sp_done = p.now(S)
        p.on("vector", lambda e: e.tensor_relu(afneg8[:, :], apar_sb[:, :]),
             waits=sp_done, inc=V)
        p.on("vector", lambda e:
             e.tensor_add(afneg8[:, :], afneg8[:, :], rsig[:, 3:4]), inc=V)
        p.on("vector", lambda e:
             e.tensor_scalar(afneg8[:, :], afneg8[:, :], -8.0, None, ALU.mult),
             inc=V)
        af_done = p.now(V)

        def wg_src(oc):
            return d["wintH"][:, oc * 2048:(oc + 1) * 2048]

        gw_sem = [WQ0, WQ1, GT0, GT1]
        gw_rd = {"T": [0, 0, 0, 0]}

        def load_gw(c):
            for k, oc in ((0, c), (1, 16 + c)):
                sl = (2 * c + k) % 4
                p.dma("gpsimd", gsl4[sl], wg_src(oc), gw_sem[sl],
                      waits=[("T", gw_rd["T"][sl])])
            return p.now(gw_sem[(2 * c) % 4], gw_sem[(2 * c + 1) % 4])

        load_gw(0)
        gates_v0 = p.counts["V"]
        av = [gates_v0] * 4
        pa_g = p.counts["S"]
        pb_g = p.counts["S"]
        E2 = [kq_s, rb]
        for grp in range(4):
            sla_s = [0] * 4
            sq_s = [0] * 4
            for j in range(4):
                c = grp * 4 + j
                slx = (2 * c) % 4
                sla = (2 * c + 1) % 4
                gxw = p.now(gw_sem[slx])
                gaw = p.now(gw_sem[sla])
                if c + 1 < 16:
                    load_gw(c + 1)
                for fc in range(NFC):
                    for nh in range(2):
                        p.on("tensor", lambda e, f=fc, n=nh, s=slx:
                             e.matmul(PAp[:, n * 512:(n + 1) * 512],
                                      gsl4[s][:, f * 128:(f + 1) * 128],
                                      BC[:, f * TLOC + n * 512:
                                         f * TLOC + (n + 1) * 512],
                                      start=(f == 0), stop=(f == NFC - 1)),
                             waits=(list(gxw) + list(bc_done)
                                    + [("S", pa_g)])
                             if (fc == 0 and nh == 0) else (),
                             inc=T if (fc == NFC - 1 and nh == 1) else None)
                gxmm = p.now(T)
                gw_rd["T"][slx] = gxmm[0][1]
                p.on("scalar", lambda e, j=j:
                     e.activation(gx4[:, j * 1024:(j + 1) * 1024], PAp[:, :],
                                  AF.Sigmoid),
                     waits=gxmm + [("V", av[j])], inc=S)
                pa_g = p.counts["S"]
                for fc in range(NFC):
                    for nh in range(2):
                        p.on("tensor", lambda e, f=fc, n=nh, s=sla:
                             e.matmul(PBp[:, n * 512:(n + 1) * 512],
                                      gsl4[s][:, f * 128:(f + 1) * 128],
                                      BC[:, f * TLOC + n * 512:
                                         f * TLOC + (n + 1) * 512],
                                      start=(f == 0), stop=(f == NFC - 1)),
                             waits=(list(gaw) + [("S", pb_g)])
                             if (fc == 0 and nh == 0) else (),
                             inc=T if (fc == NFC - 1 and nh == 1) else None)
                gamm = p.now(T)
                gw_rd["T"][sla] = gamm[0][1]
                p.on("scalar", lambda e, j=j:
                     e.activation(la4[:, j * 1024:(j + 1) * 1024], PBp[:, :],
                                  AF.Sigmoid),
                     waits=gamm + [("V", av[j])], inc=S)
                pb_g = p.counts["S"]
                sla_s[j] = p.counts["S"]
            for j in range(4):
                p.on("vector", lambda e, j=j:
                     e.tensor_scalar(la4[:, j * 1024:(j + 1) * 1024],
                                     la4[:, j * 1024:(j + 1) * 1024],
                                     afneg8[:, 0:1], None, ALU.mult),
                     waits=[("S", sla_s[j])] + (list(af_done) if j == 0
                                                else []),
                     inc=V)
            la_done = p.now(V)
            for j in range(4):
                p.on("scalar", lambda e, j=j:
                     e.activation(a4[:, j * 1024:(j + 1) * 1024],
                                  la4[:, j * 1024:(j + 1) * 1024], AF.Exp),
                     waits=la_done if j == 0 else (), inc=S)
            for jpair in (0, 1):
                for j in (2 * jpair, 2 * jpair + 1):
                    p.on("scalar", lambda e, j=j:
                         e.activation(E2[j % 2][:, :],
                                      la4[:, j * 1024:(j + 1) * 1024],
                                      AF.Exp, scale=2.0), inc=S)
                for j in (2 * jpair, 2 * jpair + 1):
                    p.on("scalar", lambda e, j=j:
                         e.activation(b4[:, j * 1024:(j + 1) * 1024],
                                      E2[j % 2][:, :],
                                      AF.Sqrt, bias=1.0, scale=-1.0),
                         waits=[("V", av[j])], inc=S)
                    sq_s[j] = p.counts["S"]
            for j in range(4):
                c = grp * 4 + j
                p.on("vector", lambda e, j=j:
                     e.tensor_mul(b4[:, j * 1024:(j + 1) * 1024],
                                  b4[:, j * 1024:(j + 1) * 1024],
                                  gx4[:, j * 1024:(j + 1) * 1024]),
                     waits=[("S", sq_s[j])], inc=V)
                p.on("vector", lambda e, j=j, c=c:
                     e.tensor_mul(b4[:, j * 1024:(j + 1) * 1024],
                                  b4[:, j * 1024:(j + 1) * 1024],
                                  BB[:, c * TLOC:(c + 1) * TLOC]), inc=V)
                p.on("vector", lambda e, j=j, c=c:
                     e.tensor_tensor_scan(gx4[:, j * 1024:(j + 1) * 1024],
                                          a4[:, j * 1024:(j + 1) * 1024],
                                          b4[:, j * 1024:(j + 1) * 1024],
                                          Hc[:, c:c + 1], ALU.mult, ALU.add),
                     waits=h_ready, inc=V)
                p.on("vector", lambda e, j=j, c=c:
                     e.tensor_add(BB[:, c * TLOC:(c + 1) * TLOC],
                                  gx4[:, j * 1024:(j + 1) * 1024],
                                  BB[:, c * TLOC:(c + 1) * TLOC]), inc=V)
                av[j] = p.counts["V"]
        r3_done = p.now(V)

        def layernorm(src_f32, dst_f32, dst_bf, waits):
            for fc in range(NFC):
                p.on("vector", lambda e, f=fc:
                     e.tensor_copy(dst_bf[:, f * TLOC:(f + 1) * TLOC],
                                   src_f32[:, f * TLOC:(f + 1) * TLOC]),
                     waits=waits if fc == 0 else (), inc=V)
            cast_done = p.now(V)
            for fc in range(NFC):
                for nh in range(2):
                    p.on("tensor", lambda e, f=fc, n=nh:
                         e.matmul(PDp[0:1, n * 512:(n + 1) * 512],
                                  ones_col_bf[:, :],
                                  dst_bf[:, f * TLOC + n * 512:
                                         f * TLOC + (n + 1) * 512],
                                  start=(f == 0), stop=(f == NFC - 1)),
                         waits=cast_done if (fc == 0 and nh == 0) else (),
                         inc=T if (fc == NFC - 1 and nh == 1) else None)
            for fc in range(NFC):
                p.on("scalar", lambda e, f=fc:
                     e.activation(sqs, dst_bf[:, f * TLOC:(f + 1) * TLOC],
                                  AF.Square),
                     waits=list(cast_done if fc == 0 else [])
                     + [("T", sqs_rd["T"])], inc=S)
                sqd = p.now(S)
                for nh in range(2):
                    p.on("tensor", lambda e, f=fc, n=nh:
                         e.matmul(PAp[0:1, n * 512:(n + 1) * 512],
                                  ones_col_bf[:, :],
                                  sqs[:, n * 512:(n + 1) * 512],
                                  start=(f == 0), stop=(f == NFC - 1)),
                         waits=(sqd + p.now(V)) if nh == 0 else (),
                         inc=T if nh == 1 else None)
                sqs_rd["T"] = p.now(T)[0][1]
            stats_mm = p.now(T)
            p.on("vector", lambda e:
                 e.tensor_scalar(rowA[:, :], PDp[0:1, :], 1.0 / DMODEL, None,
                                 ALU.mult), waits=stats_mm, inc=V)
            p.on("vector", lambda e:
                 e.tensor_tensor(rowB[:, :], rowA[:, :], rowA[:, :], ALU.mult),
                 inc=V)
            p.on("vector", lambda e:
                 e.scalar_tensor_tensor(rowC[:, :], PAp[0:1, :], 1.0 / DMODEL,
                                        rowB[:, :], ALU.mult, ALU.subtract),
                 inc=V)
            p.on("vector", lambda e:
                 e.tensor_scalar(rowC[:, :], rowC[:, :], 1e-5, None, ALU.add),
                 inc=V)
            vd = p.now(V)
            p.on("scalar", lambda e:
                 e.activation(rowC[:, :], rowC[:, :], AF.Ln),
                 waits=vd, inc=S)
            p.on("scalar", lambda e:
                 e.activation(rowC[:, :], rowC[:, :], AF.Exp, scale=-0.5), inc=S)
            rstd_done = p.now(S)
            p.on("vector", lambda e:
                 e.tensor_tensor(rowB[:, :], rowA[:, :], rowC[:, :], ALU.mult),
                 waits=rstd_done, inc=V)
            mrs_done = p.now(V)
            for nh in range(2):
                p.on("tensor", lambda e, n=nh:
                     e.matmul(PDp[:, n * 512:(n + 1) * 512], ones_row_f[:, :],
                              rowC[0:1, n * 512:(n + 1) * 512], start=True,
                              stop=True),
                     waits=mrs_done if nh == 0 else (), inc=None)
                p.on("tensor", lambda e, n=nh:
                     e.matmul(PAp[:, n * 512:(n + 1) * 512], ones_row_f[:, :],
                              rowB[0:1, n * 512:(n + 1) * 512], start=True,
                              stop=True),
                     inc=T if nh == 1 else None)
            bc_mm = p.now(T)
            p.on("scalar", lambda e: e.copy(rb[:, :], PDp[:, :]),
                 waits=bc_mm, inc=S)
            p.on("scalar", lambda e: e.copy(mrsb_v[:, :], PAp[:, :]), inc=S)
            bcd = p.now(S)
            for fc in range(NFC):
                p.on("vector", lambda e, f=fc:
                     e.tensor_tensor(kq_s[:, :], src_f32[:, f * TLOC:(f + 1) * TLOC],
                                     rb[:, :], ALU.mult),
                     waits=bcd if fc == 0 else (), inc=V)
                p.on("vector", lambda e, f=fc:
                     e.tensor_sub(dst_f32[:, f * TLOC:(f + 1) * TLOC],
                                  kq_s[:, :], mrsb_v[:, :]), inc=V)
                p.on("vector", lambda e, f=fc:
                     e.tensor_copy(dst_bf[:, f * TLOC:(f + 1) * TLOC],
                                   dst_f32[:, f * TLOC:(f + 1) * TLOC]), inc=V)

        layernorm(BB, BB, BC, r3_done)
        xn3_done = p.now(V)
        dbg_guard = []
        if "xn3" in debug:
            p.dma("sync", d["dbg_xn3"], BB[:, :], DO, waits=p.now(V))
            dbg_guard = p.now(DO)
        if upto == "xn3":
            p.on("sync", lambda e: e.nop(), waits=p.now(DO, V, T, S, G))
            p.emit()
            return nc, d

        p.on("tensor", lambda e:
             e.matmul(PDp[0:1, 0:4], ones_col_f[0:8, :], absps_sb[:, :],
                      start=True, stop=True),
             waits=consts + p.now(V, S), inc=T)
        s_mm = p.now(T)
        p.on("vector", lambda e:
             e.tensor_scalar(rows3[:, :], PDp[0:1, 0:4], 1.0 / 16777216.0,
                             None, ALU.mult), waits=s_mm, inc=V)
        p.on("vector", lambda e:
             e.tensor_scalar(rows3[:, :], rows3[:, :], 1e-8, None, ALU.add),
             inc=V)
        p.on("vector", lambda e:
             e.tensor_copy(rowA[0:1, 0:4], rows3[:, :]), inc=V)
        sc_done = p.now(V)
        p.on("scalar", lambda e: e.activation(rows3[:, :], rows3[:, :], AF.Ln),
             waits=sc_done, inc=S)
        p.on("scalar", lambda e:
             e.activation(rows3[:, :], rows3[:, :], AF.Exp, scale=-1.0), inc=S)
        rs_done = p.now(S)
        p.on("tensor", lambda e:
             e.matmul(PDp[:, 0:4], ones_row_f[:, :], rows3[0:1, :],
                      start=True, stop=True), waits=rs_done + sc_done, inc=T)
        p.on("tensor", lambda e:
             e.matmul(PDp[:, 4:8], ones_row_f[:, :], rowA[0:1, 0:4],
                      start=True, stop=True), inc=T)
        rs_mm = p.now(T)
        p.on("vector", lambda e: e.tensor_copy(rsig[:, :], PDp[:, 0:4]),
             waits=rs_mm, inc=V)
        p.on("vector", lambda e: e.tensor_copy(sig[:, :], PDp[:, 4:8]),
             inc=V)
        p.on("vector", lambda e:
             e.tensor_scalar(shalf[:, :], sig[:, :], 0.5, None, ALU.mult),
             inc=V)
        p.on("vector", lambda e:
             e.tensor_scalar(nshalf[:, :], sig[:, :], -0.5, None, ALU.mult),
             inc=V)
        rsig_done = p.now(V)

        NG = 32
        QUP = [bc16(BX[:, i * 2048:(i + 1) * 2048]) for i in range(2)]
        QGT = [bc16(BX[:, 4096 + i * 2048: 4096 + (i + 1) * 2048])
               for i in range(2)]
        QDN = [bc16(BX[:, 8192 + i * 2048: 8192 + (i + 1) * 2048])
               for i in range(3)]
        stgl = wuni[:, 0:4096].bitcast(F32)
        stg = [BX[:, 14336 + i * 1024: 14336 + (i + 1) * 1024] for i in range(2)]
        stg = stg + [stgl[:, 0:1024], stgl[:, 1024:2048]]
        i8q = [i8t[0][:, 0:1024], i8t[1][:, 0:1024],
               i8t[0][:, 1024:2048], i8t[1][:, 1024:2048]]
        hb2 = [hb[:, :], wuni2[:, 2048:4096]]

        ffn_v0 = p.counts["V"]
        ffn_s0 = p.counts["S"]
        ffn_t0 = p.counts["T"]
        stg_rd = {"V": [ffn_v0] * 4}
        i8_rd = {"S": [ffn_s0] * 4}
        q_rd_ug = [ffn_t0, ffn_t0]
        q_rd_dn = [ffn_t0, ffn_t0, ffn_t0]
        hb_rd = [ffn_t0, ffn_t0]
        kq_rd = {"V": ffn_v0}
        pa_rd = {"S": ffn_s0}
        ps_evac = [ffn_v0, ffn_v0]
        h_done_g = [(), ()]
        q_done = {}

        def quantize_pieces(g, plo, phi):
            par = g % 2
            dpar = g % 3
            for pi in range(plo, phi):
                mat, pc = divmod(pi, 4)
                sl = pi % 2 + 2 * ((pi // 2) % 2)
                st = stg[sl]
                i8v = i8q[sl]
                if mat < 2:
                    w = d["wupH"] if mat == 0 else d["wgtH"]
                    srcap = w[:, g * 4096 + pc * 1024: g * 4096 + (pc + 1) * 1024]
                    qb = QUP[par] if mat == 0 else QGT[par]
                    qslice = qb[:, pc * 1024:(pc + 1) * 1024]
                else:
                    kc2, half = divmod(pc, 2)
                    srcap = d["wdnT"][g * 256 + kc2 * 128:
                                      g * 256 + (kc2 + 1) * 128,
                                      half * 1024:(half + 1) * 1024]
                    qslice = QDN[dpar][:, kc2 * 2048 + half * 1024:
                                       kc2 * 2048 + (half + 1) * 1024]
                lane = pi % 2
                eng = "sync" if lane == 0 else "gpsimd"
                qsem = (UP0, UP1, DS1, GT0)[sl]
                tg = ("T", q_rd_ug[par]) if mat < 2 else ("T", q_rd_dn[dpar])
                p.dma(eng, st, srcap, qsem,
                      waits=[("V", stg_rd["V"][sl]), ("S", ffn_s0),
                             ("T", ffn_t0)])
                ld = p.now(qsem)
                p.on("vector", lambda e, st=st, i8v=i8v, m=min(mat, 2):
                     e.tensor_scalar(i8v, st, rsig[:, m:m + 1], None, ALU.mult),
                     waits=ld + rsig_done + [("S", i8_rd["S"][sl])], inc=V)
                stg_rd["V"][sl] = p.counts["V"]
                i8d = p.now(V)
                p.on("scalar", lambda e, i8v=i8v, qslice=qslice:
                     e.activation(qslice, i8v, AF.Sign),
                     waits=i8d + [tg], inc=S)
                i8_rd["S"][sl] = p.counts["S"]
            return p.now(S)

        def down_og(gd, og):
            par = gd % 2
            dpar = gd % 3
            PW = PCp if og % 2 == 0 else PDp
            w_ = [("V", ps_evac[og % 2])]
            if og == 0:
                w_ = w_ + list(h_done_g[par]) + dbg_guard
            for kc in range(2):
                for nh in range(2):
                    p.on("tensor", lambda e, k=kc, o=og, n=nh, PW=PW,
                         dpar=dpar, par=par:
                         e.matmul(PW[:, n * 512:(n + 1) * 512],
                                  QDN[dpar][:, k * 2048 + o * 128:
                                           k * 2048 + (o + 1) * 128],
                                  hb2[par][:, k * 1024 + n * 512:
                                           k * 1024 + (n + 1) * 512],
                                  start=(k == 0), stop=(k == 1)),
                         waits=w_ if (kc == 0 and nh == 0) else (),
                         inc=T if (kc == 1 and nh == 1) else None)
            dmm = p.now(T)
            q_rd_dn[dpar] = dmm[0][1]
            hb_rd[par] = dmm[0][1]
            c = p.on("vector", lambda e, o=og, PW=PW:
                     e.scalar_tensor_tensor(BB[:, o * TLOC:(o + 1) * TLOC],
                                            PW[:, :], sig[:, 2:3],
                                            BB[:, o * TLOC:(o + 1) * TLOC],
                                            ALU.mult, ALU.add),
                     waits=dmm, inc=V)
            ps_evac[og % 2] = c

        q_done[0] = quantize_pieces(0, 0, 12)
        q_done[1] = quantize_pieces(1, 0, 12)
        for g in range(NG):
            par = g % 2
            qd = q_done.pop(g)
            for hc in range(2):
                for fc in range(NFC):
                    for nh in range(2):
                        p.on("tensor", lambda e, f=fc, n=nh, h=hc, par=par:
                             e.matmul(PAp[:, n * 512:(n + 1) * 512],
                                      QUP[par][:, f * 256 + h * 128:
                                               f * 256 + (h + 1) * 128],
                                      BC[:, f * TLOC + n * 512:
                                         f * TLOC + (n + 1) * 512],
                                      start=(f == 0), stop=(f == NFC - 1)),
                             waits=(qd + [("V", kq_rd["V"]),
                                          ("S", pa_rd["S"])])
                             if (fc == 0 and nh == 0) else (),
                             inc=T if (fc == NFC - 1 and nh == 1) else None)
                if g > 0:
                    for og in range(8 * hc, 8 * hc + 4):
                        down_og(g - 1, og)
                for fc in range(NFC):
                    for nh in range(2):
                        p.on("tensor", lambda e, f=fc, n=nh, h=hc, par=par:
                             e.matmul(PBp[:, n * 512:(n + 1) * 512],
                                      QGT[par][:, f * 256 + h * 128:
                                               f * 256 + (h + 1) * 128],
                                      BC[:, f * TLOC + n * 512:
                                         f * TLOC + (n + 1) * 512],
                                      start=(f == 0), stop=(f == NFC - 1)),
                             inc=T if (fc == NFC - 1 and nh == 1) else None)
                if g > 0:
                    for og in range(8 * hc + 4, 8 * hc + 8):
                        down_og(g - 1, og)
                umm = p.now(T)
                q_rd_ug[par] = umm[0][1]
                p.on("scalar", lambda e:
                     e.activation(kq_s[:, :], PAp[:, :], AF.Silu,
                                  scale=sig[:, 0:1]),
                     waits=umm + [("V", kq_rd["V"])], inc=S)
                pa_rd["S"] = p.counts["S"]
                sld = p.now(S)
                p.on("vector", lambda e, h=hc, par=par:
                     e.scalar_tensor_tensor(hb2[par][:, h * 1024:(h + 1) * 1024],
                                            PBp[:, :], sig[:, 1:2],
                                            kq_s[:, :], ALU.mult, ALU.mult),
                     waits=sld + [("T", hb_rd[par])], inc=V)
                kq_rd["V"] = p.counts["V"]
                if hc == 0 and 2 <= g + 1 < NG:
                    q_done[g + 1] = quantize_pieces(g + 1, 8, 12)
            h_done_g[par] = p.now(V)
            if g + 2 < NG:
                quantize_pieces(g + 2, 0, 8)
        for og in range(16):
            down_og(NG - 1, og)
        ffn_done = p.now(V)

        if "r4" in debug:
            p.dma("sync", d["dbg_r4"], BB[:, :], DO, waits=ffn_done)
            ffn_done = list(ffn_done) + p.now(DO)
        if "qup" in debug:
            p.dma("sync", d["dbg_qup"], BX[:, :], DO, waits=ffn_done)
            ffn_done = list(ffn_done) + p.now(DO)

        layernorm(BB, BB, BC, ffn_done)
        y_done = p.now(V)
        p.dma("sync", d["yT"], BB[:, :], DO, waits=y_done)
        p.on("sync", lambda e: e.nop(), waits=p.now(DO, V, T, S, G))
        p.emit()

    return nc, d



def _rope_tables(T, dim=128, base=10000.0):
    inv = 1.0 / base ** (np.arange(0, dim, 2, dtype=np.float64) / dim)
    t = np.arange(T, dtype=np.float64)
    fr = t[:, None] * inv[None, :]
    emb = np.concatenate([fr, fr], axis=1)
    return (np.ascontiguousarray(np.cos(emb).T.astype(np.float32)),
            np.ascontiguousarray(np.sin(emb).T.astype(np.float32)))


def _rot_matrix():
    P = np.zeros((128, 128), np.float32)
    for dd in range(64):
        P[dd, dd + 64] = -1.0
        P[dd + 64, dd] = 1.0
    return np.ascontiguousarray(P.T)


def make_inputs_a(inputs):
    x = np.asarray(inputs["L0_attn_x"], np.float32)
    wat = np.ascontiguousarray(np.asarray(inputs["L0_attn_w_attn"], np.float32))
    wproj = np.asarray(inputs["L0_attn_w_proj"], np.float32)
    watqH = np.ascontiguousarray(
        wat[:, 0:2048].reshape(16, 128, 16, 128).transpose(1, 2, 0, 3)
        .reshape(128, -1))
    wprojH = np.ascontiguousarray(
        wproj.reshape(16, 128, 16, 128).transpose(1, 2, 0, 3).reshape(128, -1))
    cosT, sinT = _rope_tables(TFULL)
    PT = _rot_matrix()
    ident = np.eye(128, dtype=np.float32)
    gateT = np.ascontiguousarray(np.asarray(inputs["L0_ffn_gate"], np.float32).T)
    bias8 = np.ascontiguousarray(
        np.asarray(inputs["L0_ffn_expert_bias"], np.float32).reshape(1, 8))
    w1t = np.asarray(inputs["L0_ffn_w1"], np.float32).transpose(0, 2, 1)
    w3t = np.asarray(inputs["L0_ffn_w3"], np.float32).transpose(0, 2, 1)
    w2t = np.asarray(inputs["L0_ffn_w2"], np.float32).transpose(0, 2, 1)
    sw1t = np.asarray(inputs["L0_ffn_shared_w1"], np.float32).T
    sw3t = np.asarray(inputs["L0_ffn_shared_w3"], np.float32).T
    sw2t = np.asarray(inputs["L0_ffn_shared_w2"], np.float32).T
    w1all = np.concatenate([w1t.reshape(-1, 1024), sw1t], axis=0)
    w3all = np.concatenate([w3t.reshape(-1, 1024), sw3t], axis=0)
    w2all = np.concatenate([w2t.reshape(-1, 2048), sw2t], axis=0)
    w1H = np.ascontiguousarray(
        w1all.reshape(9, 16, 128, 8, 128).transpose(2, 0, 3, 1, 4)
        .reshape(128, -1))
    w3H = np.ascontiguousarray(
        w3all.reshape(9, 16, 128, 8, 128).transpose(2, 0, 3, 1, 4)
        .reshape(128, -1))
    w2H = np.ascontiguousarray(
        w2all.reshape(9, 8, 128, 16, 128).transpose(2, 0, 3, 1, 4)
        .reshape(128, -1))
    OH8 = np.zeros((8, 1024), np.float32)
    for e in range(8):
        OH8[e, e * 128:(e + 1) * 128] = 1.0
    wintH = np.ascontiguousarray(
        np.asarray(inputs["L1_attn_W_in"], np.float32)
        .reshape(32, 128, 16, 128).transpose(3, 0, 2, 1).reshape(128, -1))
    apar = np.ascontiguousarray(
        np.asarray(inputs["L1_attn_a_param"], np.float32).reshape(128, 1))
    wup = np.asarray(inputs["L1_ffn_W_up"], np.float32).reshape(-1)
    wgt = np.asarray(inputs["L1_ffn_W_gate"], np.float32).reshape(-1)
    wdn = np.asarray(inputs["L1_ffn_W_down"], np.float32).reshape(-1)

    maps = []
    for r in range(8):
        b, c = divmod(r, 4)
        t0 = c * TLOC
        maps.append(dict(
            xloc=np.ascontiguousarray(x[b, t0:t0 + TLOC]),
            xb=np.ascontiguousarray(x[b]),
            cosT=cosT, sinT=sinT,
            cosl=np.ascontiguousarray(cosT[:, t0:t0 + TLOC]),
            sinl=np.ascontiguousarray(sinT[:, t0:t0 + TLOC]),
            wat=wat, watqH=watqH, wprojH=wprojH, ident=ident, PT=PT,
            gateT=gateT, bias8=bias8,
            w1H=w1H, w3H=w3H, w2H=w2H, OH8=OH8,
            wintH=wintH, apar=apar,
            wupf=np.ascontiguousarray(wup[r * 2097152:(r + 1) * 2097152]
                                      .reshape(128, 16384)),
            wgtf=np.ascontiguousarray(wgt[r * 2097152:(r + 1) * 2097152]
                                      .reshape(128, 16384)),
            wdnf=np.ascontiguousarray(wdn[r * 2097152:(r + 1) * 2097152]
                                      .reshape(128, 16384)),
        ))
    return maps


def make_inputs_b(inputs, results_a):
    wintH = np.ascontiguousarray(
        np.asarray(inputs["L1_attn_W_in"], np.float32)
        .reshape(32, 128, 16, 128).transpose(3, 0, 2, 1).reshape(128, -1))
    apar = np.ascontiguousarray(
        np.asarray(inputs["L1_attn_a_param"], np.float32).reshape(128, 1))
    wupH = np.ascontiguousarray(
        np.asarray(inputs["L1_ffn_W_up"], np.float32)
        .reshape(32, 256, 16, 128).transpose(3, 0, 2, 1).reshape(128, -1))
    wgtH = np.ascontiguousarray(
        np.asarray(inputs["L1_ffn_W_gate"], np.float32)
        .reshape(32, 256, 16, 128).transpose(3, 0, 2, 1).reshape(128, -1))
    wdnT = np.ascontiguousarray(np.asarray(inputs["L1_ffn_W_down"], np.float32).T)
    absps = np.ascontiguousarray(
        np.stack([results_a[r]["absp"][0] for r in range(8)]).astype(np.float32))
    maps = []
    for r in range(8):
        b, c = divmod(r, 4)
        carA = np.ones((128, 3 * NFC), np.float32)
        carB = np.zeros((128, 3 * NFC), np.float32)
        for s in range(3):
            src_chunk = c - 3 + s
            if src_chunk >= 0:
                rr = b * 4 + src_chunk
                carA[:, s * NFC:(s + 1) * NFC] = results_a[rr]["carA"]
                carB[:, s * NFC:(s + 1) * NFC] = results_a[rr]["carB"]
        maps.append(dict(
            xn2T=np.ascontiguousarray(results_a[r]["xn2T"]),
            carApre=carA, carBpre=carB, absps=absps,
            wintH=wintH, apar=apar, wupH=wupH, wgtH=wgtH, wdnT=wdnT,
        ))
    return maps


def fm_to_tok(arr):
    return arr.reshape(128, NFC, TLOC).transpose(2, 1, 0).reshape(TLOC, DMODEL)


def kernel(**inputs):
    maps_a = make_inputs_a(inputs)
    nc_a, _ = build_launch_a()
    res_a = run_bass_kernel_spmd(nc_a, maps_a, core_ids=list(range(8)))
    maps_b = make_inputs_b(inputs, res_a.results)
    nc_b, _ = build_launch_b()
    res_b = run_bass_kernel_spmd(nc_b, maps_b, core_ids=list(range(8)))
    out = np.zeros((2, TFULL, DMODEL), np.float32)
    for r in range(8):
        b, c = divmod(r, 4)
        out[b, c * TLOC:(c + 1) * TLOC] = fm_to_tok(res_b.results[r]["yT"])
    return out
